# version 02
# baseline (speedup 1.0000x reference)
import numpy as np
import ml_dtypes
from contextlib import ExitStack

import concourse.bass as bass
import concourse.mybir as mybir
from concourse.bass_utils import run_bass_kernel_spmd

F32 = mybir.dt.float32
BF16 = mybir.dt.bfloat16
I32 = mybir.dt.int32
AF = mybir.ActivationFunctionType
ALU = mybir.AluOpType
AX = mybir.AxisListType
BF = ml_dtypes.bfloat16

DBG = ()
F_EXPERTS = list(range(9))
D = 2048
TW = 512
T = 4 * TW
NKD = 16
NT = 4
F_UP = 6144
HS = 1408
I8 = 8192
NE = 8

SV = {}
_c = 0
for _n, _k in [("norm1_w", 16), ("left_b", 16), ("right_b", 16), ("conv_b", 16),
               ("cw0", 16), ("cw1", 16), ("cw2", 16), ("cw3", 16),
               ("ga_b", 16), ("gx_b", 16), ("a_param", 16), ("out_b", 16),
               ("norm2_w", 16), ("up1_b", 48), ("up2_b", 48), ("down_b", 16),
               ("ln_g", 16), ("ln_b", 16), ("bq", 16), ("bk", 16), ("bv", 16),
               ("bqa", 16), ("bka", 16),
               ("gscale", 64), ("uscale", 64), ("dscale", 16)]:
    SV[_n] = _c
    _c += _k
NSV = _c

CONSTS = [0.0, 1.0, 1e-8, 1e-5, 1.0 / 2048.0, -0.5, 0.5, -1.0, 2.0,
          float(1.0 / np.sqrt(2048.0))]



def _pack_vec(v):
    v = np.asarray(v, np.float32)
    return np.ascontiguousarray(v.reshape(v.shape[0] // 128, 128).T)


def _t(a):
    return np.ascontiguousarray(np.asarray(a).T)


def stage_host(inputs):
    sh = {}
    x = np.asarray(inputs["L0_attn_x"], np.float32)
    sv = np.zeros((128, NSV), np.float32)

    def put(name, v):
        p = _pack_vec(v)
        sv[:, SV[name]:SV[name] + p.shape[1]] = p

    put("norm1_w", inputs["L0_attn_norm1_w"]); put("left_b", inputs["L0_attn_left_b"])
    put("right_b", inputs["L0_attn_right_b"]); put("conv_b", inputs["L0_attn_conv_b"])
    for k in range(4):
        put(f"cw{k}", inputs["L0_attn_conv_w"][k])
    put("ga_b", inputs["L0_attn_ga_b"]); put("gx_b", inputs["L0_attn_gx_b"])
    put("a_param", inputs["L0_attn_a_param"]); put("out_b", inputs["L0_attn_out_b"])
    put("norm2_w", inputs["L0_attn_norm2_w"])
    put("up1_b", inputs["L0_attn_up1_b"]); put("up2_b", inputs["L0_attn_up2_b"])
    put("down_b", inputs["L0_attn_down_b"])
    put("ln_g", inputs["L1_attn_ln_g"]); put("ln_b", inputs["L1_attn_ln_b"])
    put("bq", inputs["L1_attn_bq"]); put("bk", inputs["L1_attn_bk"]); put("bv", inputs["L1_attn_bv"])
    put("bqa", inputs["L1_attn_bqa"]); put("bka", inputs["L1_attn_bka"])
    put("gscale", inputs["L1_ffn_gate_scale"]); put("uscale", inputs["L1_ffn_up_scale"])
    put("dscale", inputs["L1_ffn_down_scale"])
    sh["svec"] = sv

    sh["w_left"] = np.asarray(inputs["L0_attn_left_W"], np.float32)
    sh["w_right"] = np.asarray(inputs["L0_attn_right_W"], np.float32)
    sh["w_ga"] = np.asarray(inputs["L0_attn_ga_W"], np.float32)
    sh["w_gx"] = np.asarray(inputs["L0_attn_gx_W"], np.float32)
    sh["w_out"] = np.asarray(inputs["L0_attn_out_W"], np.float32)
    sh["w_up1"] = np.asarray(inputs["L0_attn_up1_W"], np.float32)
    sh["w_up2"] = np.asarray(inputs["L0_attn_up2_W"], np.float32)
    sh["w_down"] = np.asarray(inputs["L0_attn_down_W"], np.float32)
    sh["w_gate0"] = _t(inputs["L0_ffn_gate_w"])
    sh["w_sw1"] = _t(inputs["L0_ffn_sw1"]); sh["w_sw3"] = _t(inputs["L0_ffn_sw3"])
    sh["w_sw2"] = _t(inputs["L0_ffn_sw2"])
    sh["w_e1"] = np.ascontiguousarray(np.asarray(inputs["L0_ffn_w1"], np.float32).transpose(0, 2, 1))
    sh["w_e3"] = np.ascontiguousarray(np.asarray(inputs["L0_ffn_w3"], np.float32).transpose(0, 2, 1))
    sh["w_e2"] = np.ascontiguousarray(np.asarray(inputs["L0_ffn_w2"], np.float32).transpose(0, 2, 1))
    sh["ebias_rep"] = np.ascontiguousarray(
        np.broadcast_to(np.asarray(inputs["L0_ffn_expert_bias"], np.float32)[None, :], (128, 8)))
    sh["w_q"] = _t(inputs["L1_attn_Wq"]); sh["w_k"] = _t(inputs["L1_attn_Wk"])
    sh["w_v"] = _t(inputs["L1_attn_Wv"]); sh["w_qa"] = _t(inputs["L1_attn_Wqa"])
    sh["w_ka"] = _t(inputs["L1_attn_Wka"])
    sh["w_g8"] = _t(inputs["L1_ffn_gate_w"]); sh["w_u8"] = _t(inputs["L1_ffn_up_w"])
    sh["w_d8"] = _t(inputs["L1_ffn_down_w"])
    sh["eye"] = np.eye(128, dtype=BF)
    cst = np.zeros((128, 16), np.float32)
    for i, v in enumerate(CONSTS):
        cst[:, i] = v
    sh["consts"] = cst

    per = []
    for cc in range(8):
        g, p = cc // 4, cc % 4
        t0 = p * T
        m = dict()
        m["xT"] = np.ascontiguousarray(x[g, t0:t0 + T, :].T)
        m["xTh"] = (np.zeros((D, 3), np.float32) if p == 0
                    else np.ascontiguousarray(x[g, t0 - 3:t0, :].T))
        m["cmask"] = np.full((128, 4), 0.0 if p == 0 else 1.0, np.float32)
        rm = np.zeros((128, 8), np.float32)
        for r in range(8):
            if r // 4 == g and (r % 4) < p:
                rm[:, r] = 1.0
        m["rmask"] = rm
        m["rmask1m"] = np.ascontiguousarray(1.0 - rm)
        per.append(m)
    return sh, per



SIM_ARENA = False


class Arena:
    def __init__(self, nc, base, top):
        self.nc = nc
        if SIM_ARENA:
            top = base + (1 << 22)
        self.free = [(base, top)]
        self.live = {}

    def alloc(self, name, shape, dtype):
        nb = int(np.prod(shape[1:])) * mybir.dt.size(dtype)
        nb = (nb + 31) & ~31
        for i, (lo, hi) in enumerate(self.free):
            if hi - lo >= nb:
                if lo + nb == hi:
                    del self.free[i]
                else:
                    self.free[i] = (lo + nb, hi)
                h = self.nc.alloc_sbuf_tensor_at(name, shape, dtype, offset=lo)
                self.live[name] = (lo, nb, h)
                return h
        raise RuntimeError(f"arena OOM {name}: need {nb}B, free={self.free}")

    def freeb(self, *names):
        for name in names:
            lo, nb, _ = self.live.pop(name)
            if SIM_ARENA:
                continue
            self.free.append((lo, lo + nb))
        self.free.sort()
        out = []
        for lo, hi in self.free:
            if out and out[-1][1] == lo:
                out[-1] = (out[-1][0], hi)
            else:
                out.append((lo, hi))
        self.free = out


class C:
    def __init__(self, nc, blk, sems):
        self.nc = nc
        self.blk = blk
        self.sems = sems
        self.n = {k: 0 for k in sems}
        self.bank_last = {}

    def tag(self, instr, sem, k=None):
        k = k if k is not None else (16 if sem in ("gq", "sq", "wa", "wb", "xq", "da", "db", "ya", "yb") else 1)
        instr.then_inc(self.sems[sem], k)
        self.n[sem] += k
        return self.n[sem]

    def wait(self, e, sem, val=None):
        v = self.n[sem] if val is None else val
        if v > 0:
            e.wait_ge(self.sems[sem], v)

    def use_bank(self, t, b, free_sem, free_val):
        prev = self.bank_last.get(b)
        if prev is not None:
            t.wait_ge(self.sems[prev[0]], prev[1])
        self.bank_last[b] = (free_sem, free_val)

    def wait_bank(self, t, b):
        prev = self.bank_last.get(b)
        if prev is not None:
            t.wait_ge(self.sems[prev[0]], prev[1])

    def free_bank(self, b, sem):
        self.bank_last[b] = (sem, self.n[sem])

    def barrier(self):
        vals = dict(self.n)
        for secname in ("tensor", "vector", "scalar", "gpsimd", "sync"):
            dec = getattr(self.blk, secname)

            @dec
            def _(e, vals=vals):
                for s, v in vals.items():
                    if v > 0:
                        e.wait_ge(self.sems[s], v)


def mmloop(c, G, wdram, NK, NM, rhs, groups, evac_eng, evac, MW=4, extra=None,
           wq="gq", ready=()):
    blk, sems, ps = c.blk, c.sems, G["ps"]
    NG = len(groups)
    esem = "es" if evac_eng == "s" else "ev"
    t0, e0, q0 = c.n["t"], c.n[esem], c.n[wq]
    if MW * NK * 128 > 8192:
        MW = max(1, 8192 // (NK * 128))
    NS = NM // MW
    assert NS * MW == NM, (NM, MW)
    wrt = wdram.rearrange("(k p) m -> p k m", p=128)
    wv = [G["wbufA"].ap()[:, 0:NK * MW * 128].rearrange("p (k m) -> p k m", k=NK),
          G["wbufB"].ap()[:, 0:NK * MW * 128].rearrange("p (k m) -> p k m", k=NK)]
    qeng = "gpsimd" if wq == "gq" else "sync"

    wsem = ["wa", "wb"]
    wtarget = []

    @getattr(blk, qeng)
    def _(e):
        for ms in range(NS):
            if ms >= 2:
                e.wait_ge(sems["t"], t0 + (ms - 1) * MW * NG)
            elif t0 > 0:
                e.wait_ge(sems["t"], t0)
            sm = wsem[ms % 2]
            e.dma_start(out=wv[ms % 2],
                        in_=wrt[:, :, ms * MW * 128:(ms + 1) * MW * 128]
                        ).then_inc(sems[sm], 16)
            c.n[sm] += 16
            wtarget.append((sm, c.n[sm]))

    @blk.tensor
    def _(e):
        for sm, vl in ready:
            if vl > 0:
                e.wait_ge(sems[sm], vl)
        idx = 0
        for ms in range(NS):
            e.wait_ge(sems[wtarget[ms][0]], wtarget[ms][1])
            for mc in range(MW):
                m = ms * MW + mc
                for g, ncols in enumerate(groups):
                    b = (idx % 6)
                    c.use_bank(e, b, esem, e0 + idx + 1)
                    for k in range(NK):
                        mi = e.matmul(ps[b][:, 0:ncols],
                                      wv[ms % 2][:, k, mc * 128:(mc + 1) * 128],
                                      rhs(k, g),
                                      start=(k == 0),
                                      stop=(k == NK - 1 and extra is None))
                    if extra is not None:
                        mi = extra(e, ps[b], m, g)
                    mi.then_inc(sems["t"], 1)
                    idx += 1
    c.n["t"] = t0 + NM * NG

    sec = blk.scalar if evac_eng == "s" else blk.vector

    @sec
    def _(e):
        idx = 0
        for m in range(NM):
            for g, ncols in enumerate(groups):
                b = (idx % 6)
                e.wait_ge(sems["t"], t0 + idx + 1)
                evac(e, m, g, ps[b]).then_inc(sems[esem], 1)
                idx += 1
    c.n[esem] = e0 + NM * NG
    return t0, e0, esem


def emit_stats_ln(c, G, zsrc_fn, ncols, out_fn, *, kind, sq_buf, zbf_buf=None):
    blk, sems, ps = c.blk, c.sems, G["ps"]
    srow, ones_bf, onesr_f = G["srow"], G["ones_bf"], G["onesr_f"]
    v0, t0, s0 = c.n["v"], c.n["t"], c.n["s"]

    @blk.vector
    def _(e):
        for k in range(NKD):
            e.wait_ge(sems["t"], t0 + k)
            if zbf_buf is not None:
                e.tensor_copy(zbf_buf[:, 0:ncols], zsrc_fn(k))
                src = zbf_buf[:, 0:ncols]
            else:
                src = zsrc_fn(k)
            c.tag(e.tensor_tensor(sq_buf[:, 0:ncols], src, src, ALU.mult), "v")

    @blk.tensor
    def _(e):
        c.use_bank(e, 6, "s", s0 + 1)
        if kind == "ln":
            c.use_bank(e, 4, "v", c.n["v"] + NKD + 2)
        for k in range(NKD):
            e.wait_ge(sems["v"], v0 + k + 1)
            mi = e.matmul(ps[6][0:1, 0:ncols], ones_bf[:], sq_buf[:, 0:ncols],
                          start=(k == 0), stop=(k == NKD - 1))
            if kind == "ln":
                src = zbf_buf[:, 0:ncols] if zbf_buf is not None else zsrc_fn(k)
                mi = e.matmul(ps[4][0:1, 0:ncols], ones_bf[:], src,
                              start=(k == 0), stop=(k == NKD - 1))
            c.tag(mi, "t")

    if kind == "rms":
        @blk.scalar
        def _(e):
            c.wait(e, "t")
            c.tag(e.activation(srow[0:1, 0:ncols], ps[6][0:1, 0:ncols], AF.Ln,
                               bias=1e-8, scale=1.0 / 2048.0), "s")

        @blk.vector
        def _(e):
            c.wait(e, "s")
            c.tag(e.tensor_copy(srow[0:1, 1536:1536 + ncols], srow[0:1, 0:ncols]), "v")

        @blk.scalar
        def _(e):
            c.wait(e, "v")
            c.tag(e.activation(srow[0:1, 512:512 + ncols], srow[0:1, 1536:1536 + ncols],
                               AF.Exp, scale=-0.5), "s")
        @blk.tensor
        def _(e):
            c.wait(e, "s")
            c.use_bank(e, 5, "v", c.n["v"] + NKD)
            c.tag(e.matmul(ps[5][:, 0:ncols], onesr_f[:],
                           srow[0:1, 512:512 + ncols], start=True, stop=True), "t")

        @blk.vector
        def _(e):
            c.wait(e, "t")
            for k in range(NKD):
                c.tag(out_fn(e, k, ps[5][:, 0:ncols], None), "v")
    else:
        @blk.vector
        def _(e):
            c.wait(e, "t")
            e.tensor_scalar(srow[0:1, 1024:1024 + ncols], ps[4][0:1, 0:ncols],
                            1.0 / 2048.0, None, ALU.mult)
            e.tensor_scalar(srow[0:1, 0:ncols], ps[6][0:1, 0:ncols],
                            1.0 / 2048.0, None, ALU.mult)
            e.tensor_tensor(srow[0:1, 512:512 + ncols], srow[0:1, 1024:1024 + ncols],
                            srow[0:1, 1024:1024 + ncols], ALU.mult)
            c.tag(e.tensor_tensor(srow[0:1, 0:ncols], srow[0:1, 0:ncols],
                                  srow[0:1, 512:512 + ncols], ALU.subtract), "v")

        @blk.scalar
        def _(e):
            c.wait(e, "v")
            c.tag(e.activation(srow[0:1, 512:512 + ncols], srow[0:1, 0:ncols],
                               AF.Ln, bias=1e-5), "s")

        @blk.vector
        def _(e):
            c.wait(e, "s")
            c.tag(e.tensor_copy(srow[0:1, 1536:1536 + ncols], srow[0:1, 512:512 + ncols]), "v")

        @blk.scalar
        def _(e):
            c.wait(e, "v")
            c.tag(e.activation(srow[0:1, 0:ncols], srow[0:1, 1536:1536 + ncols],
                               AF.Exp, scale=-0.5), "s")

        @blk.vector
        def _(e):
            c.wait(e, "s")
            c.tag(e.tensor_tensor(srow[0:1, 512:512 + ncols],
                                  srow[0:1, 1024:1024 + ncols],
                                  srow[0:1, 0:ncols], ALU.mult), "v")

        @blk.tensor
        def _(e):
            c.wait(e, "v")
            c.use_bank(e, 5, "v", c.n["v"] + NKD)
            e.matmul(ps[5][:, 0:ncols], onesr_f[:], srow[0:1, 0:ncols],
                     start=True, stop=True)
            c.use_bank(e, 3, "v", c.n["v"] + NKD)
            c.tag(e.matmul(ps[3][:, 0:ncols], onesr_f[:],
                           srow[0:1, 512:512 + ncols], start=True, stop=True), "t")

        @blk.vector
        def _(e):
            c.wait(e, "t")
            for k in range(NKD):
                c.tag(out_fn(e, k, ps[5][:, 0:ncols], ps[3][:, 0:ncols]), "v")


def emit_allgather(c, G, src_ap, width, dst_sb):
    blk, sems = c.blk, c.sems
    if width == 32:
        cc_in, cc_out = G["cc_in"], G["cc_out"]
    else:
        cc_in, cc_out = G["cc_in16"], G["cc_out16"]

    @blk.gpsimd
    def _(e):
        c.wait(e, "v")
        c.wait(e, "es")
        c.tag(e.dma_start(out=cc_in.ap()[:, 0:width], in_=src_ap), "gq")
        c.wait(e, "gq")
        c.tag(e.collective_compute(
            "AllGather", ALU.bypass,
            replica_groups=[[0, 1, 2, 3, 4, 5, 6, 7]],
            ins=[cc_in.ap()[:, 0:width].opt()],
            outs=[cc_out.ap()[:, 0:width].opt()],
        ), "cc")
        c.wait(e, "cc")
        c.tag(e.dma_start(
            out=dst_sb[:, :, 0:width],
            in_=cc_out.ap()[:, 0:width].rearrange("(r p) w -> p r w", p=128)), "gq")



def build_program(dbg=()):
    nc = bass.Bass("TRN2", target_bir_lowering=False)

    xT = nc.dram_tensor("xT", [D, T], F32, kind="ExternalInput").ap()
    xTh = nc.dram_tensor("xTh", [D, 3], F32, kind="ExternalInput").ap()
    svec_d = nc.dram_tensor("svec", [128, NSV], F32, kind="ExternalInput").ap()
    cmask_d = nc.dram_tensor("cmask", [128, 4], F32, kind="ExternalInput").ap()
    rmask_d = nc.dram_tensor("rmask", [128, 8], F32, kind="ExternalInput").ap()
    rmask1m_d = nc.dram_tensor("rmask1m", [128, 8], F32, kind="ExternalInput").ap()
    ebias_d = nc.dram_tensor("ebias_rep", [128, 8], F32, kind="ExternalInput").ap()
    eye_d = nc.dram_tensor("eye", [128, 128], BF16, kind="ExternalInput").ap()
    consts_d = nc.dram_tensor("consts", [128, 16], F32, kind="ExternalInput").ap()
    wd = {}
    for nm, shp, dt in [
        ("w_left", [D, D], F32), ("w_right", [D, D], F32), ("w_ga", [D, D], F32),
        ("w_gx", [D, D], F32), ("w_out", [D, D], F32), ("w_up1", [D, F_UP], F32),
        ("w_up2", [D, F_UP], F32), ("w_down", [F_UP, D], F32),
        ("w_gate0", [D, 8], F32), ("w_sw1", [D, HS], F32), ("w_sw3", [D, HS], F32),
        ("w_sw2", [HS, D], F32),
        ("w_e1", [NE, D, HS], F32), ("w_e3", [NE, D, HS], F32), ("w_e2", [NE, HS, D], F32),
        ("w_q", [D, D], F32), ("w_k", [D, D], F32), ("w_v", [D, D], F32),
        ("w_qa", [D, D], F32), ("w_ka", [D, D], F32),
        ("w_g8", [D, I8], I32), ("w_u8", [D, I8], I32), ("w_d8", [I8, D], I32),
    ]:
        wd[nm] = nc.dram_tensor(nm, shp, dt, kind="ExternalInput").ap()
    y_out = nc.dram_tensor("y", [D, T], F32, kind="ExternalOutput").ap()

    sp = {}
    for nm, shp, dt in [
        ("left_sp", [D, T], BF16), ("u_sp", [D, T], BF16), ("h_sp", [D, T], BF16),
        ("P_sp", [D, T], BF16), ("x1_sp", [D, T], BF16), ("s0_sp", [D, T], BF16),
        ("zr_sp", [D, T], F32), ("s1_sp", [D, T], BF16), ("k_sp", [D, T], BF16),
        ("v_sp", [D, T], BF16), ("s2_sp", [D, T], BF16),
        ("hm_sp", [I8, T], BF16), ("d8bf_sp", [I8, D], BF16),
        ("a_sp", [D, T], F32), ("b_sp", [D, T], F32),
    ]:
        sp[nm] = nc.dram_tensor(nm, shp, dt, kind="Internal")
    cc_in = nc.dram_tensor("cc_in", [128, 32], F32, kind="Internal")
    cc_out = nc.dram_tensor("cc_out", [1024, 32], F32, kind="Internal")
    cc_in16 = nc.dram_tensor("cc_in16", [128, 16], F32, kind="Internal")
    cc_out16 = nc.dram_tensor("cc_out16", [1024, 16], F32, kind="Internal")
    dbg_d = {nm: nc.dram_tensor("dbg_" + nm, [D, T], F32, kind="ExternalOutput").ap()
             for nm in dbg if nm != "gates"}
    if "gates" in dbg:
        dbg_d["gates"] = nc.dram_tensor("dbg_gates", [128, 4 * T], F32, kind="ExternalOutput").ap()
        dbg_d["ga"] = nc.dram_tensor("dbg_ga", [D, T], F32, kind="ExternalOutput").ap()
        dbg_d["gb"] = nc.dram_tensor("dbg_gb", [D, T], F32, kind="ExternalOutput").ap()

    with ExitStack() as ctx:
        svec = nc.alloc_sbuf_tensor("svt", [128, NSV], F32)
        dvec = nc.alloc_sbuf_tensor("dvt", [128, 48], F32)
        cmask = nc.alloc_sbuf_tensor("cmask_t", [128, 4], F32)
        rmask = nc.alloc_sbuf_tensor("rmask_t", [128, 8], F32)
        rmask1m = nc.alloc_sbuf_tensor("rmask1m_t", [128, 8], F32)
        ebias = nc.alloc_sbuf_tensor("ebias_t", [128, 8], F32)
        eye = nc.alloc_sbuf_tensor("eye_t", [128, 128], BF16)
        ones_bf = nc.alloc_sbuf_tensor("ones_bf", [128, 1], BF16)
        onesr_f = nc.alloc_sbuf_tensor("onesr_f", [1, 128], F32)
        onesr_bf = nc.alloc_sbuf_tensor("onesr_bf", [1, 128], BF16)
        neg128r = nc.alloc_sbuf_tensor("neg128r", [1, 128], BF16)
        neg128f = nc.alloc_sbuf_tensor("neg128f", [1, 128], F32)
        zrow = nc.alloc_sbuf_tensor("zrow", [128, T], F32)
        wbufA = nc.alloc_sbuf_tensor("wbufA", [128, 8192], BF16)
        wbufB = nc.alloc_sbuf_tensor("wbufB", [128, 8192], BF16)
        srow = nc.alloc_sbuf_tensor("srow", [1, 2048], F32)
        cbank = nc.alloc_sbuf_tensor("cbank", [128, 16], F32)
        tiny = nc.alloc_sbuf_tensor("tinyt", [128, 96], F32)
        ps = [nc.alloc_psum_tensor(f"pb{i}", [128, 512], F32) for i in range(7)]
        psb = nc.alloc_psum_tensor("pbb", [128, 1024], BF16)
        sems = {s: ctx.enter_context(nc.semaphore("sem_" + s))
                for s in ("gq", "sq", "t", "v", "s", "ev", "es", "cc", "wa", "wb", "xq", "p", "da", "db", "ya", "yb")}
        blk = ctx.enter_context(nc.Block())
        c = C(nc, blk, sems)

        for i, v in enumerate(CONSTS):
            nc.const_aps.register(F32, v, cbank[:, i:i + 1])
        probe = nc.alloc_sbuf_tensor("arena_probe", [128, 8], F32)
        base = nc.lookup_mloc(probe).addr + 64
        ar = Arena(nc, base, nc.sbuf_top - 64)

        G = dict(xT=xT, xTh=xTh, wd=wd, sp=sp, svec=svec, dvec=dvec, cmask=cmask,
                 rmask=rmask, rmask1m=rmask1m, ebias=ebias, eye=eye, ps=ps, psb=psb,
                 ones_bf=ones_bf, onesr_f=onesr_f, onesr_bf=onesr_bf, neg128r=neg128r,
                 zrow=zrow, srow=srow, tiny=tiny, wbufA=wbufA, wbufB=wbufB, neg128f=neg128f,
                 cc_in=cc_in, cc_out=cc_out, cc_in16=cc_in16, cc_out16=cc_out16,
                 dbg=dbg, dbg_d=dbg_d, y=y_out)

        G["cbank"] = cbank
        G["consts_d"] = consts_d
        emit_setup(c, G, svec_d, cmask_d, rmask_d, rmask1m_d, ebias_d, eye_d)
        emit_l0_attn(c, ar, G)
        emit_l0_ffn(c, ar, G)
        emit_l1_attn(c, ar, G)
        emit_l1_ffn(c, ar, G)

        if dbg:
            @blk.gpsimd
            def _(e):
                for s in ("sq", "v", "es", "ev", "s"):
                    c.wait(e, s)
                for nm in dbg:
                    src = {"left": "left_sp", "u": "u_sp", "h": "h_sp", "P": "P_sp",
                           "x1": "x1_sp", "s0": "s0_sp"}.get(nm)
                    if src:
                        c.tag(e.dma_start(out=dbg_d[nm].rearrange("d t -> d t"),
                                          in_=sp[src].ap()), "gq")

        @blk.gpsimd
        def _(e):
            for sname in ("sq", "v", "es", "ev", "s", "gq", "wa", "wb", "xq",
                          "da", "db", "ya", "yb", "t", "p", "cc"):
                c.wait(e, sname)
    return nc


def emit_setup(c, G, svec_d, cmask_d, rmask_d, rmask1m_d, ebias_d, eye_d):
    blk, sems = c.blk, c.sems
    svec, dvec, tiny = G["svec"], G["dvec"], G["tiny"]

    @blk.sync
    def _(e):
        c.tag(e.dma_start(out=G["svec"][:], in_=svec_d), "sq")
        c.tag(e.dma_start(out=G["cmask"][:], in_=cmask_d), "sq")
        c.tag(e.dma_start(out=G["rmask"][:], in_=rmask_d), "sq")
        c.tag(e.dma_start(out=G["rmask1m"][:], in_=rmask1m_d), "sq")
        c.tag(e.dma_start(out=G["ebias"][:], in_=ebias_d), "sq")
        c.tag(e.dma_start(out=G["eye"][:], in_=eye_d), "sq")
        c.tag(e.dma_start(out=G["cbank"][:], in_=G["consts_d"]), "sq")

    @blk.vector
    def _(e):
        c.wait(e, "sq")
        e.memset(G["ones_bf"][:], 1.0)
        e.memset(G["onesr_f"][:], 1.0)
        e.memset(G["onesr_bf"][:], 1.0)
        e.memset(G["neg128r"][:], -128.0)
        e.memset(G["neg128f"][:], -128.0)
        e.memset(G["zrow"][:], 0.0)
        a = svec[:, SV["a_param"]:SV["a_param"] + 16]
        c.tag(e.tensor_scalar(tiny[:, 0:16], a, -1.0, None, ALU.mult), "v")

    @blk.scalar
    def _(e):
        c.wait(e, "v")
        c.tag(e.activation(tiny[:, 16:32], tiny[:, 0:16], AF.Exp), "s")

    @blk.vector
    def _(e):
        c.wait(e, "s")
        c.tag(e.tensor_copy(tiny[:, 32:48], tiny[:, 16:32]), "v")

    @blk.scalar
    def _(e):
        c.wait(e, "v")
        c.tag(e.activation(tiny[:, 48:64], tiny[:, 32:48], AF.Ln, bias=1.0), "s")

    @blk.vector
    def _(e):
        c.wait(e, "s")
        a = svec[:, SV["a_param"]:SV["a_param"] + 16]
        c.tag(e.tensor_tensor(tiny[:, 64:80], a, tiny[:, 48:64], ALU.add), "v")

    @blk.scalar
    def _(e):
        c.wait(e, "v")
        c.tag(e.mul(dvec[:, 0:16], tiny[:, 64:80], -8.0), "s")

    @blk.vector
    def _(e):
        c.wait(e, "s")
        sc = float(1.0 / np.sqrt(2048.0))
        e.tensor_scalar(dvec[:, 16:32], svec[:, SV["bqa"]:SV["bqa"] + 16], sc, None, ALU.mult)
        c.tag(e.tensor_scalar(dvec[:, 32:48], svec[:, SV["bka"]:SV["bka"] + 16], sc, None, ALU.mult), "v")



def emit_l0_attn(c, ar, G):
    nc, blk, sems = c.nc, c.blk, c.sems
    sp, wd, svec, dvec, ps = G["sp"], G["wd"], G["svec"], G["dvec"], G["ps"]

    def svc(name, i):
        return svec[:, SV[name] + i:SV[name] + i + 1]

    xn = ar.alloc("xn", [128, NKD, 3 + T], BF16)
    x_t = ar.alloc("x_t", [128, NKD, TW], BF16)
    sqb = ar.alloc("sqb", [128, TW], BF16)

    for it in range(5):
        ncols = 3 if it == 0 else TW
        off = 0 if it == 0 else 3 + (it - 1) * TW
        xsrc = (G["xTh"].rearrange("(k p) t -> p k t", p=128) if it == 0 else
                G["xT"].rearrange("(k p) t -> p k t", p=128)[:, :, (it - 1) * TW:it * TW])

        @blk.gpsimd
        def _(e, xsrc=xsrc, ncols=ncols):
            c.wait(e, "v")
            c.tag(e.dma_start(out=x_t[:, :, 0:ncols], in_=xsrc), "gq")

        v0, t0, s0 = c.n["v"], c.n["t"], c.n["s"]

        @blk.vector
        def _(e, ncols=ncols, v0=v0, t0=t0):
            c.wait(e, "gq")
            for k in range(NKD):
                e.wait_ge(sems["t"], t0 + k)
                c.tag(e.tensor_tensor(sqb[:, 0:ncols], x_t[:, k, 0:ncols],
                                      x_t[:, k, 0:ncols], ALU.mult), "v")

        @blk.tensor
        def _(e, ncols=ncols, v0=v0, s0=s0):
            c.use_bank(e, 6, "s", s0 + 2)
            for k in range(NKD):
                e.wait_ge(sems["v"], v0 + k + 1)
                c.tag(e.matmul(ps[6][0:1, 0:ncols], G["ones_bf"][:], sqb[:, 0:ncols],
                               start=(k == 0), stop=(k == NKD - 1)), "t")

        @blk.scalar
        def _(e, ncols=ncols):
            c.wait(e, "t")
            c.tag(e.activation(G["srow"][0:1, 0:ncols], ps[6][0:1, 0:ncols], AF.Ln,
                               bias=1e-8, scale=1.0 / 2048.0), "s")

        @blk.vector
        def _(e, ncols=ncols):
            c.wait(e, "s")
            c.tag(e.tensor_copy(G["srow"][0:1, 1536:1536 + ncols], G["srow"][0:1, 0:ncols]), "v")

        @blk.scalar
        def _(e, ncols=ncols):
            c.wait(e, "v")
            c.tag(e.activation(G["srow"][0:1, 512:512 + ncols], G["srow"][0:1, 1536:1536 + ncols],
                               AF.Exp, scale=-0.5), "s")

        @blk.tensor
        def _(e, ncols=ncols):
            c.wait(e, "s")
            c.use_bank(e, 5, "v", c.n["v"] + NKD)
            c.tag(e.matmul(ps[5][:, 0:ncols], G["onesr_f"][:],
                           G["srow"][0:1, 512:512 + ncols], start=True, stop=True), "t")

        @blk.vector
        def _(e, ncols=ncols, off=off):
            c.wait(e, "t")
            for k in range(NKD):
                c.tag(e.scalar_tensor_tensor(xn[:, k, off:off + ncols], x_t[:, k, 0:ncols],
                                             svc("norm1_w", k), ps[5][:, 0:ncols],
                                             ALU.mult, ALU.mult), "v")

    if "xn" in G["dbg"]:
        @blk.gpsimd
        def _(e):
            c.wait(e, "v")
            c.tag(e.dma_start(out=G["dbg_d"]["xn"].rearrange("(k p) t -> p k t", p=128),
                              in_=xn[:, :, 3:3 + T]), "gq")

    lrow = ar.alloc("lrow", [128, 2, T], BF16)
    st_base = c.n["sq"]

    def evac_left(e, m, g, pst):
        if m >= 2:
            e.wait_ge(sems["sq"], st_base + 16 * m)
        return e.activation(lrow[:, m % 2, g * TW:(g + 1) * TW], pst[:, 0:TW],
                            AF.Gelu, bias=svc("left_b", m))

    t0, e0, esem = mmloop(c, G, wd["w_left"], NKD, NKD,
                          lambda k, g: xn[:, k, 3 + g * TW:3 + (g + 1) * TW],
                          [TW] * 4, "s", evac_left, ready=[("v", c.n["v"])])

    @blk.sync
    def _(e):
        for m in range(NKD):
            e.wait_ge(sems["es"], e0 + (m + 1) * 4)
            c.tag(e.dma_start(out=sp["left_sp"].ap()[m * 128:(m + 1) * 128, :],
                              in_=lrow[:, m % 2, :]), "sq")

    ci_row = ar.alloc("ci_row", [128, 3 + T], BF16)
    urow = ar.alloc("urow", [128, 2, T], BF16)
    ust_base = c.n["sq"]
    vref = {}

    def rhs_right(k, g):
        if g == 0:
            return xn[:, k, 0:3]
        return xn[:, k, 3 + (g - 1) * TW:3 + g * TW]

    def evac_right(e, m, g, pst):
        if g == 0:
            if m >= 2:
                e.wait_ge(sems["sq"], ust_base + 16 * m)
            return e.tensor_scalar(ci_row[:, 0:3], pst[:, 0:3], svc("right_b", m),
                                   G["cmask"][:, 0:1], ALU.add, ALU.mult)
        li = e.tensor_scalar(ci_row[:, 3 + (g - 1) * TW:3 + g * TW], pst[:, 0:TW],
                             svc("right_b", m), None, ALU.add)
        if g == 4:
            e.tensor_scalar(urow[:, m % 2, :], ci_row[:, 0:T], svc("cw0", m),
                            svc("conv_b", m), ALU.mult, ALU.add)
            for kk in range(1, 4):
                li = e.scalar_tensor_tensor(urow[:, m % 2, :], ci_row[:, kk:kk + T],
                                            svc(f"cw{kk}", m), urow[:, m % 2, :],
                                            ALU.mult, ALU.add)
        return li

    t0r, e0r, _ = mmloop(c, G, wd["w_right"], NKD, NKD, rhs_right,
                         [3, TW, TW, TW, TW], "v", evac_right, ready=[("v", c.n["v"])])

    @blk.sync
    def _(e):
        for m in range(NKD):
            e.wait_ge(sems["ev"], e0r + (m + 1) * 5)
            c.tag(e.dma_start(out=sp["u_sp"].ap()[m * 128:(m + 1) * 128, :],
                              in_=urow[:, m % 2, :]), "sq")

    ar.freeb("xn", "x_t", "sqb", "lrow", "ci_row", "urow")

    emit_scan(c, ar, G)

    emit_gated_x1(c, ar, G)

    emit_l0_mlp(c, ar, G)


def emit_scan(c, ar, G):
    nc, blk, sems = c.nc, c.blk, c.sems
    sp, wd, svec, dvec, ps = G["sp"], G["wd"], G["svec"], G["dvec"], G["ps"]

    u = ar.alloc("u", [128, NKD, T], BF16)
    r_row = ar.alloc("r_row", [128, 2, T], F32)
    i_row = ar.alloc("i_row", [128, 2, T], F32)
    a_row = ar.alloc("a_row", [128, T], F32)
    w_row = ar.alloc("w_row", [128, T], F32)
    b_row = ar.alloc("b_row", [128, T], F32)
    h_row = ar.alloc("h_row", [128, T], F32)
    hP_bf = ar.alloc("hP_bf", [128, 2, T], BF16)
    agg = ar.alloc("agg", [128, 96], F32)
    ag_sb = ar.alloc("ag_sb", [128, 8, 32], F32)

    @blk.sync
    def _(e):
        c.wait(e, "sq")
        c.tag(e.dma_start(out=u[:], in_=sp["u_sp"].ap().rearrange("(k p) t -> p k t", p=128)), "sq")
    uld = c.n["sq"]

    t0, q0 = c.n["t"], c.n["gq"]
    e0 = c.n["es"]
    wrt_ga = wd["w_ga"].rearrange("(k p) m -> p k m", p=128)
    wrt_gx = wd["w_gx"].rearrange("(k p) m -> p k m", p=128)
    MW = 2
    NS = NKD // MW
    wv = [G["wbufA"].ap()[:, 0:32 * MW * 128].rearrange("p (k m) -> p k m", k=32),
          G["wbufB"].ap()[:, 0:32 * MW * 128].rearrange("p (k m) -> p k m", k=32)]

    wsem = ["wa", "wb"]
    wtarget = []

    @blk.gpsimd
    def _(e):
        for ms in range(NS):
            if ms >= 2:
                e.wait_ge(sems["t"], t0 + (ms - 1) * MW * 8)
            elif t0 > 0:
                e.wait_ge(sems["t"], t0)
            sm = wsem[ms % 2]
            e.dma_start(out=wv[ms % 2][:, 0:16, :],
                        in_=wrt_ga[:, :, ms * MW * 128:(ms + 1) * MW * 128]).then_inc(sems[sm], 16)
            e.dma_start(out=wv[ms % 2][:, 16:32, :],
                        in_=wrt_gx[:, :, ms * MW * 128:(ms + 1) * MW * 128]).then_inc(sems[sm], 16)
            c.n[sm] += 32
            wtarget.append((sm, c.n[sm]))

    @blk.tensor
    def _(e):
        e.wait_ge(sems["sq"], uld)
        idx = 0
        for ms in range(NS):
            e.wait_ge(sems[wtarget[ms][0]], wtarget[ms][1])
            for mc in range(MW):
                for br in range(2):
                    for g in range(4):
                        b = idx % 6
                        c.use_bank(e, b, "es", e0 + idx + 1)
                        for k in range(NKD):
                            mi = e.matmul(ps[b][:, 0:TW],
                                          wv[ms % 2][:, br * 16 + k, mc * 128:(mc + 1) * 128],
                                          u[:, k, g * TW:(g + 1) * TW],
                                          start=(k == 0), stop=(k == NKD - 1))
                        mi.then_inc(sems["t"], 1)
                        idx += 1
    c.n["t"] = t0 + NKD * 8

    vc0 = c.n["v"]
    st0 = c.n["sq"]
    for m in range(NKD):
        @blk.scalar
        def _(e, m=m):
            if m >= 2:
                e.wait_ge(sems["v"], vc0 + (m - 1) * 6)
            for br in range(2):
                dst = (r_row if br == 0 else i_row)
                bias = (svec[:, SV["ga_b"] + m:SV["ga_b"] + m + 1] if br == 0
                        else svec[:, SV["gx_b"] + m:SV["gx_b"] + m + 1])
                for g in range(4):
                    idx = m * 8 + br * 4 + g
                    b = idx % 6
                    e.wait_ge(sems["t"], t0 + idx + 1)
                    e.activation(dst[:, m % 2, g * TW:(g + 1) * TW], ps[b][:, 0:TW],
                                 AF.Sigmoid, bias=bias).then_inc(sems["es"], 1)
            c.n["es"] += 8

        @blk.vector
        def _(e, m=m):
            e.wait_ge(sems["es"], e0 + m * 8 + 8)
            if m >= 1 and "gates" in G["dbg"]:
                e.wait_ge(sems["sq"], st0 + 64 * m)
            c.tag(e.tensor_scalar(a_row[:], r_row[:, m % 2, :], dvec[:, m:m + 1], 0.0,
                                  ALU.mult, ALU.add, accum_out=agg[:, m:m + 1]), "v")

        @blk.scalar
        def _(e, m=m):
            c.wait(e, "v")
            c.tag(e.activation(w_row[:], a_row[:], AF.Exp, scale=2.0), "s")
            c.tag(e.activation(a_row[:], a_row[:], AF.Exp), "s")

        @blk.vector
        def _(e, m=m):
            c.wait(e, "s")
            c.tag(e.tensor_scalar(w_row[:], w_row[:], -1.0, 1.0, ALU.mult, ALU.add), "v")

        @blk.scalar
        def _(e, m=m):
            c.wait(e, "v")
            e.activation(w_row[:], w_row[:], AF.Ln)
            c.tag(e.activation(w_row[:], w_row[:], AF.Exp, scale=0.5), "s")

        @blk.vector
        def _(e, m=m):
            c.wait(e, "s")
            e.tensor_tensor(b_row[:], i_row[:, m % 2, :], u[:, m, :], ALU.mult)
            c.tag(e.tensor_tensor(b_row[:], b_row[:], w_row[:], ALU.mult), "v")
            if m >= 1:
                nst = 64 if "gates" in G["dbg"] else 32
                e.wait_ge(sems["sq"], st0 + nst * m)
            c.tag(e.tensor_tensor_scan(hP_bf[:, 0, :], a_row[:], G["zrow"][:],
                                       1.0, ALU.mult, ALU.add), "v")
            c.tag(e.tensor_tensor_scan(h_row[:], a_row[:], b_row[:],
                                       0.0, ALU.mult, ALU.add), "v")
            e.tensor_copy(hP_bf[:, 1, :], h_row[:])
            c.tag(e.tensor_copy(agg[:, 16 + m:17 + m], h_row[:, T - 1:T]), "v")

        @blk.sync
        def _(e, m=m):
            c.wait(e, "v")
            c.tag(e.dma_start(out=sp["P_sp"].ap()[m * 128:(m + 1) * 128, :],
                              in_=hP_bf[:, 0, :]), "sq")
            c.tag(e.dma_start(out=sp["h_sp"].ap()[m * 128:(m + 1) * 128, :],
                              in_=hP_bf[:, 1, :]), "sq")
            if "gates" in G["dbg"]:
                c.tag(e.dma_start(out=sp["a_sp"].ap()[m * 128:(m + 1) * 128, :],
                                  in_=a_row[:]), "sq")
                c.tag(e.dma_start(out=sp["b_sp"].ap()[m * 128:(m + 1) * 128, :],
                                  in_=b_row[:]), "sq")

        if m == 0 and "gates" in G["dbg"]:
            @blk.gpsimd
            def _(e):
                c.wait(e, "v"); c.wait(e, "s")
                dd = G["dbg_d"]["gates"]
                e.dma_start(out=dd[:, 0:T], in_=r_row[:, 0, :]).then_inc(sems["gq"], 16)
                e.dma_start(out=dd[:, T:2 * T], in_=i_row[:, 0, :]).then_inc(sems["gq"], 16)
                e.dma_start(out=dd[:, 2 * T:3 * T], in_=a_row[:]).then_inc(sems["gq"], 16)
                e.dma_start(out=dd[:, 3 * T:4 * T], in_=b_row[:]).then_inc(sems["gq"], 16)
                c.n["gq"] += 64

    @blk.scalar
    def _(e):
        c.wait(e, "v")
        c.tag(e.activation(agg[:, 32:48], agg[:, 0:16], AF.Exp), "s")

    @blk.vector
    def _(e):
        c.wait(e, "s")
        c.tag(e.tensor_copy(agg[:, 0:16], agg[:, 32:48]), "v")

    emit_allgather(c, G, agg[:, 0:32], 32, ag_sb)

    @blk.vector
    def _(e):
        c.wait(e, "gq")
        c.tag(e.memset(agg[:, 48:64], 0.0), "v")
    for r in range(8):
        @blk.vector
        def _(e, r=r):
            if r > 0:
                c.wait(e, "p")
            c.tag(e.tensor_scalar(agg[:, 0:16], ag_sb[:, r, 0:16], G["rmask"][:, r:r + 1],
                                  G["rmask1m"][:, r:r + 1], ALU.mult, ALU.add), "v")

        @blk.gpsimd
        def _(e, r=r):
            c.wait(e, "v")
            e.tensor_scalar(agg[:, 32:48], ag_sb[:, r, 16:32], G["rmask"][:, r:r + 1],
                            None, ALU.mult)
            c.tag(e.tensor_tensor(agg[:, 64:80], agg[:, 48:64], agg[:, 0:16], ALU.mult), "p")

        @blk.vector
        def _(e, r=r):
            c.wait(e, "p")
            c.tag(e.tensor_tensor(agg[:, 48:64], agg[:, 64:80], agg[:, 32:48], ALU.add), "v")

    ar.freeb("u", "r_row", "i_row", "a_row", "w_row", "b_row", "h_row", "hP_bf", "ag_sb")
    G["agg"] = agg


def emit_gated_x1(c, ar, G):
    nc, blk, sems = c.nc, c.blk, c.sems
    sp, wd, svec, ps = G["sp"], G["wd"], G["svec"], G["ps"]
    agg = G["agg"]

    gated = ar.alloc("gated", [128, NKD, T], BF16)
    left_t = ar.alloc("left_t", [128, NKD, TW], BF16)
    h_t = ar.alloc("h_t", [128, NKD, TW], BF16)
    P_t = ar.alloc("P_t", [128, NKD, TW], BF16)
    t2 = ar.alloc("t2", [128, TW], F32)

    ld0 = c.n["sq"]
    v0 = c.n["v"]

    @blk.sync
    def _(e):
        for nt in range(NT):
            if nt == 0:
                c.wait(e, "v")
                c.wait(e, "p")
            else:
                e.wait_ge(sems["v"], v0 + nt * 16)
            src = slice(nt * TW, (nt + 1) * TW)
            e.dma_start(out=left_t[:], in_=sp["left_sp"].ap().rearrange(
                "(k p) t -> p k t", p=128)[:, :, src]).then_inc(sems["sq"], 16)
            e.dma_start(out=h_t[:], in_=sp["h_sp"].ap().rearrange(
                "(k p) t -> p k t", p=128)[:, :, src]).then_inc(sems["sq"], 16)
            e.dma_start(out=P_t[:], in_=sp["P_sp"].ap().rearrange(
                "(k p) t -> p k t", p=128)[:, :, src]).then_inc(sems["sq"], 16)
    c.n["sq"] = ld0 + 48 * NT

    @blk.vector
    def _(e):
        for nt in range(NT):
            e.wait_ge(sems["sq"], ld0 + 48 * (nt + 1))
            for k in range(NKD):
                e.scalar_tensor_tensor(t2[:], P_t[:, k, :], agg[:, 48 + k:49 + k],
                                       h_t[:, k, :], ALU.mult, ALU.add)
                c.tag(e.tensor_tensor(gated[:, k, nt * TW:(nt + 1) * TW],
                                      left_t[:, k, :], t2[:], ALU.mult), "v")
    ar.freeb("left_t", "h_t", "P_t", "t2")

    if "gated" in G["dbg"]:
        @blk.gpsimd
        def _(e):
            c.wait(e, "v")
            c.tag(e.dma_start(out=G["dbg_d"]["gated"].rearrange("(k p) t -> p k t", p=128),
                              in_=gated[:]), "gq")

    xres = ar.alloc("xres", [128, NKD, T], BF16)
    x1row = ar.alloc("x1row", [128, 2, T], BF16)

    @blk.gpsimd
    def _(e):
        c.wait(e, "v")
        c.tag(e.dma_start(out=xres[:], in_=G["xT"].rearrange("(k p) t -> p k t", p=128)), "xq")
    xld = c.n["xq"]
    st0 = c.n["sq"]

    def evac_x1(e, m, g, pst):
        if m >= 2 and g == 0:
            e.wait_ge(sems["sq"], st0 + 16 * m)
        if m == 0 and g == 0:
            e.wait_ge(sems["xq"], xld)
        return e.scalar_tensor_tensor(x1row[:, m % 2, g * TW:(g + 1) * TW], pst[:, 0:TW],
                                      svec[:, SV["out_b"] + m:SV["out_b"] + m + 1],
                                      xres[:, m, g * TW:(g + 1) * TW], ALU.add, ALU.add)

    t0, e0, _ = mmloop(c, G, wd["w_out"], NKD, NKD,
                       lambda k, g: gated[:, k, g * TW:(g + 1) * TW],
                       [TW] * 4, "v", evac_x1, ready=[("v", c.n["v"])])

    @blk.sync
    def _(e):
        for m in range(NKD):
            e.wait_ge(sems["ev"], e0 + (m + 1) * 4)
            c.tag(e.dma_start(out=sp["x1_sp"].ap()[m * 128:(m + 1) * 128, :],
                              in_=x1row[:, m % 2, :]), "sq")

    ar.freeb("gated", "x1row", "xres")


def emit_l0_mlp(c, ar, G):
    nc, blk, sems = c.nc, c.blk, c.sems
    sp, wd, svec, ps = G["sp"], G["wd"], G["svec"], G["ps"]
    NF = 48

    x1_t = ar.alloc("x1_t", [128, NKD, TW], BF16)
    x1n_t = ar.alloc("x1n_t", [128, NKD, TW], BF16)
    mh = ar.alloc("mh", [128, NF, TW], BF16)
    m1b = ar.alloc("m1b", [128, TW], BF16)
    m2b = ar.alloc("m2b", [128, TW], BF16)
    zw = ar.alloc("zw", [128, NKD, TW], BF16)
    sqb2 = ar.alloc("sqb2", [128, TW], BF16)
    s0row = ar.alloc("s0row", [128, NKD, TW], BF16)
    xres_t = ar.alloc("xres_t", [128, NKD, TW], BF16)

    wrt1 = wd["w_up1"].rearrange("(k p) m -> p k m", p=128)
    wrt2 = wd["w_up2"].rearrange("(k p) m -> p k m", p=128)
    wrtd = wd["w_down"].rearrange("(k p) m -> p k m", p=128)
    MWU = 2
    wvU = [G["wbufA"].ap()[:, 0:32 * MWU * 128].rearrange("p (k m) -> p k m", k=32),
           G["wbufB"].ap()[:, 0:32 * MWU * 128].rearrange("p (k m) -> p k m", k=32)]
    wvD = [G["wbufA"].ap()[:, 0:48 * 128].rearrange("p (k m) -> p k m", k=48),
           G["wbufB"].ap()[:, 0:48 * 128].rearrange("p (k m) -> p k m", k=48)]


    for st in range(NT):
        ld0 = c.n["sq"]

        @blk.sync
        def _(e, st=st):
            c.wait(e, "v")
            c.wait(e, "ev")
            c.tag(e.dma_start(out=x1_t[:], in_=sp["x1_sp"].ap().rearrange(
                "(k p) t -> p k t", p=128)[:, :, st * TW:(st + 1) * TW]), "sq")

        @blk.gpsimd
        def _(e, st=st):
            c.wait(e, "ev")
            c.tag(e.dma_start(out=xres_t[:], in_=G["xT"].rearrange(
                "(k p) t -> p k t", p=128)[:, :, st * TW:(st + 1) * TW]), "xq")
        xres_ld = c.n["xq"]

        @blk.vector
        def _(e):
            c.wait(e, "sq")
            c.tag(e.engine_nop(), "v")

        def out_rms(e, k, A, Bn):
            return e.scalar_tensor_tensor(x1n_t[:, k, :], x1_t[:, k, :],
                                          svec[:, SV["norm2_w"] + k:SV["norm2_w"] + k + 1],
                                          A, ALU.mult, ALU.mult)

        emit_stats_ln(c, G, lambda k: x1_t[:, k, :], TW, out_rms, kind="rms", sq_buf=sqb2)

        t0, q0, e0s, e0v = c.n["t"], c.n["gq"], c.n["es"], c.n["ev"]
        MW = MWU
        NSf = NF // MW

        wsem = ["wa", "wb"]
        wtargetU = []

        @blk.gpsimd
        def _(e):
            for ms in range(NSf):
                if ms >= 2:
                    e.wait_ge(sems["t"], t0 + (ms - 1) * MW * 2)
                elif t0 > 0:
                    e.wait_ge(sems["t"], t0)
                sm = wsem[ms % 2]
                e.dma_start(out=wvU[ms % 2][:, 0:16, :],
                            in_=wrt1[:, :, ms * MW * 128:(ms + 1) * MW * 128]).then_inc(sems[sm], 16)
                e.dma_start(out=wvU[ms % 2][:, 16:32, :],
                            in_=wrt2[:, :, ms * MW * 128:(ms + 1) * MW * 128]).then_inc(sems[sm], 16)
                c.n[sm] += 32
                wtargetU.append((sm, c.n[sm]))

        @blk.tensor
        def _(e):
            e.wait_ge(sems["v"], c.n["v"])
            idx = 0
            for ms in range(NSf):
                e.wait_ge(sems[wtargetU[ms][0]], wtargetU[ms][1])
                for mc in range(MW):
                    f = ms * MW + mc
                    for br in range(2):
                        b = idx % 6
                        c.use_bank(e, b, "es", e0s + idx + 1)
                        for k in range(NKD):
                            mi = e.matmul(ps[b][:, 0:TW],
                                          wvU[ms % 2][:, br * 16 + k, mc * 128:(mc + 1) * 128],
                                          x1n_t[:, k, :], start=(k == 0), stop=(k == NKD - 1))
                        mi.then_inc(sems["t"], 1)
                        idx += 1
        c.n["t"] = t0 + NF * 2

        @blk.scalar
        def _(e):
            idx = 0
            for f in range(NF):
                for br in range(2):
                    b = idx % 6
                    e.wait_ge(sems["t"], t0 + idx + 1)
                    if f >= 1 and br == 0:
                        e.wait_ge(sems["ev"], e0v + f)
                    if br == 0:
                        e.activation(m1b[:], ps[b][:, 0:TW], AF.Gelu,
                                     bias=svec[:, SV["up1_b"] + f:SV["up1_b"] + f + 1]
                                     ).then_inc(sems["es"], 1)
                    else:
                        e.activation(m2b[:], ps[b][:, 0:TW], AF.Identity,
                                     bias=svec[:, SV["up2_b"] + f:SV["up2_b"] + f + 1]
                                     ).then_inc(sems["es"], 1)
                    idx += 1
        c.n["es"] = e0s + NF * 2

        @blk.vector
        def _(e):
            for f in range(NF):
                e.wait_ge(sems["es"], e0s + f * 2 + 2)
                e.tensor_tensor(mh[:, f, :], m1b[:], m2b[:], ALU.mult).then_inc(sems["ev"], 1)
        c.n["ev"] = e0v + NF

        t0d, q0d, e0d = c.n["t"], c.n["gq"], c.n["ev"]
        mhdone = c.n["ev"]

        wtargetD = []

        @blk.gpsimd
        def _(e):
            for ms in range(NKD):
                if ms >= 2:
                    e.wait_ge(sems["t"], t0d + (ms - 1))
                elif t0d > 0:
                    e.wait_ge(sems["t"], t0d)
                sm = wsem[ms % 2]
                e.dma_start(out=wvD[ms % 2],
                            in_=wrtd[:, :, ms * 128:(ms + 1) * 128]).then_inc(sems[sm], 16)
                c.n[sm] += 16
                wtargetD.append((sm, c.n[sm]))

        @blk.tensor
        def _(e):
            e.wait_ge(sems["ev"], mhdone)
            idx = 0
            for m in range(NKD):
                e.wait_ge(sems[wtargetD[m][0]], wtargetD[m][1])
                b = idx % 6
                c.use_bank(e, b, "ev", e0d + idx + 1)
                for k in range(NF):
                    mi = e.matmul(ps[b][:, 0:TW],
                                  wvD[m % 2][:, k, :],
                                  mh[:, k, :], start=(k == 0), stop=(k == NF - 1))
                mi.then_inc(sems["t"], 1)
                idx += 1
        c.n["t"] = t0d + NKD

        @blk.vector
        def _(e, st=st, xres_ld=xres_ld):
            e.wait_ge(sems["xq"], xres_ld)
            for m in range(NKD):
                b = m % 6
                e.wait_ge(sems["t"], t0d + m + 1)
                e.scalar_tensor_tensor(zw[:, m, :], ps[b][:, 0:TW],
                                       svec[:, SV["down_b"] + m:SV["down_b"] + m + 1],
                                       x1_t[:, m, :], ALU.add, ALU.add)
                c.tag(e.tensor_tensor(zw[:, m, :], zw[:, m, :],
                                      xres_t[:, m, :], ALU.add),
                      "ev")
        c.n["ev"] = e0d + NKD

        def out_ln(e, k, A, Bn):
            e.tensor_tensor(s0row[:, k, :], zw[:, k, :], A, ALU.mult)
            return e.tensor_tensor(s0row[:, k, :], s0row[:, k, :], Bn, ALU.subtract)

        emit_stats_ln(c, G, lambda k: zw[:, k, :], TW, out_ln, kind="ln", sq_buf=sqb2)

        @blk.sync
        def _(e, st=st):
            c.wait(e, "v")
            c.tag(e.dma_start(out=sp["s0_sp"].ap().rearrange(
                "(k p) t -> p k t", p=128)[:, :, st * TW:(st + 1) * TW],
                in_=s0row[:]), "sq")

    ar.freeb("x1_t", "x1n_t", "mh", "m1b", "m2b", "zw", "sqb2", "s0row", "xres_t", "agg")





def emit_l0_ffn(c, ar, G):
    nc, blk, sems = c.nc, c.blk, c.sems
    sp, wd, svec, ps, psb = G["sp"], G["wd"], G["svec"], G["ps"], G["psb"]
    NH = 11
    NTC = T // 128

    s0r = ar.alloc("s0r", [128, NKD, T], BF16)
    he = ar.alloc("he", [128, NH, T], BF16)
    pbc = ar.alloc("pbc", [128, T], F32)
    prow = ar.alloc("prow", [1, T], BF16)
    gwt = ar.alloc("gwt", [128, NKD * 8], BF16)
    sco = ar.alloc("sco", [128, NTC, 8], F32)
    sbx = ar.alloc("sbx", [128, NTC, 8], F32)
    mk1 = ar.alloc("mk1", [128, NTC, 8], F32)
    sb2 = ar.alloc("sb2", [128, NTC, 8], F32)
    mk2 = ar.alloc("mk2", [128, NTC, 8], F32)
    ptm = ar.alloc("ptm", [128, NTC, 8], BF16)
    mxc = ar.alloc("mxc", [128, 2 * NTC], F32)
    m1b = ar.alloc("fm1", [128, 2, TW], F32)
    m1c = ar.alloc("fm1c", [128, 2, TW], F32)
    m2b = ar.alloc("fm2", [128, 2, TW], F32)
    zb = ar.alloc("fzb", [128, 4, TW], F32)

    gv = gwt.ap().rearrange("p (k e) -> p k e", k=NKD)

    @blk.sync
    def _(e):
        c.wait(e, "sq")
        c.tag(e.dma_start(out=s0r[:], in_=sp["s0_sp"].ap().rearrange("(k p) t -> p k t", p=128)), "sq")
    s0ld = c.n["sq"]

    @blk.gpsimd
    def _(e):
        c.tag(e.dma_start(out=gv, in_=wd["w_gate0"].rearrange("(k p) e -> p k e", p=128)), "xq")
    gwld = c.n["xq"]

    @blk.tensor
    def _(e):
        e.wait_ge(sems["sq"], s0ld)
        e.wait_ge(sems["xq"], gwld)
        for tc in range(NTC):
            b = tc % 6
            c.wait_bank(e, b)
            for k in range(NKD):
                mi = e.matmul(ps[b][:, 0:8], s0r[:, k, tc * 128:(tc + 1) * 128],
                              gv[:, k, :], start=(k == 0), stop=(k == NKD - 1))
            c.tag(mi, "t")
    tsc = c.n["t"] - NTC

    @blk.scalar
    def _(e):
        for tc in range(NTC):
            b = tc % 6
            e.wait_ge(sems["t"], tsc + tc + 1)
            c.tag(e.activation(sco[:, tc, :], ps[b][0:128, 0:8], AF.Sigmoid), "es")
    for tc in range(NTC):
        c.free_bank(tc % 6, "es")

    @blk.vector
    def _(e):
        c.wait(e, "es")

        def flush():
            e.tensor_copy(m1b[:, 0, :], G["zrow"][:, 0:TW])

        for tc in range(NTC):
            e.tensor_tensor(sbx[:, tc, :], sco[:, tc, :], G["ebias"][:], ALU.add)
        flush()
        for tc in range(NTC):
            e.tensor_reduce(mxc[:, tc:tc + 1], sbx[:, tc, :], AX.X, ALU.max)
        flush()
        for tc in range(NTC):
            e.tensor_scalar(mk1[:, tc, :], sbx[:, tc, :], mxc[:, tc:tc + 1], None, ALU.is_ge)
        flush()
        for tc in range(NTC):
            e.scalar_tensor_tensor(sb2[:, tc, :], mk1[:, tc, :], -1e9,
                                   sbx[:, tc, :], ALU.mult, ALU.add)
        flush()
        for tc in range(NTC):
            e.tensor_reduce(mxc[:, NTC + tc:NTC + tc + 1], sb2[:, tc, :], AX.X, ALU.max)
        flush()
        for tc in range(NTC):
            e.tensor_scalar(mk2[:, tc, :], sb2[:, tc, :], mxc[:, NTC + tc:NTC + tc + 1],
                            None, ALU.is_ge)
        flush()
        for tc in range(NTC):
            e.tensor_tensor(mk1[:, tc, :], mk1[:, tc, :], mk2[:, tc, :], ALU.add)
        flush()
        for tc in range(NTC):
            e.tensor_tensor(ptm[:, tc, :], mk1[:, tc, :], sco[:, tc, :], ALU.mult)
        flush()
        c.tag(e.engine_nop(), "p")

    zr_base = [c.n["gq"]]
    dpar = ["da", "db"]
    for exi, ex in enumerate(F_EXPERTS):
        shared = (ex == 8)
        if shared:
            w1d = wd["w_sw1"]; w3d = wd["w_sw3"]; w2d = wd["w_sw2"]
        else:
            w1d = wd["w_e1"][ex]; w3d = wd["w_e3"][ex]; w2d = wd["w_e2"][ex]
        w1r = w1d.rearrange("(k p) m -> p k m", p=128)
        w3r = w3d.rearrange("(k p) m -> p k m", p=128)
        w2r = w2d.rearrange("(k p) m -> p k m", p=128)

        if not shared:
            tp0 = c.n["t"]
            vp0 = c.n["v"]

            NTC = T // 128
            NHH = (NTC + 7) // 8

            @blk.tensor
            def _(e, ex=ex, vp0=vp0, NTC=NTC, NHH=NHH):
                c.wait(e, "p")
                c.wait_bank(e, 7)
                for hh in range(NHH):
                    if hh >= 1:
                        e.wait_ge(sems["v"], vp0 + hh)
                    for tc in range(min(8, NTC - hh * 8)):
                        tci = hh * 8 + tc
                        mi = e.transpose(psb[0:1, tc * 128:(tc + 1) * 128],
                                         ptm[:, tci, ex:ex + 1], G["eye"][:])
                    c.tag(mi, "t")

            @blk.vector
            def _(e, tp0=tp0, NTC=NTC, NHH=NHH):
                for hh in range(NHH):
                    w = 128 * min(8, NTC - hh * 8)
                    e.wait_ge(sems["t"], tp0 + hh + 1)
                    c.tag(e.tensor_copy(prow[0:1, hh * 1024:hh * 1024 + w],
                                        psb[0:1, 0:w]), "v")
            c.free_bank(7, "v")

            tb0 = c.n["t"]

            @blk.tensor
            def _(e):
                c.wait(e, "v")
                for g in range(NT):
                    b = g % 6
                    c.wait_bank(e, b)
                    c.tag(e.matmul(ps[b][:, 0:TW], G["onesr_bf"][:],
                                   prow[0:1, g * TW:(g + 1) * TW], start=True, stop=True), "t")

            @blk.vector
            def _(e, tb0=tb0):
                for g in range(NT):
                    b = g % 6
                    e.wait_ge(sems["t"], tb0 + g + 1)
                    c.tag(e.tensor_copy(pbc[:, g * TW:(g + 1) * TW], ps[b][:, 0:TW]), "v")
            for g in range(NT):
                c.free_bank(g % 6, "v")

        wtg = []
        t0e = c.n["t"]

        @blk.gpsimd
        def _(e, t0e=t0e, w1r=w1r, w3r=w3r):
            for f in range(NH):
                if f >= 2:
                    e.wait_ge(sems["t"], t0e + (f - 1) * 2 * NT)
                elif t0e > 0:
                    e.wait_ge(sems["t"], t0e)
                sm = ("wa", "wb")[f % 2]
                e.dma_start(out=G["wbufA" if f % 2 == 0 else "wbufB"].ap()[:, 0:NKD * 128].rearrange(
                    "p (k m) -> p k m", k=NKD),
                    in_=w1r[:, :, f * 128:(f + 1) * 128]).then_inc(sems[sm], 16)
                e.dma_start(out=G["wbufA" if f % 2 == 0 else "wbufB"].ap()[:, NKD * 128:2 * NKD * 128].rearrange(
                    "p (k m) -> p k m", k=NKD),
                    in_=w3r[:, :, f * 128:(f + 1) * 128]).then_inc(sems[sm], 16)
                c.n[sm] += 32
                wtg.append((sm, c.n[sm]))

        for f in range(NH):
            bsel = []

            @blk.tensor
            def _(e, f=f):
                e.wait_ge(sems[wtg[f][0]], wtg[f][1])
                if f == 0:
                    e.wait_ge(sems["sq"], s0ld)
                wv1 = G["wbufA" if f % 2 == 0 else "wbufB"].ap()[:, 0:NKD * 128].rearrange(
                    "p (k m) -> p k m", k=NKD)
                wv3 = G["wbufA" if f % 2 == 0 else "wbufB"].ap()[:, NKD * 128:2 * NKD * 128].rearrange(
                    "p (k m) -> p k m", k=NKD)
                for g in range(NT):
                    for br, wv_ in ((0, wv1), (1, wv3)):
                        b = (f * 8 + g * 2 + br) % 6
                        bsel.append(b)
                        c.wait_bank(e, b)
                        for k in range(NKD):
                            mi = e.matmul(ps[b][:, 0:TW], wv_[:, k, :],
                                          s0r[:, k, g * TW:(g + 1) * TW],
                                          start=(k == 0), stop=(k == NKD - 1))
                        c.tag(mi, "t")
            tgrp = c.n["t"] - 2 * NT

            for g in range(NT):
                b1 = bsel[g * 2]
                b3 = bsel[g * 2 + 1]
                par = (f * NT + g) % 2

                @blk.vector
                def _(e, f=f, g=g, b1=b1, b3=b3, par=par, tgrp=tgrp, shared=shared):
                    e.wait_ge(sems["t"], tgrp + g * 2 + 2)
                    c.wait(e, "es")
                    if not shared:
                        e.tensor_tensor(m1b[:, par, :], ps[b1][:, 0:TW],
                                        pbc[:, g * TW:(g + 1) * TW], ALU.mult)
                        c.tag(e.tensor_tensor(m2b[:, par, :], ps[b3][:, 0:TW],
                                              pbc[:, g * TW:(g + 1) * TW], ALU.mult), "ev")
                    else:
                        e.tensor_copy(m1b[:, par, :], ps[b1][:, 0:TW])
                        c.tag(e.tensor_copy(m2b[:, par, :], ps[b3][:, 0:TW]), "ev")
                c.free_bank(b1, "ev")
                c.free_bank(b3, "ev")

                @blk.scalar
                def _(e, par=par):
                    c.wait(e, "ev")
                    c.tag(e.activation(m1c[:, par, :], m1b[:, par, :], AF.Silu), "es")

                @blk.vector
                def _(e, f=f, g=g, par=par):
                    c.wait(e, "es")
                    c.tag(e.tensor_tensor(he[:, f, g * TW:(g + 1) * TW],
                                          m1c[:, par, :], m2b[:, par, :], ALU.mult), "ev")

        t0d = c.n["t"]
        hedone = c.n["ev"]
        wtd = []
        ev_of_group = {}
        pend = []

        EXI = [exi]

        def emit_stores(e, ex):
            while pend:
                m_, g_, par_ = pend.pop(0)
                e.wait_ge(sems["ev"], ev_of_group[(m_, g_)])
                if EXI[0] > 0 and m_ == 0 and g_ == 0:
                    c.wait(e, "da")
                    c.wait(e, "db")
                op = ALU.bypass if EXI[0] == 0 else ALU.add
                dst = sp["zr_sp"].ap()[m_ * 128:(m_ + 1) * 128, g_ * TW:(g_ + 1) * TW]
                c.tag(e.dma_start(out=dst, in_=zb[:, par_, :], accum_op=op), dpar[par_ % 2])

        for m in range(NKD):
            @blk.gpsimd
            def _(e, m=m, t0d=t0d, ex=ex, w2r=w2r):
                emit_stores(e, ex)
                if m >= 2:
                    e.wait_ge(sems["t"], t0d + (m - 1) * NT)
                elif t0d > 0:
                    e.wait_ge(sems["t"], t0d)
                sm = ("wa", "wb")[m % 2]
                e.dma_start(out=G["wbufA" if m % 2 == 0 else "wbufB"].ap()[:, 0:NH * 128].rearrange(
                    "p (k m) -> p k m", k=NH),
                    in_=w2r[:, :, m * 128:(m + 1) * 128]).then_inc(sems[sm], 16)
                c.n[sm] += 16
                wtd.append((sm, c.n[sm]))

            bsel = []

            @blk.tensor
            def _(e, m=m, hedone=hedone):
                e.wait_ge(sems[wtd[m][0]], wtd[m][1])
                if m == 0:
                    e.wait_ge(sems["ev"], hedone)
                wv2 = G["wbufA" if m % 2 == 0 else "wbufB"].ap()[:, 0:NH * 128].rearrange(
                    "p (k m) -> p k m", k=NH)
                for g in range(NT):
                    b = (m * NT + g) % 6
                    bsel.append(b)
                    c.wait_bank(e, b)
                    for k in range(NH):
                        mi = e.matmul(ps[b][:, 0:TW], wv2[:, k, :],
                                      he[:, k, g * TW:(g + 1) * TW],
                                      start=(k == 0), stop=(k == NH - 1))
                    c.tag(mi, "t")
            tgrp = c.n["t"] - NT

            for g in range(NT):
                b = bsel[g]
                idx = m * NT + g
                par = idx % 4

                @blk.vector
                def _(e, b=b, par=par, tgrp=tgrp, g=g):
                    e.wait_ge(sems["t"], tgrp + g + 1)
                    c.wait(e, dpar[par % 2])
                    c.tag(e.tensor_copy(zb[:, par, :], ps[b][:, 0:TW]), "ev")
                c.free_bank(b, "ev")
                ev_of_group[(m, g)] = c.n["ev"]
                pend.append((m, g, par))

        @blk.gpsimd
        def _(e, ex=ex):
            emit_stores(e, ex)

    ar.freeb("he", "pbc", "prow", "gwt", "sco", "sbx", "mk1", "sb2", "mk2",
             "ptm", "mxc", "fm1", "fm1c", "fm2", "fzb")
    zrt = ar.alloc("zrt", [128, NKD, TW], BF16)
    zf = ar.alloc("zf", [128, NKD, TW], BF16)
    s1row = ar.alloc("s1row", [128, NKD, TW], BF16)
    sqf = ar.alloc("sqf", [128, TW], BF16)
    for nt in range(NT):
        @blk.gpsimd
        def _(e, nt=nt):
            c.wait(e, "da")
            c.wait(e, "db")
            if nt > 0:
                c.wait(e, "v")
            c.tag(e.dma_start(out=zrt[:], in_=sp["zr_sp"].ap().rearrange(
                "(k p) t -> p k t", p=128)[:, :, nt * TW:(nt + 1) * TW]), "xq")
        zld = c.n["xq"]

        @blk.vector
        def _(e, nt=nt, zld=zld):
            e.wait_ge(sems["xq"], zld)
            for k in range(NKD):
                c.tag(e.tensor_tensor(zf[:, k, :], zrt[:, k, :],
                                      s0r[:, k, nt * TW:(nt + 1) * TW], ALU.add), "v")

        def out_ln1(e, k, A, Bn, nt=nt):
            e.tensor_tensor(s1row[:, k, :], zf[:, k, :], A, ALU.mult)
            return e.tensor_tensor(s1row[:, k, :], s1row[:, k, :], Bn, ALU.subtract)

        emit_stats_ln(c, G, lambda k: zf[:, k, :], TW, out_ln1, kind="ln", sq_buf=sqf)

        @blk.sync
        def _(e, nt=nt):
            c.wait(e, "v")
            c.tag(e.dma_start(out=sp["s1_sp"].ap().rearrange(
                "(k p) t -> p k t", p=128)[:, :, nt * TW:(nt + 1) * TW], in_=s1row[:]), "sq")

    ar.freeb("s0r", "zrt", "zf", "s1row", "sqf")



def emit_l1_attn(c, ar, G):
    nc, blk, sems = c.nc, c.blk, c.sems
    sp, wd, svec, dvec, ps = G["sp"], G["wd"], G["svec"], G["dvec"], G["ps"]

    def svc(name, i):
        return svec[:, SV[name] + i:SV[name] + i + 1]

    hln = ar.alloc("hln", [128, NKD, T], BF16)
    s1t = ar.alloc("s1t", [128, NKD, TW], BF16)
    sqg = ar.alloc("sqg", [128, TW], BF16)
    for nt in range(NT):
        @blk.sync
        def _(e, nt=nt):
            c.wait(e, "sq")
            if nt > 0:
                c.wait(e, "v")
            c.tag(e.dma_start(out=s1t[:], in_=sp["s1_sp"].ap().rearrange(
                "(k p) t -> p k t", p=128)[:, :, nt * TW:(nt + 1) * TW]), "sq")
        sld = c.n["sq"]

        @blk.vector
        def _(e, sld=sld):
            e.wait_ge(sems["sq"], sld)
            c.tag(e.engine_nop(), "v")

        def out_g1(e, k, A, Bn, nt=nt):
            e.tensor_tensor(hln[:, k, nt * TW:(nt + 1) * TW], s1t[:, k, :], A, ALU.mult)
            e.tensor_tensor(hln[:, k, nt * TW:(nt + 1) * TW],
                            hln[:, k, nt * TW:(nt + 1) * TW], Bn, ALU.subtract)
            return e.tensor_scalar(hln[:, k, nt * TW:(nt + 1) * TW],
                                   hln[:, k, nt * TW:(nt + 1) * TW],
                                   svc("ln_g", k), svc("ln_b", k), ALU.mult, ALU.add)

        emit_stats_ln(c, G, lambda k: s1t[:, k, :], TW, out_g1, kind="ln", sq_buf=sqg)

    ar.freeb("s1t", "sqg")
    qr = ar.alloc("qr", [128, NKD, T], BF16)
    krow = ar.alloc("krow", [128, 2, T], BF16)

    def evac_q(e, m, g, pst):
        return e.activation(qr[:, m, g * TW:(g + 1) * TW], pst[:, 0:TW], AF.Identity,
                            bias=svc("bq", m))
    mmloop(c, G, wd["w_q"], NKD, NKD, lambda k, g: hln[:, k, g * TW:(g + 1) * TW],
           [TW] * 4, "s", evac_q, ready=[("v", c.n["v"])])

    for wnm, bnm, spnm in (("w_k", "bk", "k_sp"), ("w_v", "bv", "v_sp")):
        st_b = c.n["sq"]

        def evac_kv(e, m, g, pst, bnm=bnm, st_b=st_b):
            if m >= 2 and g == 0:
                e.wait_ge(sems["sq"], st_b + 16 * m)
            return e.activation(krow[:, m % 2, g * TW:(g + 1) * TW], pst[:, 0:TW],
                                AF.Identity, bias=svc(bnm, m))

        t0k, e0k, _ = mmloop(c, G, wd[wnm], NKD, NKD,
                             lambda k, g: hln[:, k, g * TW:(g + 1) * TW],
                             [TW] * 4, "s", evac_kv, ready=[("v", c.n["v"])])

        @blk.sync
        def _(e, e0k=e0k, spnm=spnm):
            for m in range(NKD):
                e.wait_ge(sems["es"], e0k + (m + 1) * 4)
                c.tag(e.dma_start(out=sp[spnm].ap()[m * 128:(m + 1) * 128, :],
                                  in_=krow[:, m % 2, :]), "sq")

    ar.freeb("hln")
    qw = ar.alloc("qw", [128, NKD, T], BF16)
    gsum = ar.alloc("gsum", [128, 96], F32)
    ag2 = ar.alloc("ag2", [128, 8, 16], F32)
    wrow = ar.alloc("wrow", [128, T], F32)
    w2row = ar.alloc("w2row", [128, T], F32)

    sc = float(1.0 / np.sqrt(2048.0))

    def evac_qa(e, m, g, pst):
        return e.activation(qw[:, m, g * TW:(g + 1) * TW], pst[:, 0:TW], AF.Exp,
                            scale=sc, bias=dvec[:, 16 + m:17 + m])

    mmloop(c, G, wd["w_qa"], NKD, NKD, lambda k, g: qr[:, k, g * TW:(g + 1) * TW],
           [TW] * 4, "s", evac_qa, ready=[("es", c.n["es"])])

    @blk.vector
    def _(e):
        c.wait(e, "es")
        for m in range(NKD):
            c.tag(e.tensor_reduce(gsum[:, m:m + 1], qw[:, m, :], AX.X, ALU.add), "v")

    emit_carry_exchange(c, G, gsum, ag2)

    @blk.vector
    def _(e):
        c.wait(e, "p")
        for m in range(NKD):
            e.tensor_tensor_scan(wrow[:], qw[:, m, :], G["zrow"][:], 0.0, ALU.add, ALU.add)
            e.tensor_scalar(wrow[:], wrow[:], gsum[:, 16 + m:17 + m], None, ALU.add)
            e.reciprocal(w2row[:], wrow[:])
            e.tensor_tensor(w2row[:], w2row[:], qw[:, m, :], ALU.mult)
            c.tag(e.tensor_tensor(qw[:, m, :], w2row[:], qr[:, m, :], ALU.mult), "v")
            c.tag(e.tensor_reduce(gsum[:, m:m + 1], qw[:, m, :], AX.X, ALU.add), "v")

    emit_carry_exchange(c, G, gsum, ag2)
    ar.freeb("qr")
    mixed = ar.alloc("mixed", [128, NKD, T], BF16)

    kvb = c.n["v"]

    @blk.sync
    def _(e):
        c.wait(e, "p")
        for m in range(NKD):
            if m >= 2:
                e.wait_ge(sems["v"], kvb + (m - 1))
            c.tag(e.dma_start(out=krow[:, m % 2, :],
                              in_=sp["k_sp"].ap()[m * 128:(m + 1) * 128, :]), "sq")
    kld0 = c.n["sq"] - 16 * NKD

    @blk.vector
    def _(e, kld0=kld0):
        c.wait(e, "p")
        for m in range(NKD):
            e.wait_ge(sems["sq"], kld0 + 16 * (m + 1))
            e.tensor_tensor_scan(wrow[:], qw[:, m, :], G["zrow"][:], 0.0, ALU.add, ALU.add)
            e.tensor_scalar(wrow[:], wrow[:], gsum[:, 16 + m:17 + m], None, ALU.add)
            c.tag(e.tensor_tensor(mixed[:, m, :], wrow[:], krow[:, m % 2, :], ALU.mult), "v")

    ar.freeb("qw")
    kw = ar.alloc("kw", [128, NKD, T], BF16)

    def evac_ka(e, m, g, pst):
        return e.activation(kw[:, m, g * TW:(g + 1) * TW], pst[:, 0:TW], AF.Exp,
                            scale=sc, bias=dvec[:, 32 + m:33 + m])

    mmloop(c, G, wd["w_ka"], NKD, NKD, lambda k, g: mixed[:, k, g * TW:(g + 1) * TW],
           [TW] * 4, "s", evac_ka, ready=[("v", c.n["v"])])

    @blk.vector
    def _(e):
        c.wait(e, "es")
        for m in range(NKD):
            c.tag(e.tensor_reduce(gsum[:, m:m + 1], kw[:, m, :], AX.X, ALU.add), "v")

    emit_carry_exchange(c, G, gsum, ag2)

    @blk.vector
    def _(e):
        c.wait(e, "p")
        for m in range(NKD):
            e.tensor_tensor_scan(wrow[:], kw[:, m, :], G["zrow"][:], 0.0, ALU.add, ALU.add)
            e.tensor_scalar(wrow[:], wrow[:], gsum[:, 16 + m:17 + m], None, ALU.add)
            e.reciprocal(w2row[:], wrow[:])
            e.tensor_tensor(w2row[:], w2row[:], kw[:, m, :], ALU.mult)
            c.tag(e.tensor_tensor(kw[:, m, :], w2row[:], mixed[:, m, :], ALU.mult), "v")
            c.tag(e.tensor_reduce(gsum[:, m:m + 1], kw[:, m, :], AX.X, ALU.add), "v")

    emit_carry_exchange(c, G, gsum, ag2)

    vld_b = c.n["sq"]
    kvb2 = c.n["v"]

    @blk.sync
    def _(e, kvb2=kvb2):
        c.wait(e, "p")
        for m in range(NKD):
            if m >= 2:
                e.wait_ge(sems["v"], kvb2 + (m - 1))
            c.tag(e.dma_start(out=krow[:, m % 2, :],
                              in_=sp["v_sp"].ap()[m * 128:(m + 1) * 128, :]), "sq")

    @blk.vector
    def _(e, vld_b=vld_b):
        c.wait(e, "p")
        for m in range(NKD):
            e.wait_ge(sems["sq"], vld_b + 16 * (m + 1))
            e.tensor_tensor_scan(wrow[:], kw[:, m, :], G["zrow"][:], 0.0, ALU.add, ALU.add)
            e.tensor_scalar(wrow[:], wrow[:], gsum[:, 16 + m:17 + m], None, ALU.add)
            c.tag(e.tensor_tensor(mixed[:, m, :], wrow[:], krow[:, m % 2, :], ALU.mult), "v")

    ar.freeb("kw", "wrow", "w2row", "krow", "gsum", "ag2")
    s1t = ar.alloc("s1tb", [128, NKD, TW], BF16)
    sqg = ar.alloc("sqgb", [128, TW], BF16)
    s2row = ar.alloc("s2row", [128, NKD, TW], BF16)
    zf2 = ar.alloc("zf2", [128, NKD, TW], BF16)
    for nt in range(NT):
        @blk.sync
        def _(e, nt=nt):
            c.wait(e, "v")
            c.tag(e.dma_start(out=s1t[:], in_=sp["s1_sp"].ap().rearrange(
                "(k p) t -> p k t", p=128)[:, :, nt * TW:(nt + 1) * TW]), "sq")
        sld = c.n["sq"]

        @blk.vector
        def _(e, nt=nt, sld=sld):
            e.wait_ge(sems["sq"], sld)
            for k in range(NKD):
                c.tag(e.tensor_tensor(zf2[:, k, :], s1t[:, k, :],
                                      mixed[:, k, nt * TW:(nt + 1) * TW], ALU.add), "v")

        def out_g7(e, k, A, Bn):
            e.tensor_tensor(s2row[:, k, :], zf2[:, k, :], A, ALU.mult)
            return e.tensor_tensor(s2row[:, k, :], s2row[:, k, :], Bn, ALU.subtract)

        emit_stats_ln(c, G, lambda k: zf2[:, k, :], TW, out_g7, kind="ln", sq_buf=sqg)

        @blk.sync
        def _(e, nt=nt):
            c.wait(e, "v")
            c.tag(e.dma_start(out=sp["s2_sp"].ap().rearrange(
                "(k p) t -> p k t", p=128)[:, :, nt * TW:(nt + 1) * TW], in_=s2row[:]), "sq")

    ar.freeb("s1tb", "sqgb", "mixed", "s2row", "zf2")


def emit_carry_exchange(c, G, gsum, ag2):
    blk, sems = c.blk, c.sems
    emit_allgather(c, G, gsum[:, 0:16], 16, ag2)

    @blk.vector
    def _(e):
        c.wait(e, "gq")
        c.tag(e.memset(gsum[:, 16:32], 0.0), "v")
    for r in range(8):
        if r % 2 == 0:
            @blk.vector
            def _(e, r=r):
                c.wait(e, "p")
                c.tag(e.tensor_scalar(gsum[:, 32:48], ag2[:, r, :], G["rmask"][:, r:r + 1],
                                      None, ALU.mult), "v")

            @blk.gpsimd
            def _(e, r=r):
                c.wait(e, "v")
                c.tag(e.tensor_tensor(gsum[:, 16:32], gsum[:, 16:32], gsum[:, 32:48],
                                      ALU.add), "p")
        else:
            @blk.gpsimd
            def _(e, r=r):
                c.wait(e, "v")
                c.wait(e, "p")
                c.tag(e.tensor_scalar(gsum[:, 48:64], ag2[:, r, :], G["rmask"][:, r:r + 1],
                                      None, ALU.mult), "p")

            @blk.vector
            def _(e, r=r):
                c.wait(e, "p")
                c.tag(e.tensor_tensor(gsum[:, 16:32], gsum[:, 16:32], gsum[:, 48:64],
                                      ALU.add), "v")
    @blk.vector
    def _(e):
        c.wait(e, "p")
        c.tag(e.engine_nop(), "p")



def emit_l1_ffn(c, ar, G):
    nc, blk, sems = c.nc, c.blk, c.sems
    sp, wd, svec, ps = G["sp"], G["wd"], G["svec"], G["ps"]
    NI = 64

    s2r = ar.alloc("s2r", [128, NKD, T], BF16)
    hbuf = ar.alloc("hbuf", [128, 2, 2, D], BF16)
    rs_bf = ar.alloc("rs_bf", [1, T], F32)
    hmrow = ar.alloc("hmrow", [128, 2, T], BF16)
    hm1 = ar.alloc("hm1", [128, 2, TW], F32)
    hm2 = ar.alloc("hm2", [128, 2, TW], F32)

    pc0 = c.n["xq"]
    st0 = c.n["sq"]

    @blk.gpsimd
    def _(e):
        c.wait(e, "v")
        c.wait(e, "sq")
        for i in range(32):
            if i >= 2:
                e.wait_ge(sems["sq"], st0 + 16 * (i - 1))
            c.tag(e.dma_start(out=hbuf[:, i % 2, :, :], in_=wd["w_d8"].rearrange(
                "(i p) d -> p i d", p=128)[:, i * 2:(i + 1) * 2, :]), "xq")

    @blk.sync
    def _(e):
        for i in range(32):
            e.wait_ge(sems["xq"], pc0 + 16 * (i + 1))
            c.tag(e.dma_start(out=sp["d8bf_sp"].ap().rearrange(
                "(i p) d -> p i d", p=128)[:, i * 2:(i + 1) * 2, :],
                in_=hbuf[:, i % 2, :, :]), "sq")

    @blk.sync
    def _(e):
        c.wait(e, "sq")
        c.tag(e.dma_start(out=s2r[:], in_=sp["s2_sp"].ap().rearrange(
            "(k p) t -> p k t", p=128)), "sq")
    s2ld = c.n["sq"]

    t0r = c.n["t"]

    @blk.tensor
    def _(e):
        e.wait_ge(sems["sq"], s2ld)
        for g in range(NT):
            b = g % 6
            c.wait_bank(e, b)
            for k in range(NKD):
                mi = e.matmul(ps[b][0:1, 0:TW], G["ones_bf"][:],
                              s2r[:, k, g * TW:(g + 1) * TW],
                              start=(k == 0), stop=(k == NKD - 1))
            c.tag(mi, "t")

    @blk.vector
    def _(e, t0r=t0r):
        for g in range(NT):
            b = g % 6
            e.wait_ge(sems["t"], t0r + g + 1)
            c.tag(e.tensor_copy(rs_bf[0:1, g * TW:(g + 1) * TW], ps[b][0:1, 0:TW]), "v")
    for g in range(NT):
        c.free_bank(g % 6, "v")

    g8r = wd["w_g8"].rearrange("(k p) m -> p k m", p=128)
    u8r = wd["w_u8"].rearrange("(k p) m -> p k m", p=128)
    wtg = []
    t0h = c.n["t"]
    rsdone = c.n["v"]

    @blk.gpsimd
    def _(e, t0h=t0h):
        for f in range(NI):
            if f >= 2:
                e.wait_ge(sems["t"], t0h + (f - 1) * 2 * NT)
            elif t0h > 0:
                e.wait_ge(sems["t"], t0h)
            sm = ("wa", "wb")[f % 2]
            e.dma_start(out=G["wbufA" if f % 2 == 0 else "wbufB"].ap()[:, 0:NKD * 128].rearrange(
                "p (k m) -> p k m", k=NKD),
                in_=g8r[:, :, f * 128:(f + 1) * 128]).then_inc(sems[sm], 16)
            e.dma_start(out=G["wbufA" if f % 2 == 0 else "wbufB"].ap()[:, NKD * 128:2 * NKD * 128].rearrange(
                "p (k m) -> p k m", k=NKD),
                in_=u8r[:, :, f * 128:(f + 1) * 128]).then_inc(sems[sm], 16)
            c.n[sm] += 32
            wtg.append((sm, c.n[sm]))

    hst0 = c.n["sq"]
    for f in range(NI):
        bsel = []

        @blk.tensor
        def _(e, f=f):
            e.wait_ge(sems[wtg[f][0]], wtg[f][1])
            if f == 0:
                e.wait_ge(sems["v"], rsdone)
            wvg = G["wbufA" if f % 2 == 0 else "wbufB"].ap()[:, 0:NKD * 128].rearrange(
                "p (k m) -> p k m", k=NKD)
            wvu = G["wbufA" if f % 2 == 0 else "wbufB"].ap()[:, NKD * 128:2 * NKD * 128].rearrange(
                "p (k m) -> p k m", k=NKD)
            for g in range(NT):
                for br, wv_ in ((0, wvg), (1, wvu)):
                    b = (f * 8 + g * 2 + br) % 6
                    bsel.append(b)
                    c.wait_bank(e, b)
                    for k in range(NKD):
                        e.matmul(ps[b][:, 0:TW], wv_[:, k, :],
                                 s2r[:, k, g * TW:(g + 1) * TW],
                                 start=(k == 0), stop=False)
                    mi = e.matmul(ps[b][:, 0:TW], G["neg128f"][:],
                                  rs_bf[0:1, g * TW:(g + 1) * TW],
                                  start=False, stop=True)
                    c.tag(mi, "t")
        tgrp = c.n["t"] - 2 * NT

        for g in range(NT):
            b1 = bsel[g * 2]
            b2 = bsel[g * 2 + 1]
            par = (f * NT + g) % 2

            @blk.scalar
            def _(e, f=f, g=g, b1=b1, b2=b2, par=par, tgrp=tgrp):
                e.wait_ge(sems["t"], tgrp + g * 2 + 2)
                c.wait(e, "ev")
                e.activation(hm1[:, par, :], ps[b1][:, 0:TW], AF.Silu,
                             scale=svec[:, SV["gscale"] + f:SV["gscale"] + f + 1])
                c.tag(e.activation(hm2[:, par, :], ps[b2][:, 0:TW], AF.Identity,
                                   scale=svec[:, SV["uscale"] + f:SV["uscale"] + f + 1]), "es")
            c.free_bank(b1, "es")
            c.free_bank(b2, "es")

            @blk.vector
            def _(e, f=f, g=g, par=par):
                c.wait(e, "es")
                if f >= 2 and g == 0:
                    e.wait_ge(sems["sq"], hst0 + 16 * (f - 1))
                c.tag(e.tensor_tensor(hmrow[:, f % 2, g * TW:(g + 1) * TW],
                                      hm1[:, par, :], hm2[:, par, :], ALU.mult), "ev")

        @blk.sync
        def _(e, f=f):
            c.wait(e, "ev")
            c.tag(e.dma_start(out=sp["hm_sp"].ap()[f * 128:(f + 1) * 128, :],
                              in_=hmrow[:, f % 2, :]), "sq")

    ar.freeb("hbuf", "s2r", "rs_bf", "hmrow", "hm1", "hm2")
    s2t = ar.alloc("s2t", [128, NKD, TW], BF16)
    hmt = ar.alloc("hmt", [128, NI, TW], BF16)
    rs2 = ar.alloc("rs2", [1, TW], F32)
    zq = ar.alloc("zq", [128, 2, TW], F32)
    zf3 = ar.alloc("zf3", [128, NKD, TW], BF16)
    yrow = ar.alloc("yrow", [128, NKD, TW], F32)
    sqh = ar.alloc("sqh", [128, TW], BF16)

    d8r = sp["d8bf_sp"].ap().rearrange("(k p) m -> p k m", p=128)
    for nt in range(NT):
        @blk.sync
        def _(e, nt=nt):
            c.wait(e, "sq")
            if nt > 0:
                c.wait(e, "v")
                c.wait(e, "t", c.n["t"])
            c.tag(e.dma_start(out=hmt[:], in_=sp["hm_sp"].ap().rearrange(
                "(k p) t -> p k t", p=128)[:, :, nt * TW:(nt + 1) * TW]), "sq")
            c.tag(e.dma_start(out=s2t[:], in_=sp["s2_sp"].ap().rearrange(
                "(k p) t -> p k t", p=128)[:, :, nt * TW:(nt + 1) * TW]), "sq")
        hld = c.n["sq"]

        t0s = c.n["t"]

        @blk.tensor
        def _(e, hld=hld):
            e.wait_ge(sems["sq"], hld)
            b = 6
            c.wait_bank(e, b)
            for k in range(NI):
                mi = e.matmul(ps[6][0:1, 0:TW], G["ones_bf"][:], hmt[:, k, :],
                              start=(k == 0), stop=(k == NI - 1))
            c.tag(mi, "t")

        @blk.vector
        def _(e, t0s=t0s):
            e.wait_ge(sems["t"], t0s + 1)
            c.tag(e.tensor_copy(rs2[0:1, :], ps[6][0:1, 0:TW]), "v")
        c.free_bank(6, "v")

        wtd = []
        t0d = c.n["t"]
        rdn = c.n["v"]

        @blk.sync
        def _(e, t0d=t0d):
            for m in range(NKD):
                if m >= 2:
                    e.wait_ge(sems["t"], t0d + (m - 1))
                elif t0d > 0:
                    e.wait_ge(sems["t"], t0d)
                sm = ("ya", "yb")[m % 2]
                e.dma_start(out=G["wbufA" if m % 2 == 0 else "wbufB"].ap()[:, 0:NI * 128].rearrange(
                    "p (k m) -> p k m", k=NI),
                    in_=d8r[:, :, m * 128:(m + 1) * 128]).then_inc(sems[sm], 16)
                c.n[sm] += 16
                wtd.append((sm, c.n[sm]))

        for m in range(NKD):
            @blk.tensor
            def _(e, m=m, rdn=rdn):
                e.wait_ge(sems[wtd[m][0]], wtd[m][1])
                if m == 0:
                    e.wait_ge(sems["v"], rdn)
                wvd = G["wbufA" if m % 2 == 0 else "wbufB"].ap()[:, 0:NI * 128].rearrange(
                    "p (k m) -> p k m", k=NI)
                b = m % 6
                c.wait_bank(e, b)
                for k in range(NI):
                    e.matmul(ps[b][:, 0:TW], wvd[:, k, :], hmt[:, k, :],
                             start=(k == 0), stop=False)
                mi = e.matmul(ps[b][:, 0:TW], G["neg128f"][:], rs2[0:1, 0:TW],
                              start=False, stop=True)
                c.tag(mi, "t")
            tv = c.n["t"]

            @blk.scalar
            def _(e, m=m, tv=tv):
                e.wait_ge(sems["t"], tv)
                c.wait(e, "v")
                c.tag(e.activation(zq[:, m % 2, :], ps[m % 6][:, 0:TW], AF.Identity,
                                   scale=svec[:, SV["dscale"] + m:SV["dscale"] + m + 1]), "es")
            c.free_bank(m % 6, "es")

            @blk.vector
            def _(e, m=m, nt=nt):
                c.wait(e, "es")
                c.tag(e.tensor_tensor(zf3[:, m, :], zq[:, m % 2, :],
                                      s2t[:, m, :], ALU.add), "v")

        def out_fin(e, k, A, Bn):
            e.tensor_tensor(yrow[:, k, :], zf3[:, k, :], A, ALU.mult)
            return e.tensor_tensor(yrow[:, k, :], yrow[:, k, :], Bn, ALU.subtract)

        emit_stats_ln(c, G, lambda k: zf3[:, k, :], TW, out_fin, kind="ln", sq_buf=sqh)

        @blk.sync
        def _(e, nt=nt):
            c.wait(e, "v")
            c.tag(e.dma_start(out=G["y"].rearrange(
                "(k p) t -> p k t", p=128)[:, :, nt * TW:(nt + 1) * TW], in_=yrow[:]), "sq")

    ar.freeb("s2t", "hmt", "rs2", "zq", "zf3", "yrow", "sqh")



def kernel(**inputs):
    sh, per = stage_host(inputs)
    nc = build_program(dbg=DBG)
    in_maps = []
    for cc in range(8):
        m = dict(sh)
        m.update(per[cc])
        in_maps.append(m)
    res = run_bass_kernel_spmd(nc, in_maps, core_ids=list(range(8)))
    outs = []
    for cc in range(8):
        outs.append(res.results[cc]["y"].T)
    full = np.empty((2, 8192, D), np.float32)
    for cc in range(8):
        g, p = cc // 4, cc % 4
        full[g, p * T:(p + 1) * T, :] = outs[cc]
    return full


# version 04
# speedup vs baseline: 1.0007x; 1.0007x over previous
import numpy as np
import ml_dtypes
from contextlib import ExitStack

import concourse.bass as bass
import concourse.mybir as mybir
from concourse.bass_utils import run_bass_kernel_spmd

F32 = mybir.dt.float32
BF16 = mybir.dt.bfloat16
I32 = mybir.dt.int32
AF = mybir.ActivationFunctionType
ALU = mybir.AluOpType
AX = mybir.AxisListType
BF = ml_dtypes.bfloat16

DBG = ()
F_EXPERTS = list(range(9))
D = 2048
TW = 512
T = 4 * TW
NKD = 16
NT = 4
F_UP = 6144
HS = 1408
I8 = 8192
NE = 8

SV = {}
_c = 0
for _n, _k in [("norm1_w", 16), ("left_b", 16), ("right_b", 16), ("conv_b", 16),
               ("cw0", 16), ("cw1", 16), ("cw2", 16), ("cw3", 16),
               ("ga_b", 16), ("gx_b", 16), ("a_param", 16), ("out_b", 16),
               ("norm2_w", 16), ("up1_b", 48), ("up2_b", 48), ("down_b", 16),
               ("ln_g", 16), ("ln_b", 16), ("bq", 16), ("bk", 16), ("bv", 16),
               ("bqa", 16), ("bka", 16),
               ("gscale", 64), ("uscale", 64), ("dscale", 16)]:
    SV[_n] = _c
    _c += _k
NSV = _c

CONSTS = [0.0, 1.0, 1e-8, 1e-5, 1.0 / 2048.0, -0.5, 0.5, -1.0, 2.0,
          float(1.0 / np.sqrt(2048.0))]



def _pack_vec(v):
    v = np.asarray(v, np.float32)
    return np.ascontiguousarray(v.reshape(v.shape[0] // 128, 128).T)


def _t(a):
    return np.ascontiguousarray(np.asarray(a).T)


def stage_host(inputs):
    sh = {}
    x = np.asarray(inputs["L0_attn_x"], np.float32)
    sv = np.zeros((128, NSV), np.float32)

    def put(name, v):
        p = _pack_vec(v)
        sv[:, SV[name]:SV[name] + p.shape[1]] = p

    put("norm1_w", inputs["L0_attn_norm1_w"]); put("left_b", inputs["L0_attn_left_b"])
    put("right_b", inputs["L0_attn_right_b"]); put("conv_b", inputs["L0_attn_conv_b"])
    for k in range(4):
        put(f"cw{k}", inputs["L0_attn_conv_w"][k])
    put("ga_b", inputs["L0_attn_ga_b"]); put("gx_b", inputs["L0_attn_gx_b"])
    put("a_param", inputs["L0_attn_a_param"]); put("out_b", inputs["L0_attn_out_b"])
    put("norm2_w", inputs["L0_attn_norm2_w"])
    put("up1_b", inputs["L0_attn_up1_b"]); put("up2_b", inputs["L0_attn_up2_b"])
    put("down_b", inputs["L0_attn_down_b"])
    put("ln_g", inputs["L1_attn_ln_g"]); put("ln_b", inputs["L1_attn_ln_b"])
    put("bq", inputs["L1_attn_bq"]); put("bk", inputs["L1_attn_bk"]); put("bv", inputs["L1_attn_bv"])
    put("bqa", inputs["L1_attn_bqa"]); put("bka", inputs["L1_attn_bka"])
    put("gscale", inputs["L1_ffn_gate_scale"]); put("uscale", inputs["L1_ffn_up_scale"])
    put("dscale", inputs["L1_ffn_down_scale"])
    sh["svec"] = sv

    sh["w_left"] = np.asarray(inputs["L0_attn_left_W"], np.float32)
    sh["w_right"] = np.asarray(inputs["L0_attn_right_W"], np.float32)
    sh["w_ga"] = np.asarray(inputs["L0_attn_ga_W"], np.float32)
    sh["w_gx"] = np.asarray(inputs["L0_attn_gx_W"], np.float32)
    sh["w_out"] = np.asarray(inputs["L0_attn_out_W"], np.float32)
    sh["w_up1"] = np.asarray(inputs["L0_attn_up1_W"], np.float32)
    sh["w_up2"] = np.asarray(inputs["L0_attn_up2_W"], np.float32)
    sh["w_down"] = np.asarray(inputs["L0_attn_down_W"], np.float32)
    sh["w_gate0"] = _t(inputs["L0_ffn_gate_w"])
    sh["w_sw1"] = _t(inputs["L0_ffn_sw1"]); sh["w_sw3"] = _t(inputs["L0_ffn_sw3"])
    sh["w_sw2"] = _t(inputs["L0_ffn_sw2"])
    sh["w_e1"] = np.ascontiguousarray(np.asarray(inputs["L0_ffn_w1"], np.float32).transpose(0, 2, 1))
    sh["w_e3"] = np.ascontiguousarray(np.asarray(inputs["L0_ffn_w3"], np.float32).transpose(0, 2, 1))
    sh["w_e2"] = np.ascontiguousarray(np.asarray(inputs["L0_ffn_w2"], np.float32).transpose(0, 2, 1))
    sh["ebias_rep"] = np.ascontiguousarray(
        np.broadcast_to(np.asarray(inputs["L0_ffn_expert_bias"], np.float32)[None, :], (128, 8)))
    sh["w_q"] = _t(inputs["L1_attn_Wq"]); sh["w_k"] = _t(inputs["L1_attn_Wk"])
    sh["w_v"] = _t(inputs["L1_attn_Wv"]); sh["w_qa"] = _t(inputs["L1_attn_Wqa"])
    sh["w_ka"] = _t(inputs["L1_attn_Wka"])
    sh["w_g8"] = _t(inputs["L1_ffn_gate_w"]); sh["w_u8"] = _t(inputs["L1_ffn_up_w"])
    sh["w_d8"] = _t(inputs["L1_ffn_down_w"])
    sh["eye"] = np.eye(128, dtype=BF)
    cst = np.zeros((128, 16), np.float32)
    for i, v in enumerate(CONSTS):
        cst[:, i] = v
    sh["consts"] = cst

    per = []
    for cc in range(8):
        g, p = cc // 4, cc % 4
        t0 = p * T
        m = dict()
        m["xT"] = np.ascontiguousarray(x[g, t0:t0 + T, :].T)
        m["xTh"] = (np.zeros((D, 3), np.float32) if p == 0
                    else np.ascontiguousarray(x[g, t0 - 3:t0, :].T))
        m["cmask"] = np.full((128, 4), 0.0 if p == 0 else 1.0, np.float32)
        rm = np.zeros((128, 8), np.float32)
        for r in range(8):
            if r // 4 == g and (r % 4) < p:
                rm[:, r] = 1.0
        m["rmask"] = rm
        m["rmask1m"] = np.ascontiguousarray(1.0 - rm)
        per.append(m)
    return sh, per



SIM_ARENA = False


class Arena:
    def __init__(self, nc, base, top):
        self.nc = nc
        if SIM_ARENA:
            top = base + (1 << 22)
        self.free = [(base, top)]
        self.live = {}

    def alloc(self, name, shape, dtype):
        nb = int(np.prod(shape[1:])) * mybir.dt.size(dtype)
        nb = (nb + 31) & ~31
        for i, (lo, hi) in enumerate(self.free):
            if hi - lo >= nb:
                if lo + nb == hi:
                    del self.free[i]
                else:
                    self.free[i] = (lo + nb, hi)
                h = self.nc.alloc_sbuf_tensor_at(name, shape, dtype, offset=lo)
                self.live[name] = (lo, nb, h)
                return h
        raise RuntimeError(f"arena OOM {name}: need {nb}B, free={self.free}")

    def freeb(self, *names):
        for name in names:
            lo, nb, _ = self.live.pop(name)
            if SIM_ARENA:
                continue
            self.free.append((lo, lo + nb))
        self.free.sort()
        out = []
        for lo, hi in self.free:
            if out and out[-1][1] == lo:
                out[-1] = (out[-1][0], hi)
            else:
                out.append((lo, hi))
        self.free = out


class C:
    def __init__(self, nc, blk, sems):
        self.nc = nc
        self.blk = blk
        self.sems = sems
        self.n = {k: 0 for k in sems}
        self.bank_last = {}

    def tag(self, instr, sem, k=None):
        k = k if k is not None else (16 if sem in ("gq", "sq", "wa", "wb", "xq", "da", "db", "ya", "yb") else 1)
        instr.then_inc(self.sems[sem], k)
        self.n[sem] += k
        return self.n[sem]

    def wait(self, e, sem, val=None):
        v = self.n[sem] if val is None else val
        if v > 0:
            e.wait_ge(self.sems[sem], v)

    def use_bank(self, t, b, free_sem, free_val):
        prev = self.bank_last.get(b)
        if prev is not None:
            t.wait_ge(self.sems[prev[0]], prev[1])
        self.bank_last[b] = (free_sem, free_val)

    def wait_bank(self, t, b):
        prev = self.bank_last.get(b)
        if prev is not None:
            t.wait_ge(self.sems[prev[0]], prev[1])

    def free_bank(self, b, sem):
        self.bank_last[b] = (sem, self.n[sem])

    def barrier(self):
        vals = dict(self.n)
        for secname in ("tensor", "vector", "scalar", "gpsimd", "sync"):
            dec = getattr(self.blk, secname)

            @dec
            def _(e, vals=vals):
                for s, v in vals.items():
                    if v > 0:
                        e.wait_ge(self.sems[s], v)


def mmloop(c, G, wdram, NK, NM, rhs, groups, evac_eng, evac, MW=4, extra=None,
           wq="gq", ready=()):
    blk, sems, ps = c.blk, c.sems, G["ps"]
    NG = len(groups)
    esem = "es" if evac_eng == "s" else "ev"
    t0, e0, q0 = c.n["t"], c.n[esem], c.n[wq]
    if MW * NK * 128 > 8192:
        MW = max(1, 8192 // (NK * 128))
    NS = NM // MW
    assert NS * MW == NM, (NM, MW)
    wrt = wdram.rearrange("(k p) m -> p k m", p=128)
    wv = [G["wbufA"].ap()[:, 0:NK * MW * 128].rearrange("p (k m) -> p k m", k=NK),
          G["wbufB"].ap()[:, 0:NK * MW * 128].rearrange("p (k m) -> p k m", k=NK)]
    qeng = "gpsimd" if wq == "gq" else "sync"

    wsem = ["wa", "wb"]
    wtarget = []

    @getattr(blk, qeng)
    def _(e):
        for ms in range(NS):
            if ms >= 2:
                e.wait_ge(sems["t"], t0 + (ms - 1) * MW * NG)
            elif t0 > 0:
                e.wait_ge(sems["t"], t0)
            sm = wsem[ms % 2]
            e.dma_start(out=wv[ms % 2],
                        in_=wrt[:, :, ms * MW * 128:(ms + 1) * MW * 128]
                        ).then_inc(sems[sm], 16)
            c.n[sm] += 16
            wtarget.append((sm, c.n[sm]))

    @blk.tensor
    def _(e):
        for sm, vl in ready:
            if vl > 0:
                e.wait_ge(sems[sm], vl)
        idx = 0
        for ms in range(NS):
            e.wait_ge(sems[wtarget[ms][0]], wtarget[ms][1])
            for mc in range(MW):
                m = ms * MW + mc
                for g, ncols in enumerate(groups):
                    b = (idx % 6)
                    c.use_bank(e, b, esem, e0 + idx + 1)
                    for k in range(NK):
                        mi = e.matmul(ps[b][:, 0:ncols],
                                      wv[ms % 2][:, k, mc * 128:(mc + 1) * 128],
                                      rhs(k, g),
                                      start=(k == 0),
                                      stop=(k == NK - 1 and extra is None))
                    if extra is not None:
                        mi = extra(e, ps[b], m, g)
                    mi.then_inc(sems["t"], 1)
                    idx += 1
    c.n["t"] = t0 + NM * NG

    sec = blk.scalar if evac_eng == "s" else blk.vector

    @sec
    def _(e):
        idx = 0
        for m in range(NM):
            for g, ncols in enumerate(groups):
                b = (idx % 6)
                e.wait_ge(sems["t"], t0 + idx + 1)
                evac(e, m, g, ps[b]).then_inc(sems[esem], 1)
                idx += 1
    c.n[esem] = e0 + NM * NG
    return t0, e0, esem


def emit_stats_ln(c, G, zsrc_fn, ncols, out_fn, *, kind, sq_buf, zbf_buf=None):
    blk, sems, ps = c.blk, c.sems, G["ps"]
    srow, ones_bf, onesr_f = G["srow"], G["ones_bf"], G["onesr_f"]
    v0, t0, s0 = c.n["v"], c.n["t"], c.n["s"]

    @blk.vector
    def _(e):
        for k in range(NKD):
            e.wait_ge(sems["t"], t0 + k)
            if zbf_buf is not None:
                e.tensor_copy(zbf_buf[:, 0:ncols], zsrc_fn(k))
                src = zbf_buf[:, 0:ncols]
            else:
                src = zsrc_fn(k)
            c.tag(e.tensor_tensor(sq_buf[:, 0:ncols], src, src, ALU.mult), "v")

    @blk.tensor
    def _(e):
        c.use_bank(e, 6, "s", s0 + 1)
        if kind == "ln":
            c.use_bank(e, 4, "v", c.n["v"] + NKD + 2)
        for k in range(NKD):
            e.wait_ge(sems["v"], v0 + k + 1)
            mi = e.matmul(ps[6][0:1, 0:ncols], ones_bf[:], sq_buf[:, 0:ncols],
                          start=(k == 0), stop=(k == NKD - 1))
            if kind == "ln":
                src = zbf_buf[:, 0:ncols] if zbf_buf is not None else zsrc_fn(k)
                mi = e.matmul(ps[4][0:1, 0:ncols], ones_bf[:], src,
                              start=(k == 0), stop=(k == NKD - 1))
            c.tag(mi, "t")

    if kind == "rms":
        @blk.scalar
        def _(e):
            c.wait(e, "t")
            c.tag(e.activation(srow[0:1, 0:ncols], ps[6][0:1, 0:ncols], AF.Ln,
                               bias=1e-8, scale=1.0 / 2048.0), "s")

        @blk.vector
        def _(e):
            c.wait(e, "s")
            c.tag(e.tensor_copy(srow[0:1, 1536:1536 + ncols], srow[0:1, 0:ncols]), "v")

        @blk.scalar
        def _(e):
            c.wait(e, "v")
            c.tag(e.activation(G["srowb"][0:1, 0:ncols], srow[0:1, 1536:1536 + ncols],
                               AF.Exp, scale=-0.5), "s")
        @blk.tensor
        def _(e):
            c.wait(e, "s")
            c.use_bank(e, 5, "v", c.n["v"] + NKD)
            c.tag(e.matmul(ps[5][:, 0:ncols], G["onesr_bf"][:],
                           G["srowb"][0:1, 0:ncols], start=True, stop=True), "t")

        @blk.vector
        def _(e):
            c.wait(e, "t")
            for k in range(NKD):
                c.tag(out_fn(e, k, ps[5][:, 0:ncols], None), "v")
    else:
        @blk.vector
        def _(e):
            c.wait(e, "t")
            e.tensor_scalar(srow[0:1, 1024:1024 + ncols], ps[4][0:1, 0:ncols],
                            1.0 / 2048.0, None, ALU.mult)
            e.tensor_scalar(srow[0:1, 0:ncols], ps[6][0:1, 0:ncols],
                            1.0 / 2048.0, None, ALU.mult)
            e.tensor_tensor(srow[0:1, 512:512 + ncols], srow[0:1, 1024:1024 + ncols],
                            srow[0:1, 1024:1024 + ncols], ALU.mult)
            c.tag(e.tensor_tensor(srow[0:1, 0:ncols], srow[0:1, 0:ncols],
                                  srow[0:1, 512:512 + ncols], ALU.subtract), "v")

        @blk.scalar
        def _(e):
            c.wait(e, "v")
            c.tag(e.activation(srow[0:1, 512:512 + ncols], srow[0:1, 0:ncols],
                               AF.Ln, bias=1e-5), "s")

        @blk.vector
        def _(e):
            c.wait(e, "s")
            c.tag(e.tensor_copy(srow[0:1, 1536:1536 + ncols], srow[0:1, 512:512 + ncols]), "v")

        @blk.scalar
        def _(e):
            c.wait(e, "v")
            c.tag(e.activation(G["srowb"][0:1, 0:ncols], srow[0:1, 1536:1536 + ncols],
                               AF.Exp, scale=-0.5), "s")

        @blk.vector
        def _(e):
            c.wait(e, "s")
            c.tag(e.tensor_tensor(G["srowb"][0:1, 512:512 + ncols],
                                  srow[0:1, 1024:1024 + ncols],
                                  G["srowb"][0:1, 0:ncols], ALU.mult), "v")

        @blk.tensor
        def _(e):
            c.wait(e, "v")
            c.use_bank(e, 5, "v", c.n["v"] + NKD)
            e.matmul(ps[5][:, 0:ncols], G["onesr_bf"][:], G["srowb"][0:1, 0:ncols],
                     start=True, stop=True)
            c.use_bank(e, 3, "v", c.n["v"] + NKD)
            c.tag(e.matmul(ps[3][:, 0:ncols], G["onesr_bf"][:],
                           G["srowb"][0:1, 512:512 + ncols], start=True, stop=True), "t")

        @blk.vector
        def _(e):
            c.wait(e, "t")
            for k in range(NKD):
                c.tag(out_fn(e, k, ps[5][:, 0:ncols], ps[3][:, 0:ncols]), "v")


def emit_allgather(c, G, src_ap, width, dst_sb):
    blk, sems = c.blk, c.sems
    if width == 32:
        cc_in, cc_out = G["cc_in"], G["cc_out"]
    else:
        cc_in, cc_out = G["cc_in16"], G["cc_out16"]

    @blk.gpsimd
    def _(e):
        c.wait(e, "v")
        c.wait(e, "es")
        c.tag(e.dma_start(out=cc_in.ap()[:, 0:width], in_=src_ap), "gq")
        c.wait(e, "gq")
        c.tag(e.collective_compute(
            "AllGather", ALU.bypass,
            replica_groups=[[0, 1, 2, 3, 4, 5, 6, 7]],
            ins=[cc_in.ap()[:, 0:width].opt()],
            outs=[cc_out.ap()[:, 0:width].opt()],
        ), "cc")
        c.wait(e, "cc")
        c.tag(e.dma_start(
            out=dst_sb[:, :, 0:width],
            in_=cc_out.ap()[:, 0:width].rearrange("(r p) w -> p r w", p=128)), "gq")



def build_program(dbg=()):
    nc = bass.Bass("TRN2", target_bir_lowering=False)

    xT = nc.dram_tensor("xT", [D, T], F32, kind="ExternalInput").ap()
    xTh = nc.dram_tensor("xTh", [D, 3], F32, kind="ExternalInput").ap()
    svec_d = nc.dram_tensor("svec", [128, NSV], F32, kind="ExternalInput").ap()
    cmask_d = nc.dram_tensor("cmask", [128, 4], F32, kind="ExternalInput").ap()
    rmask_d = nc.dram_tensor("rmask", [128, 8], F32, kind="ExternalInput").ap()
    rmask1m_d = nc.dram_tensor("rmask1m", [128, 8], F32, kind="ExternalInput").ap()
    ebias_d = nc.dram_tensor("ebias_rep", [128, 8], F32, kind="ExternalInput").ap()
    eye_d = nc.dram_tensor("eye", [128, 128], BF16, kind="ExternalInput").ap()
    consts_d = nc.dram_tensor("consts", [128, 16], F32, kind="ExternalInput").ap()
    wd = {}
    for nm, shp, dt in [
        ("w_left", [D, D], F32), ("w_right", [D, D], F32), ("w_ga", [D, D], F32),
        ("w_gx", [D, D], F32), ("w_out", [D, D], F32), ("w_up1", [D, F_UP], F32),
        ("w_up2", [D, F_UP], F32), ("w_down", [F_UP, D], F32),
        ("w_gate0", [D, 8], F32), ("w_sw1", [D, HS], F32), ("w_sw3", [D, HS], F32),
        ("w_sw2", [HS, D], F32),
        ("w_e1", [NE, D, HS], F32), ("w_e3", [NE, D, HS], F32), ("w_e2", [NE, HS, D], F32),
        ("w_q", [D, D], F32), ("w_k", [D, D], F32), ("w_v", [D, D], F32),
        ("w_qa", [D, D], F32), ("w_ka", [D, D], F32),
        ("w_g8", [D, I8], I32), ("w_u8", [D, I8], I32), ("w_d8", [I8, D], I32),
    ]:
        wd[nm] = nc.dram_tensor(nm, shp, dt, kind="ExternalInput").ap()
    y_out = nc.dram_tensor("y", [D, T], F32, kind="ExternalOutput").ap()

    sp = {}
    for nm, shp, dt in [
        ("left_sp", [D, T], BF16), ("u_sp", [D, T], BF16), ("h_sp", [D, T], BF16),
        ("P_sp", [D, T], BF16), ("x1_sp", [D, T], BF16), ("s0_sp", [D, T], BF16),
        ("zr_sp", [D, T], F32), ("s1_sp", [D, T], BF16), ("k_sp", [D, T], BF16),
        ("v_sp", [D, T], BF16), ("s2_sp", [D, T], BF16),
        ("hm_sp", [I8, T], BF16), ("d8bf_sp", [I8, D], BF16),
        ("a_sp", [D, T], F32), ("b_sp", [D, T], F32),
    ]:
        sp[nm] = nc.dram_tensor(nm, shp, dt, kind="Internal")
    cc_in = nc.dram_tensor("cc_in", [128, 32], F32, kind="Internal")
    cc_out = nc.dram_tensor("cc_out", [1024, 32], F32, kind="Internal")
    cc_in16 = nc.dram_tensor("cc_in16", [128, 16], F32, kind="Internal")
    cc_out16 = nc.dram_tensor("cc_out16", [1024, 16], F32, kind="Internal")
    dbg_d = {nm: nc.dram_tensor("dbg_" + nm, [D, T], F32, kind="ExternalOutput").ap()
             for nm in dbg if nm != "gates"}
    if "gates" in dbg:
        dbg_d["gates"] = nc.dram_tensor("dbg_gates", [128, 4 * T], F32, kind="ExternalOutput").ap()
        dbg_d["ga"] = nc.dram_tensor("dbg_ga", [D, T], F32, kind="ExternalOutput").ap()
        dbg_d["gb"] = nc.dram_tensor("dbg_gb", [D, T], F32, kind="ExternalOutput").ap()

    with ExitStack() as ctx:
        svec = nc.alloc_sbuf_tensor("svt", [128, NSV], F32)
        dvec = nc.alloc_sbuf_tensor("dvt", [128, 48], F32)
        cmask = nc.alloc_sbuf_tensor("cmask_t", [128, 4], F32)
        rmask = nc.alloc_sbuf_tensor("rmask_t", [128, 8], F32)
        rmask1m = nc.alloc_sbuf_tensor("rmask1m_t", [128, 8], F32)
        ebias = nc.alloc_sbuf_tensor("ebias_t", [128, 8], F32)
        eye = nc.alloc_sbuf_tensor("eye_t", [128, 128], BF16)
        ones_bf = nc.alloc_sbuf_tensor("ones_bf", [128, 1], BF16)
        onesr_f = nc.alloc_sbuf_tensor("onesr_f", [1, 128], F32)
        onesr_bf = nc.alloc_sbuf_tensor("onesr_bf", [1, 128], BF16)
        neg128r = nc.alloc_sbuf_tensor("neg128r", [1, 128], BF16)
        neg128f = nc.alloc_sbuf_tensor("neg128f", [1, 128], F32)
        zrow = nc.alloc_sbuf_tensor("zrow", [128, T], BF16)
        wbufA = nc.alloc_sbuf_tensor("wbufA", [128, 8192], BF16)
        wbufB = nc.alloc_sbuf_tensor("wbufB", [128, 8192], BF16)
        srow = nc.alloc_sbuf_tensor("srow", [1, 2048], F32)
        srowb = nc.alloc_sbuf_tensor("srowb", [1, 2048], BF16)
        cbank = nc.alloc_sbuf_tensor("cbank", [128, 16], F32)
        tiny = nc.alloc_sbuf_tensor("tinyt", [128, 96], F32)
        ps = [nc.alloc_psum_tensor(f"pb{i}", [128, 512], F32) for i in range(7)]
        psb = nc.alloc_psum_tensor("pbb", [128, 1024], BF16)
        sems = {s: ctx.enter_context(nc.semaphore("sem_" + s))
                for s in ("gq", "sq", "t", "v", "s", "ev", "es", "cc", "wa", "wb", "xq", "p", "da", "db", "ya", "yb")}
        blk = ctx.enter_context(nc.Block())
        c = C(nc, blk, sems)

        for i, v in enumerate(CONSTS):
            nc.const_aps.register(F32, v, cbank[:, i:i + 1])
        probe = nc.alloc_sbuf_tensor("arena_probe", [128, 8], F32)
        base = nc.lookup_mloc(probe).addr + 64
        ar = Arena(nc, base, nc.sbuf_top - 64)

        G = dict(xT=xT, xTh=xTh, wd=wd, sp=sp, svec=svec, dvec=dvec, cmask=cmask,
                 rmask=rmask, rmask1m=rmask1m, ebias=ebias, eye=eye, ps=ps, psb=psb,
                 ones_bf=ones_bf, onesr_f=onesr_f, onesr_bf=onesr_bf, neg128r=neg128r,
                 zrow=zrow, srow=srow, srowb=srowb, tiny=tiny, wbufA=wbufA, wbufB=wbufB, neg128f=neg128f,
                 cc_in=cc_in, cc_out=cc_out, cc_in16=cc_in16, cc_out16=cc_out16,
                 dbg=dbg, dbg_d=dbg_d, y=y_out)

        G["cbank"] = cbank
        G["consts_d"] = consts_d
        emit_setup(c, G, svec_d, cmask_d, rmask_d, rmask1m_d, ebias_d, eye_d)
        emit_l0_attn(c, ar, G)
        emit_l0_ffn(c, ar, G)
        emit_l1_attn(c, ar, G)
        emit_l1_ffn(c, ar, G)

        if dbg:
            @blk.gpsimd
            def _(e):
                for s in ("sq", "v", "es", "ev", "s"):
                    c.wait(e, s)
                for nm in dbg:
                    src = {"left": "left_sp", "u": "u_sp", "h": "h_sp", "P": "P_sp",
                           "x1": "x1_sp", "s0": "s0_sp"}.get(nm)
                    if src:
                        c.tag(e.dma_start(out=dbg_d[nm].rearrange("d t -> d t"),
                                          in_=sp[src].ap()), "gq")

        @blk.gpsimd
        def _(e):
            for sname in ("sq", "v", "es", "ev", "s", "gq", "wa", "wb", "xq",
                          "da", "db", "ya", "yb", "t", "p", "cc"):
                c.wait(e, sname)
    return nc


def emit_setup(c, G, svec_d, cmask_d, rmask_d, rmask1m_d, ebias_d, eye_d):
    blk, sems = c.blk, c.sems
    svec, dvec, tiny = G["svec"], G["dvec"], G["tiny"]

    @blk.sync
    def _(e):
        c.tag(e.dma_start(out=G["svec"][:], in_=svec_d), "sq")
        c.tag(e.dma_start(out=G["cmask"][:], in_=cmask_d), "sq")
        c.tag(e.dma_start(out=G["rmask"][:], in_=rmask_d), "sq")
        c.tag(e.dma_start(out=G["rmask1m"][:], in_=rmask1m_d), "sq")
        c.tag(e.dma_start(out=G["ebias"][:], in_=ebias_d), "sq")
        c.tag(e.dma_start(out=G["eye"][:], in_=eye_d), "sq")
        c.tag(e.dma_start(out=G["cbank"][:], in_=G["consts_d"]), "sq")

    @blk.vector
    def _(e):
        c.wait(e, "sq")
        e.memset(G["ones_bf"][:], 1.0)
        e.memset(G["onesr_f"][:], 1.0)
        e.memset(G["onesr_bf"][:], 1.0)
        e.memset(G["neg128r"][:], -128.0)
        e.memset(G["neg128f"][:], -128.0)
        e.memset(G["zrow"][:], 0.0)
        a = svec[:, SV["a_param"]:SV["a_param"] + 16]
        c.tag(e.tensor_scalar(tiny[:, 0:16], a, -1.0, None, ALU.mult), "v")

    @blk.scalar
    def _(e):
        c.wait(e, "v")
        c.tag(e.activation(tiny[:, 16:32], tiny[:, 0:16], AF.Exp), "s")

    @blk.vector
    def _(e):
        c.wait(e, "s")
        c.tag(e.tensor_copy(tiny[:, 32:48], tiny[:, 16:32]), "v")

    @blk.scalar
    def _(e):
        c.wait(e, "v")
        c.tag(e.activation(tiny[:, 48:64], tiny[:, 32:48], AF.Ln, bias=1.0), "s")

    @blk.vector
    def _(e):
        c.wait(e, "s")
        a = svec[:, SV["a_param"]:SV["a_param"] + 16]
        c.tag(e.tensor_tensor(tiny[:, 64:80], a, tiny[:, 48:64], ALU.add), "v")

    @blk.scalar
    def _(e):
        c.wait(e, "v")
        c.tag(e.mul(dvec[:, 0:16], tiny[:, 64:80], -8.0), "s")

    @blk.vector
    def _(e):
        c.wait(e, "s")
        sc = float(1.0 / np.sqrt(2048.0))
        e.tensor_scalar(dvec[:, 16:32], svec[:, SV["bqa"]:SV["bqa"] + 16], sc, None, ALU.mult)
        c.tag(e.tensor_scalar(dvec[:, 32:48], svec[:, SV["bka"]:SV["bka"] + 16], sc, None, ALU.mult), "v")



def emit_l0_attn(c, ar, G):
    nc, blk, sems = c.nc, c.blk, c.sems
    sp, wd, svec, dvec, ps = G["sp"], G["wd"], G["svec"], G["dvec"], G["ps"]

    def svc(name, i):
        return svec[:, SV[name] + i:SV[name] + i + 1]

    xn = ar.alloc("xn", [128, NKD, 3 + T], BF16)
    x_t = ar.alloc("x_t", [128, NKD, TW], BF16)
    sqb = ar.alloc("sqb", [128, TW], BF16)

    for it in range(5):
        ncols = 3 if it == 0 else TW
        off = 0 if it == 0 else 3 + (it - 1) * TW
        xsrc = (G["xTh"].rearrange("(k p) t -> p k t", p=128) if it == 0 else
                G["xT"].rearrange("(k p) t -> p k t", p=128)[:, :, (it - 1) * TW:it * TW])

        @blk.gpsimd
        def _(e, xsrc=xsrc, ncols=ncols):
            c.wait(e, "v")
            c.tag(e.dma_start(out=x_t[:, :, 0:ncols], in_=xsrc), "gq")

        v0, t0, s0 = c.n["v"], c.n["t"], c.n["s"]

        @blk.vector
        def _(e, ncols=ncols, v0=v0, t0=t0):
            c.wait(e, "gq")
            for k in range(NKD):
                e.wait_ge(sems["t"], t0 + k)
                c.tag(e.tensor_tensor(sqb[:, 0:ncols], x_t[:, k, 0:ncols],
                                      x_t[:, k, 0:ncols], ALU.mult), "v")

        @blk.tensor
        def _(e, ncols=ncols, v0=v0, s0=s0):
            c.use_bank(e, 6, "s", s0 + 2)
            for k in range(NKD):
                e.wait_ge(sems["v"], v0 + k + 1)
                c.tag(e.matmul(ps[6][0:1, 0:ncols], G["ones_bf"][:], sqb[:, 0:ncols],
                               start=(k == 0), stop=(k == NKD - 1)), "t")

        @blk.scalar
        def _(e, ncols=ncols):
            c.wait(e, "t")
            c.tag(e.activation(G["srow"][0:1, 0:ncols], ps[6][0:1, 0:ncols], AF.Ln,
                               bias=1e-8, scale=1.0 / 2048.0), "s")

        @blk.vector
        def _(e, ncols=ncols):
            c.wait(e, "s")
            c.tag(e.tensor_copy(G["srow"][0:1, 1536:1536 + ncols], G["srow"][0:1, 0:ncols]), "v")

        @blk.scalar
        def _(e, ncols=ncols):
            c.wait(e, "v")
            c.tag(e.activation(G["srowb"][0:1, 0:ncols], G["srow"][0:1, 1536:1536 + ncols],
                               AF.Exp, scale=-0.5), "s")

        @blk.tensor
        def _(e, ncols=ncols):
            c.wait(e, "s")
            c.use_bank(e, 5, "v", c.n["v"] + NKD)
            c.tag(e.matmul(ps[5][:, 0:ncols], G["onesr_bf"][:],
                           G["srowb"][0:1, 0:ncols], start=True, stop=True), "t")

        @blk.vector
        def _(e, ncols=ncols, off=off):
            c.wait(e, "t")
            for k in range(NKD):
                c.tag(e.scalar_tensor_tensor(xn[:, k, off:off + ncols], x_t[:, k, 0:ncols],
                                             svc("norm1_w", k), ps[5][:, 0:ncols],
                                             ALU.mult, ALU.mult), "v")

    if "xn" in G["dbg"]:
        @blk.gpsimd
        def _(e):
            c.wait(e, "v")
            c.tag(e.dma_start(out=G["dbg_d"]["xn"].rearrange("(k p) t -> p k t", p=128),
                              in_=xn[:, :, 3:3 + T]), "gq")

    lrow = ar.alloc("lrow", [128, 2, T], BF16)
    st_base = c.n["sq"]

    def evac_left(e, m, g, pst):
        if m >= 2:
            e.wait_ge(sems["sq"], st_base + 16 * m)
        return e.activation(lrow[:, m % 2, g * TW:(g + 1) * TW], pst[:, 0:TW],
                            AF.Gelu, bias=svc("left_b", m))

    t0, e0, esem = mmloop(c, G, wd["w_left"], NKD, NKD,
                          lambda k, g: xn[:, k, 3 + g * TW:3 + (g + 1) * TW],
                          [TW] * 4, "s", evac_left, ready=[("v", c.n["v"])])

    @blk.sync
    def _(e):
        for m in range(NKD):
            e.wait_ge(sems["es"], e0 + (m + 1) * 4)
            c.tag(e.dma_start(out=sp["left_sp"].ap()[m * 128:(m + 1) * 128, :],
                              in_=lrow[:, m % 2, :]), "sq")

    ci_row = ar.alloc("ci_row", [128, 3 + T], BF16)
    urow = ar.alloc("urow", [128, 2, T], BF16)
    ust_base = c.n["sq"]
    vref = {}

    def rhs_right(k, g):
        if g == 0:
            return xn[:, k, 0:3]
        return xn[:, k, 3 + (g - 1) * TW:3 + g * TW]

    def evac_right(e, m, g, pst):
        if g == 0:
            if m >= 2:
                e.wait_ge(sems["sq"], ust_base + 16 * m)
            return e.tensor_scalar(ci_row[:, 0:3], pst[:, 0:3], svc("right_b", m),
                                   G["cmask"][:, 0:1], ALU.add, ALU.mult)
        li = e.tensor_scalar(ci_row[:, 3 + (g - 1) * TW:3 + g * TW], pst[:, 0:TW],
                             svc("right_b", m), None, ALU.add)
        if g == 4:
            e.tensor_scalar(urow[:, m % 2, :], ci_row[:, 0:T], svc("cw0", m),
                            svc("conv_b", m), ALU.mult, ALU.add)
            for kk in range(1, 4):
                li = e.scalar_tensor_tensor(urow[:, m % 2, :], ci_row[:, kk:kk + T],
                                            svc(f"cw{kk}", m), urow[:, m % 2, :],
                                            ALU.mult, ALU.add)
        return li

    t0r, e0r, _ = mmloop(c, G, wd["w_right"], NKD, NKD, rhs_right,
                         [3, TW, TW, TW, TW], "v", evac_right, ready=[("v", c.n["v"])])

    @blk.sync
    def _(e):
        for m in range(NKD):
            e.wait_ge(sems["ev"], e0r + (m + 1) * 5)
            c.tag(e.dma_start(out=sp["u_sp"].ap()[m * 128:(m + 1) * 128, :],
                              in_=urow[:, m % 2, :]), "sq")

    ar.freeb("xn", "x_t", "sqb", "lrow", "ci_row", "urow")

    emit_scan(c, ar, G)

    emit_gated_x1(c, ar, G)

    emit_l0_mlp(c, ar, G)


def emit_scan(c, ar, G):
    nc, blk, sems = c.nc, c.blk, c.sems
    sp, wd, svec, dvec, ps = G["sp"], G["wd"], G["svec"], G["dvec"], G["ps"]

    u = ar.alloc("u", [128, NKD, T], BF16)
    r_row = ar.alloc("r_row", [128, 2, T], F32)
    i_row = ar.alloc("i_row", [128, 2, T], F32)
    a_row = ar.alloc("a_row", [128, T], F32)
    w_row = ar.alloc("w_row", [128, T], F32)
    b_row = ar.alloc("b_row", [128, T], F32)
    h_row = ar.alloc("h_row", [128, T], F32)
    hP_bf = ar.alloc("hP_bf", [128, 2, T], BF16)
    agg = ar.alloc("agg", [128, 96], F32)
    ag_sb = ar.alloc("ag_sb", [128, 8, 32], F32)

    @blk.sync
    def _(e):
        c.wait(e, "sq")
        c.tag(e.dma_start(out=u[:], in_=sp["u_sp"].ap().rearrange("(k p) t -> p k t", p=128)), "sq")
    uld = c.n["sq"]

    t0, q0 = c.n["t"], c.n["gq"]
    e0 = c.n["es"]
    wrt_ga = wd["w_ga"].rearrange("(k p) m -> p k m", p=128)
    wrt_gx = wd["w_gx"].rearrange("(k p) m -> p k m", p=128)
    MW = 2
    NS = NKD // MW
    wv = [G["wbufA"].ap()[:, 0:32 * MW * 128].rearrange("p (k m) -> p k m", k=32),
          G["wbufB"].ap()[:, 0:32 * MW * 128].rearrange("p (k m) -> p k m", k=32)]

    wsem = ["wa", "wb"]
    wtarget = []

    @blk.gpsimd
    def _(e):
        for ms in range(NS):
            if ms >= 2:
                e.wait_ge(sems["t"], t0 + (ms - 1) * MW * 8)
            elif t0 > 0:
                e.wait_ge(sems["t"], t0)
            sm = wsem[ms % 2]
            e.dma_start(out=wv[ms % 2][:, 0:16, :],
                        in_=wrt_ga[:, :, ms * MW * 128:(ms + 1) * MW * 128]).then_inc(sems[sm], 16)
            e.dma_start(out=wv[ms % 2][:, 16:32, :],
                        in_=wrt_gx[:, :, ms * MW * 128:(ms + 1) * MW * 128]).then_inc(sems[sm], 16)
            c.n[sm] += 32
            wtarget.append((sm, c.n[sm]))

    @blk.tensor
    def _(e):
        e.wait_ge(sems["sq"], uld)
        idx = 0
        for ms in range(NS):
            e.wait_ge(sems[wtarget[ms][0]], wtarget[ms][1])
            for mc in range(MW):
                for br in range(2):
                    for g in range(4):
                        b = idx % 6
                        c.use_bank(e, b, "es", e0 + idx + 1)
                        for k in range(NKD):
                            mi = e.matmul(ps[b][:, 0:TW],
                                          wv[ms % 2][:, br * 16 + k, mc * 128:(mc + 1) * 128],
                                          u[:, k, g * TW:(g + 1) * TW],
                                          start=(k == 0), stop=(k == NKD - 1))
                        mi.then_inc(sems["t"], 1)
                        idx += 1
    c.n["t"] = t0 + NKD * 8

    vc0 = c.n["v"]
    st0 = c.n["sq"]
    for m in range(NKD):
        @blk.scalar
        def _(e, m=m):
            if m >= 2:
                e.wait_ge(sems["v"], vc0 + (m - 1) * 6)
            for br in range(2):
                dst = (r_row if br == 0 else i_row)
                bias = (svec[:, SV["ga_b"] + m:SV["ga_b"] + m + 1] if br == 0
                        else svec[:, SV["gx_b"] + m:SV["gx_b"] + m + 1])
                for g in range(4):
                    idx = m * 8 + br * 4 + g
                    b = idx % 6
                    e.wait_ge(sems["t"], t0 + idx + 1)
                    e.activation(dst[:, m % 2, g * TW:(g + 1) * TW], ps[b][:, 0:TW],
                                 AF.Sigmoid, bias=bias).then_inc(sems["es"], 1)
            c.n["es"] += 8

        @blk.vector
        def _(e, m=m):
            e.wait_ge(sems["es"], e0 + m * 8 + 8)
            if m >= 1 and "gates" in G["dbg"]:
                e.wait_ge(sems["sq"], st0 + 64 * m)
            c.tag(e.tensor_scalar(a_row[:], r_row[:, m % 2, :], dvec[:, m:m + 1], 0.0,
                                  ALU.mult, ALU.add, accum_out=agg[:, m:m + 1]), "v")

        @blk.scalar
        def _(e, m=m):
            c.wait(e, "v")
            c.tag(e.activation(w_row[:], a_row[:], AF.Exp, scale=2.0), "s")
            c.tag(e.activation(a_row[:], a_row[:], AF.Exp), "s")

        @blk.vector
        def _(e, m=m):
            c.wait(e, "s")
            c.tag(e.tensor_scalar(w_row[:], w_row[:], -1.0, 1.0, ALU.mult, ALU.add), "v")

        @blk.scalar
        def _(e, m=m):
            c.wait(e, "v")
            e.activation(w_row[:], w_row[:], AF.Ln)
            c.tag(e.activation(w_row[:], w_row[:], AF.Exp, scale=0.5), "s")

        @blk.vector
        def _(e, m=m):
            c.wait(e, "s")
            e.tensor_tensor(b_row[:], i_row[:, m % 2, :], u[:, m, :], ALU.mult)
            c.tag(e.tensor_tensor(b_row[:], b_row[:], w_row[:], ALU.mult), "v")
            if m >= 1:
                nst = 64 if "gates" in G["dbg"] else 32
                e.wait_ge(sems["sq"], st0 + nst * m)
            c.tag(e.tensor_tensor_scan(hP_bf[:, 0, :], a_row[:], G["zrow"][:],
                                       1.0, ALU.mult, ALU.add), "v")
            c.tag(e.tensor_tensor_scan(h_row[:], a_row[:], b_row[:],
                                       0.0, ALU.mult, ALU.add), "v")
            e.tensor_copy(hP_bf[:, 1, :], h_row[:])
            c.tag(e.tensor_copy(agg[:, 16 + m:17 + m], h_row[:, T - 1:T]), "v")

        @blk.sync
        def _(e, m=m):
            c.wait(e, "v")
            c.tag(e.dma_start(out=sp["P_sp"].ap()[m * 128:(m + 1) * 128, :],
                              in_=hP_bf[:, 0, :]), "sq")
            c.tag(e.dma_start(out=sp["h_sp"].ap()[m * 128:(m + 1) * 128, :],
                              in_=hP_bf[:, 1, :]), "sq")
            if "gates" in G["dbg"]:
                c.tag(e.dma_start(out=sp["a_sp"].ap()[m * 128:(m + 1) * 128, :],
                                  in_=a_row[:]), "sq")
                c.tag(e.dma_start(out=sp["b_sp"].ap()[m * 128:(m + 1) * 128, :],
                                  in_=b_row[:]), "sq")

        if m == 0 and "gates" in G["dbg"]:
            @blk.gpsimd
            def _(e):
                c.wait(e, "v"); c.wait(e, "s")
                dd = G["dbg_d"]["gates"]
                e.dma_start(out=dd[:, 0:T], in_=r_row[:, 0, :]).then_inc(sems["gq"], 16)
                e.dma_start(out=dd[:, T:2 * T], in_=i_row[:, 0, :]).then_inc(sems["gq"], 16)
                e.dma_start(out=dd[:, 2 * T:3 * T], in_=a_row[:]).then_inc(sems["gq"], 16)
                e.dma_start(out=dd[:, 3 * T:4 * T], in_=b_row[:]).then_inc(sems["gq"], 16)
                c.n["gq"] += 64

    @blk.scalar
    def _(e):
        c.wait(e, "v")
        c.tag(e.activation(agg[:, 32:48], agg[:, 0:16], AF.Exp), "s")

    @blk.vector
    def _(e):
        c.wait(e, "s")
        c.tag(e.tensor_copy(agg[:, 0:16], agg[:, 32:48]), "v")

    emit_allgather(c, G, agg[:, 0:32], 32, ag_sb)

    @blk.vector
    def _(e):
        c.wait(e, "gq")
        c.tag(e.memset(agg[:, 48:64], 0.0), "v")
    for r in range(8):
        @blk.vector
        def _(e, r=r):
            if r > 0:
                c.wait(e, "p")
            c.tag(e.tensor_scalar(agg[:, 0:16], ag_sb[:, r, 0:16], G["rmask"][:, r:r + 1],
                                  G["rmask1m"][:, r:r + 1], ALU.mult, ALU.add), "v")

        @blk.gpsimd
        def _(e, r=r):
            c.wait(e, "v")
            e.tensor_scalar(agg[:, 32:48], ag_sb[:, r, 16:32], G["rmask"][:, r:r + 1],
                            None, ALU.mult)
            c.tag(e.tensor_tensor(agg[:, 64:80], agg[:, 48:64], agg[:, 0:16], ALU.mult), "p")

        @blk.vector
        def _(e, r=r):
            c.wait(e, "p")
            c.tag(e.tensor_tensor(agg[:, 48:64], agg[:, 64:80], agg[:, 32:48], ALU.add), "v")

    ar.freeb("u", "r_row", "i_row", "a_row", "w_row", "b_row", "h_row", "hP_bf", "ag_sb")
    G["agg"] = agg


def emit_gated_x1(c, ar, G):
    nc, blk, sems = c.nc, c.blk, c.sems
    sp, wd, svec, ps = G["sp"], G["wd"], G["svec"], G["ps"]
    agg = G["agg"]

    gated = ar.alloc("gated", [128, NKD, T], BF16)
    left_t = ar.alloc("left_t", [128, NKD, TW], BF16)
    h_t = ar.alloc("h_t", [128, NKD, TW], BF16)
    P_t = ar.alloc("P_t", [128, NKD, TW], BF16)
    t2 = ar.alloc("t2", [128, TW], F32)

    ld0 = c.n["sq"]
    v0 = c.n["v"]

    @blk.sync
    def _(e):
        for nt in range(NT):
            if nt == 0:
                c.wait(e, "v")
                c.wait(e, "p")
            else:
                e.wait_ge(sems["v"], v0 + nt * 16)
            src = slice(nt * TW, (nt + 1) * TW)
            e.dma_start(out=left_t[:], in_=sp["left_sp"].ap().rearrange(
                "(k p) t -> p k t", p=128)[:, :, src]).then_inc(sems["sq"], 16)
            e.dma_start(out=h_t[:], in_=sp["h_sp"].ap().rearrange(
                "(k p) t -> p k t", p=128)[:, :, src]).then_inc(sems["sq"], 16)
            e.dma_start(out=P_t[:], in_=sp["P_sp"].ap().rearrange(
                "(k p) t -> p k t", p=128)[:, :, src]).then_inc(sems["sq"], 16)
    c.n["sq"] = ld0 + 48 * NT

    @blk.vector
    def _(e):
        for nt in range(NT):
            e.wait_ge(sems["sq"], ld0 + 48 * (nt + 1))
            for k in range(NKD):
                e.scalar_tensor_tensor(t2[:], P_t[:, k, :], agg[:, 48 + k:49 + k],
                                       h_t[:, k, :], ALU.mult, ALU.add)
                c.tag(e.tensor_tensor(gated[:, k, nt * TW:(nt + 1) * TW],
                                      left_t[:, k, :], t2[:], ALU.mult), "v")
    ar.freeb("left_t", "h_t", "P_t", "t2")

    if "gated" in G["dbg"]:
        @blk.gpsimd
        def _(e):
            c.wait(e, "v")
            c.tag(e.dma_start(out=G["dbg_d"]["gated"].rearrange("(k p) t -> p k t", p=128),
                              in_=gated[:]), "gq")

    xres = ar.alloc("xres", [128, NKD, T], BF16)
    x1row = ar.alloc("x1row", [128, 2, T], BF16)

    @blk.gpsimd
    def _(e):
        c.wait(e, "v")
        c.tag(e.dma_start(out=xres[:], in_=G["xT"].rearrange("(k p) t -> p k t", p=128)), "xq")
    xld = c.n["xq"]
    st0 = c.n["sq"]

    def evac_x1(e, m, g, pst):
        if m >= 2 and g == 0:
            e.wait_ge(sems["sq"], st0 + 16 * m)
        if m == 0 and g == 0:
            e.wait_ge(sems["xq"], xld)
        return e.scalar_tensor_tensor(x1row[:, m % 2, g * TW:(g + 1) * TW], pst[:, 0:TW],
                                      svec[:, SV["out_b"] + m:SV["out_b"] + m + 1],
                                      xres[:, m, g * TW:(g + 1) * TW], ALU.add, ALU.add)

    t0, e0, _ = mmloop(c, G, wd["w_out"], NKD, NKD,
                       lambda k, g: gated[:, k, g * TW:(g + 1) * TW],
                       [TW] * 4, "v", evac_x1, ready=[("v", c.n["v"])])

    @blk.sync
    def _(e):
        for m in range(NKD):
            e.wait_ge(sems["ev"], e0 + (m + 1) * 4)
            c.tag(e.dma_start(out=sp["x1_sp"].ap()[m * 128:(m + 1) * 128, :],
                              in_=x1row[:, m % 2, :]), "sq")

    ar.freeb("gated", "x1row", "xres")


def emit_l0_mlp(c, ar, G):
    nc, blk, sems = c.nc, c.blk, c.sems
    sp, wd, svec, ps = G["sp"], G["wd"], G["svec"], G["ps"]
    NF = 48

    x1_t = ar.alloc("x1_t", [128, NKD, TW], BF16)
    x1n_t = ar.alloc("x1n_t", [128, NKD, TW], BF16)
    mh = ar.alloc("mh", [128, NF, TW], BF16)
    m1b = ar.alloc("m1b", [128, TW], BF16)
    m2b = ar.alloc("m2b", [128, TW], BF16)
    zw = ar.alloc("zw", [128, NKD, TW], BF16)
    sqb2 = ar.alloc("sqb2", [128, TW], BF16)
    s0row = ar.alloc("s0row", [128, NKD, TW], BF16)
    xres_t = ar.alloc("xres_t", [128, NKD, TW], BF16)

    wrt1 = wd["w_up1"].rearrange("(k p) m -> p k m", p=128)
    wrt2 = wd["w_up2"].rearrange("(k p) m -> p k m", p=128)
    wrtd = wd["w_down"].rearrange("(k p) m -> p k m", p=128)
    MWU = 2
    wvU = [G["wbufA"].ap()[:, 0:32 * MWU * 128].rearrange("p (k m) -> p k m", k=32),
           G["wbufB"].ap()[:, 0:32 * MWU * 128].rearrange("p (k m) -> p k m", k=32)]
    wvD = [G["wbufA"].ap()[:, 0:48 * 128].rearrange("p (k m) -> p k m", k=48),
           G["wbufB"].ap()[:, 0:48 * 128].rearrange("p (k m) -> p k m", k=48)]


    for st in range(NT):
        ld0 = c.n["sq"]

        @blk.sync
        def _(e, st=st):
            c.wait(e, "v")
            c.wait(e, "ev")
            c.tag(e.dma_start(out=x1_t[:], in_=sp["x1_sp"].ap().rearrange(
                "(k p) t -> p k t", p=128)[:, :, st * TW:(st + 1) * TW]), "sq")

        @blk.gpsimd
        def _(e, st=st):
            c.wait(e, "ev")
            c.tag(e.dma_start(out=xres_t[:], in_=G["xT"].rearrange(
                "(k p) t -> p k t", p=128)[:, :, st * TW:(st + 1) * TW]), "xq")
        xres_ld = c.n["xq"]

        @blk.vector
        def _(e):
            c.wait(e, "sq")
            c.tag(e.engine_nop(), "v")

        def out_rms(e, k, A, Bn):
            return e.scalar_tensor_tensor(x1n_t[:, k, :], x1_t[:, k, :],
                                          svec[:, SV["norm2_w"] + k:SV["norm2_w"] + k + 1],
                                          A, ALU.mult, ALU.mult)

        emit_stats_ln(c, G, lambda k: x1_t[:, k, :], TW, out_rms, kind="rms", sq_buf=sqb2)

        t0, q0, e0s, e0v = c.n["t"], c.n["gq"], c.n["es"], c.n["ev"]
        MW = MWU
        NSf = NF // MW

        wsem = ["wa", "wb"]
        wtargetU = []

        @blk.gpsimd
        def _(e):
            for ms in range(NSf):
                if ms >= 2:
                    e.wait_ge(sems["t"], t0 + (ms - 1) * MW * 2)
                elif t0 > 0:
                    e.wait_ge(sems["t"], t0)
                sm = wsem[ms % 2]
                e.dma_start(out=wvU[ms % 2][:, 0:16, :],
                            in_=wrt1[:, :, ms * MW * 128:(ms + 1) * MW * 128]).then_inc(sems[sm], 16)
                e.dma_start(out=wvU[ms % 2][:, 16:32, :],
                            in_=wrt2[:, :, ms * MW * 128:(ms + 1) * MW * 128]).then_inc(sems[sm], 16)
                c.n[sm] += 32
                wtargetU.append((sm, c.n[sm]))

        @blk.tensor
        def _(e):
            e.wait_ge(sems["v"], c.n["v"])
            idx = 0
            for ms in range(NSf):
                e.wait_ge(sems[wtargetU[ms][0]], wtargetU[ms][1])
                for mc in range(MW):
                    f = ms * MW + mc
                    for br in range(2):
                        b = idx % 6
                        c.use_bank(e, b, "es", e0s + idx + 1)
                        for k in range(NKD):
                            mi = e.matmul(ps[b][:, 0:TW],
                                          wvU[ms % 2][:, br * 16 + k, mc * 128:(mc + 1) * 128],
                                          x1n_t[:, k, :], start=(k == 0), stop=(k == NKD - 1))
                        mi.then_inc(sems["t"], 1)
                        idx += 1
        c.n["t"] = t0 + NF * 2

        @blk.scalar
        def _(e):
            idx = 0
            for f in range(NF):
                for br in range(2):
                    b = idx % 6
                    e.wait_ge(sems["t"], t0 + idx + 1)
                    if f >= 1 and br == 0:
                        e.wait_ge(sems["ev"], e0v + f)
                    if br == 0:
                        e.activation(m1b[:], ps[b][:, 0:TW], AF.Gelu,
                                     bias=svec[:, SV["up1_b"] + f:SV["up1_b"] + f + 1]
                                     ).then_inc(sems["es"], 1)
                    else:
                        e.activation(m2b[:], ps[b][:, 0:TW], AF.Identity,
                                     bias=svec[:, SV["up2_b"] + f:SV["up2_b"] + f + 1]
                                     ).then_inc(sems["es"], 1)
                    idx += 1
        c.n["es"] = e0s + NF * 2

        @blk.vector
        def _(e):
            for f in range(NF):
                e.wait_ge(sems["es"], e0s + f * 2 + 2)
                e.tensor_tensor(mh[:, f, :], m1b[:], m2b[:], ALU.mult).then_inc(sems["ev"], 1)
        c.n["ev"] = e0v + NF

        t0d, q0d, e0d = c.n["t"], c.n["gq"], c.n["ev"]
        mhdone = c.n["ev"]

        wtargetD = []

        @blk.gpsimd
        def _(e):
            for ms in range(NKD):
                if ms >= 2:
                    e.wait_ge(sems["t"], t0d + (ms - 1))
                elif t0d > 0:
                    e.wait_ge(sems["t"], t0d)
                sm = wsem[ms % 2]
                e.dma_start(out=wvD[ms % 2],
                            in_=wrtd[:, :, ms * 128:(ms + 1) * 128]).then_inc(sems[sm], 16)
                c.n[sm] += 16
                wtargetD.append((sm, c.n[sm]))

        @blk.tensor
        def _(e):
            e.wait_ge(sems["ev"], mhdone)
            idx = 0
            for m in range(NKD):
                e.wait_ge(sems[wtargetD[m][0]], wtargetD[m][1])
                b = idx % 6
                c.use_bank(e, b, "ev", e0d + idx + 1)
                for k in range(NF):
                    mi = e.matmul(ps[b][:, 0:TW],
                                  wvD[m % 2][:, k, :],
                                  mh[:, k, :], start=(k == 0), stop=(k == NF - 1))
                mi.then_inc(sems["t"], 1)
                idx += 1
        c.n["t"] = t0d + NKD

        @blk.vector
        def _(e, st=st, xres_ld=xres_ld):
            e.wait_ge(sems["xq"], xres_ld)
            for m in range(NKD):
                b = m % 6
                e.wait_ge(sems["t"], t0d + m + 1)
                e.scalar_tensor_tensor(zw[:, m, :], ps[b][:, 0:TW],
                                       svec[:, SV["down_b"] + m:SV["down_b"] + m + 1],
                                       x1_t[:, m, :], ALU.add, ALU.add)
                c.tag(e.tensor_tensor(zw[:, m, :], zw[:, m, :],
                                      xres_t[:, m, :], ALU.add),
                      "ev")
        c.n["ev"] = e0d + NKD

        def out_ln(e, k, A, Bn):
            e.tensor_tensor(s0row[:, k, :], zw[:, k, :], A, ALU.mult)
            return e.tensor_tensor(s0row[:, k, :], s0row[:, k, :], Bn, ALU.subtract)

        emit_stats_ln(c, G, lambda k: zw[:, k, :], TW, out_ln, kind="ln", sq_buf=sqb2)

        @blk.sync
        def _(e, st=st):
            c.wait(e, "v")
            c.tag(e.dma_start(out=sp["s0_sp"].ap().rearrange(
                "(k p) t -> p k t", p=128)[:, :, st * TW:(st + 1) * TW],
                in_=s0row[:]), "sq")

    ar.freeb("x1_t", "x1n_t", "mh", "m1b", "m2b", "zw", "sqb2", "s0row", "xres_t", "agg")





def emit_l0_ffn(c, ar, G):
    nc, blk, sems = c.nc, c.blk, c.sems
    sp, wd, svec, ps, psb = G["sp"], G["wd"], G["svec"], G["ps"], G["psb"]
    NH = 11
    NTC = T // 128

    s0r = ar.alloc("s0r", [128, NKD, T], BF16)
    he = ar.alloc("he", [128, NH, T], BF16)
    pbc = ar.alloc("pbc", [128, T], F32)
    prow = ar.alloc("prow", [1, T], BF16)
    gwt = ar.alloc("gwt", [128, NKD * 8], BF16)
    sco = ar.alloc("sco", [128, NTC, 8], F32)
    sbx = ar.alloc("sbx", [128, NTC, 8], F32)
    mk1 = ar.alloc("mk1", [128, NTC, 8], F32)
    sb2 = ar.alloc("sb2", [128, NTC, 8], F32)
    mk2 = ar.alloc("mk2", [128, NTC, 8], F32)
    ptm = ar.alloc("ptm", [128, NTC, 8], BF16)
    mxc = ar.alloc("mxc", [128, 2 * NTC], F32)
    m1b = ar.alloc("fm1", [128, 2, TW], F32)
    m1c = ar.alloc("fm1c", [128, 2, TW], F32)
    m2b = ar.alloc("fm2", [128, 2, TW], F32)
    zb = ar.alloc("fzb", [128, 4, TW], F32)

    gv = gwt.ap().rearrange("p (k e) -> p k e", k=NKD)

    @blk.sync
    def _(e):
        c.wait(e, "sq")
        c.tag(e.dma_start(out=s0r[:], in_=sp["s0_sp"].ap().rearrange("(k p) t -> p k t", p=128)), "sq")
    s0ld = c.n["sq"]

    @blk.gpsimd
    def _(e):
        c.tag(e.dma_start(out=gv, in_=wd["w_gate0"].rearrange("(k p) e -> p k e", p=128)), "xq")
    gwld = c.n["xq"]

    @blk.tensor
    def _(e):
        e.wait_ge(sems["sq"], s0ld)
        e.wait_ge(sems["xq"], gwld)
        for tc in range(NTC):
            b = tc % 6
            c.wait_bank(e, b)
            for k in range(NKD):
                mi = e.matmul(ps[b][:, 0:8], s0r[:, k, tc * 128:(tc + 1) * 128],
                              gv[:, k, :], start=(k == 0), stop=(k == NKD - 1))
            c.tag(mi, "t")
    tsc = c.n["t"] - NTC

    @blk.scalar
    def _(e):
        for tc in range(NTC):
            b = tc % 6
            e.wait_ge(sems["t"], tsc + tc + 1)
            c.tag(e.activation(sco[:, tc, :], ps[b][0:128, 0:8], AF.Sigmoid), "es")
    for tc in range(NTC):
        c.free_bank(tc % 6, "es")

    @blk.vector
    def _(e):
        c.wait(e, "es")

        def flush():
            e.tensor_copy(m1b[:, 0, :], G["zrow"][:, 0:TW])

        for tc in range(NTC):
            e.tensor_tensor(sbx[:, tc, :], sco[:, tc, :], G["ebias"][:], ALU.add)
        flush()
        for tc in range(NTC):
            e.tensor_reduce(mxc[:, tc:tc + 1], sbx[:, tc, :], AX.X, ALU.max)
        flush()
        for tc in range(NTC):
            e.tensor_scalar(mk1[:, tc, :], sbx[:, tc, :], mxc[:, tc:tc + 1], None, ALU.is_ge)
        flush()
        for tc in range(NTC):
            e.scalar_tensor_tensor(sb2[:, tc, :], mk1[:, tc, :], -1e9,
                                   sbx[:, tc, :], ALU.mult, ALU.add)
        flush()
        for tc in range(NTC):
            e.tensor_reduce(mxc[:, NTC + tc:NTC + tc + 1], sb2[:, tc, :], AX.X, ALU.max)
        flush()
        for tc in range(NTC):
            e.tensor_scalar(mk2[:, tc, :], sb2[:, tc, :], mxc[:, NTC + tc:NTC + tc + 1],
                            None, ALU.is_ge)
        flush()
        for tc in range(NTC):
            e.tensor_tensor(mk1[:, tc, :], mk1[:, tc, :], mk2[:, tc, :], ALU.add)
        flush()
        for tc in range(NTC):
            e.tensor_tensor(ptm[:, tc, :], mk1[:, tc, :], sco[:, tc, :], ALU.mult)
        flush()
        c.tag(e.engine_nop(), "p")

    zr_base = [c.n["gq"]]
    dpar = ["da", "db"]
    for exi, ex in enumerate(F_EXPERTS):
        shared = (ex == 8)
        if shared:
            w1d = wd["w_sw1"]; w3d = wd["w_sw3"]; w2d = wd["w_sw2"]
        else:
            w1d = wd["w_e1"][ex]; w3d = wd["w_e3"][ex]; w2d = wd["w_e2"][ex]
        w1r = w1d.rearrange("(k p) m -> p k m", p=128)
        w3r = w3d.rearrange("(k p) m -> p k m", p=128)
        w2r = w2d.rearrange("(k p) m -> p k m", p=128)

        if not shared:
            tp0 = c.n["t"]
            vp0 = c.n["v"]

            NTC = T // 128
            NHH = (NTC + 7) // 8

            @blk.tensor
            def _(e, ex=ex, vp0=vp0, NTC=NTC, NHH=NHH):
                c.wait(e, "p")
                c.wait_bank(e, 7)
                for hh in range(NHH):
                    if hh >= 1:
                        e.wait_ge(sems["v"], vp0 + hh)
                    for tc in range(min(8, NTC - hh * 8)):
                        tci = hh * 8 + tc
                        mi = e.transpose(psb[0:1, tc * 128:(tc + 1) * 128],
                                         ptm[:, tci, ex:ex + 1], G["eye"][:])
                    c.tag(mi, "t")

            @blk.vector
            def _(e, tp0=tp0, NTC=NTC, NHH=NHH):
                for hh in range(NHH):
                    w = 128 * min(8, NTC - hh * 8)
                    e.wait_ge(sems["t"], tp0 + hh + 1)
                    c.tag(e.tensor_copy(prow[0:1, hh * 1024:hh * 1024 + w],
                                        psb[0:1, 0:w]), "v")
            c.free_bank(7, "v")

            tb0 = c.n["t"]

            @blk.tensor
            def _(e):
                c.wait(e, "v")
                for g in range(NT):
                    b = g % 6
                    c.wait_bank(e, b)
                    c.tag(e.matmul(ps[b][:, 0:TW], G["onesr_bf"][:],
                                   prow[0:1, g * TW:(g + 1) * TW], start=True, stop=True), "t")

            @blk.vector
            def _(e, tb0=tb0):
                for g in range(NT):
                    b = g % 6
                    e.wait_ge(sems["t"], tb0 + g + 1)
                    c.tag(e.tensor_copy(pbc[:, g * TW:(g + 1) * TW], ps[b][:, 0:TW]), "v")
            for g in range(NT):
                c.free_bank(g % 6, "v")

        wtg = []
        t0e = c.n["t"]

        @blk.gpsimd
        def _(e, t0e=t0e, w1r=w1r, w3r=w3r):
            for f in range(NH):
                if f >= 2:
                    e.wait_ge(sems["t"], t0e + (f - 1) * 2 * NT)
                elif t0e > 0:
                    e.wait_ge(sems["t"], t0e)
                sm = ("wa", "wb")[f % 2]
                e.dma_start(out=G["wbufA" if f % 2 == 0 else "wbufB"].ap()[:, 0:NKD * 128].rearrange(
                    "p (k m) -> p k m", k=NKD),
                    in_=w1r[:, :, f * 128:(f + 1) * 128]).then_inc(sems[sm], 16)
                e.dma_start(out=G["wbufA" if f % 2 == 0 else "wbufB"].ap()[:, NKD * 128:2 * NKD * 128].rearrange(
                    "p (k m) -> p k m", k=NKD),
                    in_=w3r[:, :, f * 128:(f + 1) * 128]).then_inc(sems[sm], 16)
                c.n[sm] += 32
                wtg.append((sm, c.n[sm]))

        for f in range(NH):
            bsel = []

            @blk.tensor
            def _(e, f=f):
                e.wait_ge(sems[wtg[f][0]], wtg[f][1])
                if f == 0:
                    e.wait_ge(sems["sq"], s0ld)
                wv1 = G["wbufA" if f % 2 == 0 else "wbufB"].ap()[:, 0:NKD * 128].rearrange(
                    "p (k m) -> p k m", k=NKD)
                wv3 = G["wbufA" if f % 2 == 0 else "wbufB"].ap()[:, NKD * 128:2 * NKD * 128].rearrange(
                    "p (k m) -> p k m", k=NKD)
                for g in range(NT):
                    for br, wv_ in ((0, wv1), (1, wv3)):
                        b = (f * 8 + g * 2 + br) % 6
                        bsel.append(b)
                        c.wait_bank(e, b)
                        for k in range(NKD):
                            mi = e.matmul(ps[b][:, 0:TW], wv_[:, k, :],
                                          s0r[:, k, g * TW:(g + 1) * TW],
                                          start=(k == 0), stop=(k == NKD - 1))
                        c.tag(mi, "t")
            tgrp = c.n["t"] - 2 * NT

            for g in range(NT):
                b1 = bsel[g * 2]
                b3 = bsel[g * 2 + 1]
                par = (f * NT + g) % 2

                @blk.vector
                def _(e, f=f, g=g, b1=b1, b3=b3, par=par, tgrp=tgrp, shared=shared):
                    e.wait_ge(sems["t"], tgrp + g * 2 + 2)
                    c.wait(e, "es")
                    if not shared:
                        e.tensor_tensor(m1b[:, par, :], ps[b1][:, 0:TW],
                                        pbc[:, g * TW:(g + 1) * TW], ALU.mult)
                        c.tag(e.tensor_tensor(m2b[:, par, :], ps[b3][:, 0:TW],
                                              pbc[:, g * TW:(g + 1) * TW], ALU.mult), "ev")
                    else:
                        e.tensor_copy(m1b[:, par, :], ps[b1][:, 0:TW])
                        c.tag(e.tensor_copy(m2b[:, par, :], ps[b3][:, 0:TW]), "ev")
                c.free_bank(b1, "ev")
                c.free_bank(b3, "ev")

                @blk.scalar
                def _(e, par=par):
                    c.wait(e, "ev")
                    c.tag(e.activation(m1c[:, par, :], m1b[:, par, :], AF.Silu), "es")

                @blk.vector
                def _(e, f=f, g=g, par=par):
                    c.wait(e, "es")
                    c.tag(e.tensor_tensor(he[:, f, g * TW:(g + 1) * TW],
                                          m1c[:, par, :], m2b[:, par, :], ALU.mult), "ev")

        t0d = c.n["t"]
        hedone = c.n["ev"]
        wtd = []
        ev_of_group = {}
        pend = []

        EXI = [exi]

        def emit_stores(e, ex):
            while pend:
                m_, g_, par_ = pend.pop(0)
                e.wait_ge(sems["ev"], ev_of_group[(m_, g_)])
                if EXI[0] > 0 and m_ == 0 and g_ == 0:
                    c.wait(e, "da")
                    c.wait(e, "db")
                op = ALU.bypass if EXI[0] == 0 else ALU.add
                dst = sp["zr_sp"].ap()[m_ * 128:(m_ + 1) * 128, g_ * TW:(g_ + 1) * TW]
                c.tag(e.dma_start(out=dst, in_=zb[:, par_, :], accum_op=op), dpar[par_ % 2])

        for m in range(NKD):
            @blk.gpsimd
            def _(e, m=m, t0d=t0d, ex=ex, w2r=w2r):
                emit_stores(e, ex)
                if m >= 2:
                    e.wait_ge(sems["t"], t0d + (m - 1) * NT)
                elif t0d > 0:
                    e.wait_ge(sems["t"], t0d)
                sm = ("wa", "wb")[m % 2]
                e.dma_start(out=G["wbufA" if m % 2 == 0 else "wbufB"].ap()[:, 0:NH * 128].rearrange(
                    "p (k m) -> p k m", k=NH),
                    in_=w2r[:, :, m * 128:(m + 1) * 128]).then_inc(sems[sm], 16)
                c.n[sm] += 16
                wtd.append((sm, c.n[sm]))

            bsel = []

            @blk.tensor
            def _(e, m=m, hedone=hedone):
                e.wait_ge(sems[wtd[m][0]], wtd[m][1])
                if m == 0:
                    e.wait_ge(sems["ev"], hedone)
                wv2 = G["wbufA" if m % 2 == 0 else "wbufB"].ap()[:, 0:NH * 128].rearrange(
                    "p (k m) -> p k m", k=NH)
                for g in range(NT):
                    b = (m * NT + g) % 6
                    bsel.append(b)
                    c.wait_bank(e, b)
                    for k in range(NH):
                        mi = e.matmul(ps[b][:, 0:TW], wv2[:, k, :],
                                      he[:, k, g * TW:(g + 1) * TW],
                                      start=(k == 0), stop=(k == NH - 1))
                    c.tag(mi, "t")
            tgrp = c.n["t"] - NT

            for g in range(NT):
                b = bsel[g]
                idx = m * NT + g
                par = idx % 4

                @blk.vector
                def _(e, b=b, par=par, tgrp=tgrp, g=g):
                    e.wait_ge(sems["t"], tgrp + g + 1)
                    c.wait(e, dpar[par % 2])
                    c.tag(e.tensor_copy(zb[:, par, :], ps[b][:, 0:TW]), "ev")
                c.free_bank(b, "ev")
                ev_of_group[(m, g)] = c.n["ev"]
                pend.append((m, g, par))

        @blk.gpsimd
        def _(e, ex=ex):
            emit_stores(e, ex)

    ar.freeb("he", "pbc", "prow", "gwt", "sco", "sbx", "mk1", "sb2", "mk2",
             "ptm", "mxc", "fm1", "fm1c", "fm2", "fzb")
    zrt = ar.alloc("zrt", [128, NKD, TW], BF16)
    zf = ar.alloc("zf", [128, NKD, TW], BF16)
    s1row = ar.alloc("s1row", [128, NKD, TW], BF16)
    sqf = ar.alloc("sqf", [128, TW], BF16)
    for nt in range(NT):
        @blk.gpsimd
        def _(e, nt=nt):
            c.wait(e, "da")
            c.wait(e, "db")
            if nt > 0:
                c.wait(e, "v")
            c.tag(e.dma_start(out=zrt[:], in_=sp["zr_sp"].ap().rearrange(
                "(k p) t -> p k t", p=128)[:, :, nt * TW:(nt + 1) * TW]), "xq")
        zld = c.n["xq"]

        @blk.vector
        def _(e, nt=nt, zld=zld):
            e.wait_ge(sems["xq"], zld)
            for k in range(NKD):
                c.tag(e.tensor_tensor(zf[:, k, :], zrt[:, k, :],
                                      s0r[:, k, nt * TW:(nt + 1) * TW], ALU.add), "v")

        def out_ln1(e, k, A, Bn, nt=nt):
            e.tensor_tensor(s1row[:, k, :], zf[:, k, :], A, ALU.mult)
            return e.tensor_tensor(s1row[:, k, :], s1row[:, k, :], Bn, ALU.subtract)

        emit_stats_ln(c, G, lambda k: zf[:, k, :], TW, out_ln1, kind="ln", sq_buf=sqf)

        @blk.sync
        def _(e, nt=nt):
            c.wait(e, "v")
            c.tag(e.dma_start(out=sp["s1_sp"].ap().rearrange(
                "(k p) t -> p k t", p=128)[:, :, nt * TW:(nt + 1) * TW], in_=s1row[:]), "sq")

    ar.freeb("s0r", "zrt", "zf", "s1row", "sqf")



def emit_l1_attn(c, ar, G):
    nc, blk, sems = c.nc, c.blk, c.sems
    sp, wd, svec, dvec, ps = G["sp"], G["wd"], G["svec"], G["dvec"], G["ps"]

    def svc(name, i):
        return svec[:, SV[name] + i:SV[name] + i + 1]

    hln = ar.alloc("hln", [128, NKD, T], BF16)
    s1t = ar.alloc("s1t", [128, NKD, TW], BF16)
    sqg = ar.alloc("sqg", [128, TW], BF16)
    for nt in range(NT):
        @blk.sync
        def _(e, nt=nt):
            c.wait(e, "sq")
            if nt > 0:
                c.wait(e, "v")
            c.tag(e.dma_start(out=s1t[:], in_=sp["s1_sp"].ap().rearrange(
                "(k p) t -> p k t", p=128)[:, :, nt * TW:(nt + 1) * TW]), "sq")
        sld = c.n["sq"]

        @blk.vector
        def _(e, sld=sld):
            e.wait_ge(sems["sq"], sld)
            c.tag(e.engine_nop(), "v")

        def out_g1(e, k, A, Bn, nt=nt):
            e.tensor_tensor(hln[:, k, nt * TW:(nt + 1) * TW], s1t[:, k, :], A, ALU.mult)
            e.tensor_tensor(hln[:, k, nt * TW:(nt + 1) * TW],
                            hln[:, k, nt * TW:(nt + 1) * TW], Bn, ALU.subtract)
            return e.tensor_scalar(hln[:, k, nt * TW:(nt + 1) * TW],
                                   hln[:, k, nt * TW:(nt + 1) * TW],
                                   svc("ln_g", k), svc("ln_b", k), ALU.mult, ALU.add)

        emit_stats_ln(c, G, lambda k: s1t[:, k, :], TW, out_g1, kind="ln", sq_buf=sqg)

    ar.freeb("s1t", "sqg")
    qr = ar.alloc("qr", [128, NKD, T], BF16)
    krow = ar.alloc("krow", [128, 2, T], BF16)

    def evac_q(e, m, g, pst):
        return e.activation(qr[:, m, g * TW:(g + 1) * TW], pst[:, 0:TW], AF.Identity,
                            bias=svc("bq", m))
    mmloop(c, G, wd["w_q"], NKD, NKD, lambda k, g: hln[:, k, g * TW:(g + 1) * TW],
           [TW] * 4, "s", evac_q, ready=[("v", c.n["v"])])

    for wnm, bnm, spnm in (("w_k", "bk", "k_sp"), ("w_v", "bv", "v_sp")):
        st_b = c.n["sq"]

        def evac_kv(e, m, g, pst, bnm=bnm, st_b=st_b):
            if m >= 2 and g == 0:
                e.wait_ge(sems["sq"], st_b + 16 * m)
            return e.activation(krow[:, m % 2, g * TW:(g + 1) * TW], pst[:, 0:TW],
                                AF.Identity, bias=svc(bnm, m))

        t0k, e0k, _ = mmloop(c, G, wd[wnm], NKD, NKD,
                             lambda k, g: hln[:, k, g * TW:(g + 1) * TW],
                             [TW] * 4, "s", evac_kv, ready=[("v", c.n["v"])])

        @blk.sync
        def _(e, e0k=e0k, spnm=spnm):
            for m in range(NKD):
                e.wait_ge(sems["es"], e0k + (m + 1) * 4)
                c.tag(e.dma_start(out=sp[spnm].ap()[m * 128:(m + 1) * 128, :],
                                  in_=krow[:, m % 2, :]), "sq")

    ar.freeb("hln")
    qw = ar.alloc("qw", [128, NKD, T], BF16)
    gsum = ar.alloc("gsum", [128, 96], F32)
    ag2 = ar.alloc("ag2", [128, 8, 16], F32)
    wrow = ar.alloc("wrow", [128, T], F32)
    w2row = ar.alloc("w2row", [128, T], F32)

    sc = float(1.0 / np.sqrt(2048.0))

    def evac_qa(e, m, g, pst):
        return e.activation(qw[:, m, g * TW:(g + 1) * TW], pst[:, 0:TW], AF.Exp,
                            scale=sc, bias=dvec[:, 16 + m:17 + m])

    mmloop(c, G, wd["w_qa"], NKD, NKD, lambda k, g: qr[:, k, g * TW:(g + 1) * TW],
           [TW] * 4, "s", evac_qa, ready=[("es", c.n["es"])])

    @blk.vector
    def _(e):
        c.wait(e, "es")
        for m in range(NKD):
            c.tag(e.tensor_reduce(gsum[:, m:m + 1], qw[:, m, :], AX.X, ALU.add), "v")

    emit_carry_exchange(c, G, gsum, ag2)

    @blk.vector
    def _(e):
        c.wait(e, "p")
        for m in range(NKD):
            e.tensor_tensor_scan(wrow[:], qw[:, m, :], G["zrow"][:], 0.0, ALU.add, ALU.add)
            e.tensor_scalar(wrow[:], wrow[:], gsum[:, 16 + m:17 + m], None, ALU.add)
            e.reciprocal(w2row[:], wrow[:])
            e.tensor_tensor(w2row[:], w2row[:], qw[:, m, :], ALU.mult)
            c.tag(e.tensor_tensor(qw[:, m, :], w2row[:], qr[:, m, :], ALU.mult), "v")
            c.tag(e.tensor_reduce(gsum[:, m:m + 1], qw[:, m, :], AX.X, ALU.add), "v")

    emit_carry_exchange(c, G, gsum, ag2)
    ar.freeb("qr")
    mixed = ar.alloc("mixed", [128, NKD, T], BF16)

    kvb = c.n["v"]

    @blk.sync
    def _(e):
        c.wait(e, "p")
        for m in range(NKD):
            if m >= 2:
                e.wait_ge(sems["v"], kvb + (m - 1))
            c.tag(e.dma_start(out=krow[:, m % 2, :],
                              in_=sp["k_sp"].ap()[m * 128:(m + 1) * 128, :]), "sq")
    kld0 = c.n["sq"] - 16 * NKD

    @blk.vector
    def _(e, kld0=kld0):
        c.wait(e, "p")
        for m in range(NKD):
            e.wait_ge(sems["sq"], kld0 + 16 * (m + 1))
            e.tensor_tensor_scan(wrow[:], qw[:, m, :], G["zrow"][:], 0.0, ALU.add, ALU.add)
            e.tensor_scalar(wrow[:], wrow[:], gsum[:, 16 + m:17 + m], None, ALU.add)
            c.tag(e.tensor_tensor(mixed[:, m, :], wrow[:], krow[:, m % 2, :], ALU.mult), "v")

    ar.freeb("qw")
    kw = ar.alloc("kw", [128, NKD, T], BF16)

    def evac_ka(e, m, g, pst):
        return e.activation(kw[:, m, g * TW:(g + 1) * TW], pst[:, 0:TW], AF.Exp,
                            scale=sc, bias=dvec[:, 32 + m:33 + m])

    mmloop(c, G, wd["w_ka"], NKD, NKD, lambda k, g: mixed[:, k, g * TW:(g + 1) * TW],
           [TW] * 4, "s", evac_ka, ready=[("v", c.n["v"])])

    @blk.vector
    def _(e):
        c.wait(e, "es")
        for m in range(NKD):
            c.tag(e.tensor_reduce(gsum[:, m:m + 1], kw[:, m, :], AX.X, ALU.add), "v")

    emit_carry_exchange(c, G, gsum, ag2)

    @blk.vector
    def _(e):
        c.wait(e, "p")
        for m in range(NKD):
            e.tensor_tensor_scan(wrow[:], kw[:, m, :], G["zrow"][:], 0.0, ALU.add, ALU.add)
            e.tensor_scalar(wrow[:], wrow[:], gsum[:, 16 + m:17 + m], None, ALU.add)
            e.reciprocal(w2row[:], wrow[:])
            e.tensor_tensor(w2row[:], w2row[:], kw[:, m, :], ALU.mult)
            c.tag(e.tensor_tensor(kw[:, m, :], w2row[:], mixed[:, m, :], ALU.mult), "v")
            c.tag(e.tensor_reduce(gsum[:, m:m + 1], kw[:, m, :], AX.X, ALU.add), "v")

    emit_carry_exchange(c, G, gsum, ag2)

    vld_b = c.n["sq"]
    kvb2 = c.n["v"]

    @blk.sync
    def _(e, kvb2=kvb2):
        c.wait(e, "p")
        for m in range(NKD):
            if m >= 2:
                e.wait_ge(sems["v"], kvb2 + (m - 1))
            c.tag(e.dma_start(out=krow[:, m % 2, :],
                              in_=sp["v_sp"].ap()[m * 128:(m + 1) * 128, :]), "sq")

    @blk.vector
    def _(e, vld_b=vld_b):
        c.wait(e, "p")
        for m in range(NKD):
            e.wait_ge(sems["sq"], vld_b + 16 * (m + 1))
            e.tensor_tensor_scan(wrow[:], kw[:, m, :], G["zrow"][:], 0.0, ALU.add, ALU.add)
            e.tensor_scalar(wrow[:], wrow[:], gsum[:, 16 + m:17 + m], None, ALU.add)
            c.tag(e.tensor_tensor(mixed[:, m, :], wrow[:], krow[:, m % 2, :], ALU.mult), "v")

    ar.freeb("kw", "wrow", "w2row", "krow", "gsum", "ag2")
    s1t = ar.alloc("s1tb", [128, NKD, TW], BF16)
    sqg = ar.alloc("sqgb", [128, TW], BF16)
    s2row = ar.alloc("s2row", [128, NKD, TW], BF16)
    zf2 = ar.alloc("zf2", [128, NKD, TW], BF16)
    for nt in range(NT):
        @blk.sync
        def _(e, nt=nt):
            c.wait(e, "v")
            c.tag(e.dma_start(out=s1t[:], in_=sp["s1_sp"].ap().rearrange(
                "(k p) t -> p k t", p=128)[:, :, nt * TW:(nt + 1) * TW]), "sq")
        sld = c.n["sq"]

        @blk.vector
        def _(e, nt=nt, sld=sld):
            e.wait_ge(sems["sq"], sld)
            for k in range(NKD):
                c.tag(e.tensor_tensor(zf2[:, k, :], s1t[:, k, :],
                                      mixed[:, k, nt * TW:(nt + 1) * TW], ALU.add), "v")

        def out_g7(e, k, A, Bn):
            e.tensor_tensor(s2row[:, k, :], zf2[:, k, :], A, ALU.mult)
            return e.tensor_tensor(s2row[:, k, :], s2row[:, k, :], Bn, ALU.subtract)

        emit_stats_ln(c, G, lambda k: zf2[:, k, :], TW, out_g7, kind="ln", sq_buf=sqg)

        @blk.sync
        def _(e, nt=nt):
            c.wait(e, "v")
            c.tag(e.dma_start(out=sp["s2_sp"].ap().rearrange(
                "(k p) t -> p k t", p=128)[:, :, nt * TW:(nt + 1) * TW], in_=s2row[:]), "sq")

    ar.freeb("s1tb", "sqgb", "mixed", "s2row", "zf2")


def emit_carry_exchange(c, G, gsum, ag2):
    blk, sems = c.blk, c.sems
    emit_allgather(c, G, gsum[:, 0:16], 16, ag2)

    @blk.vector
    def _(e):
        c.wait(e, "gq")
        c.tag(e.memset(gsum[:, 16:32], 0.0), "v")
    for r in range(8):
        if r % 2 == 0:
            @blk.vector
            def _(e, r=r):
                c.wait(e, "p")
                c.tag(e.tensor_scalar(gsum[:, 32:48], ag2[:, r, :], G["rmask"][:, r:r + 1],
                                      None, ALU.mult), "v")

            @blk.gpsimd
            def _(e, r=r):
                c.wait(e, "v")
                c.tag(e.tensor_tensor(gsum[:, 16:32], gsum[:, 16:32], gsum[:, 32:48],
                                      ALU.add), "p")
        else:
            @blk.gpsimd
            def _(e, r=r):
                c.wait(e, "v")
                c.wait(e, "p")
                c.tag(e.tensor_scalar(gsum[:, 48:64], ag2[:, r, :], G["rmask"][:, r:r + 1],
                                      None, ALU.mult), "p")

            @blk.vector
            def _(e, r=r):
                c.wait(e, "p")
                c.tag(e.tensor_tensor(gsum[:, 16:32], gsum[:, 16:32], gsum[:, 48:64],
                                      ALU.add), "v")
    @blk.vector
    def _(e):
        c.wait(e, "p")
        c.tag(e.engine_nop(), "p")



def emit_l1_ffn(c, ar, G):
    nc, blk, sems = c.nc, c.blk, c.sems
    sp, wd, svec, ps = G["sp"], G["wd"], G["svec"], G["ps"]
    NI = 64

    s2r = ar.alloc("s2r", [128, NKD, T], BF16)
    hbuf = ar.alloc("hbuf", [128, 2, 2, D], BF16)
    rs_bf = ar.alloc("rs_bf", [1, T], F32)
    hmrow = ar.alloc("hmrow", [128, 2, T], BF16)
    hm1 = ar.alloc("hm1", [128, 2, TW], F32)
    hm2 = ar.alloc("hm2", [128, 2, TW], F32)

    pc0 = c.n["xq"]
    st0 = c.n["sq"]

    @blk.gpsimd
    def _(e):
        c.wait(e, "v")
        c.wait(e, "sq")
        for i in range(32):
            if i >= 2:
                e.wait_ge(sems["sq"], st0 + 16 * (i - 1))
            c.tag(e.dma_start(out=hbuf[:, i % 2, :, :], in_=wd["w_d8"].rearrange(
                "(i p) d -> p i d", p=128)[:, i * 2:(i + 1) * 2, :]), "xq")

    @blk.sync
    def _(e):
        for i in range(32):
            e.wait_ge(sems["xq"], pc0 + 16 * (i + 1))
            c.tag(e.dma_start(out=sp["d8bf_sp"].ap().rearrange(
                "(i p) d -> p i d", p=128)[:, i * 2:(i + 1) * 2, :],
                in_=hbuf[:, i % 2, :, :]), "sq")

    @blk.sync
    def _(e):
        c.wait(e, "sq")
        c.tag(e.dma_start(out=s2r[:], in_=sp["s2_sp"].ap().rearrange(
            "(k p) t -> p k t", p=128)), "sq")
    s2ld = c.n["sq"]

    t0r = c.n["t"]

    @blk.tensor
    def _(e):
        e.wait_ge(sems["sq"], s2ld)
        for g in range(NT):
            b = g % 6
            c.wait_bank(e, b)
            for k in range(NKD):
                mi = e.matmul(ps[b][0:1, 0:TW], G["ones_bf"][:],
                              s2r[:, k, g * TW:(g + 1) * TW],
                              start=(k == 0), stop=(k == NKD - 1))
            c.tag(mi, "t")

    @blk.vector
    def _(e, t0r=t0r):
        for g in range(NT):
            b = g % 6
            e.wait_ge(sems["t"], t0r + g + 1)
            c.tag(e.tensor_copy(rs_bf[0:1, g * TW:(g + 1) * TW], ps[b][0:1, 0:TW]), "v")
    for g in range(NT):
        c.free_bank(g % 6, "v")

    g8r = wd["w_g8"].rearrange("(k p) m -> p k m", p=128)
    u8r = wd["w_u8"].rearrange("(k p) m -> p k m", p=128)
    wtg = []
    t0h = c.n["t"]
    rsdone = c.n["v"]

    @blk.gpsimd
    def _(e, t0h=t0h):
        for f in range(NI):
            if f >= 2:
                e.wait_ge(sems["t"], t0h + (f - 1) * 2 * NT)
            elif t0h > 0:
                e.wait_ge(sems["t"], t0h)
            sm = ("wa", "wb")[f % 2]
            e.dma_start(out=G["wbufA" if f % 2 == 0 else "wbufB"].ap()[:, 0:NKD * 128].rearrange(
                "p (k m) -> p k m", k=NKD),
                in_=g8r[:, :, f * 128:(f + 1) * 128]).then_inc(sems[sm], 16)
            e.dma_start(out=G["wbufA" if f % 2 == 0 else "wbufB"].ap()[:, NKD * 128:2 * NKD * 128].rearrange(
                "p (k m) -> p k m", k=NKD),
                in_=u8r[:, :, f * 128:(f + 1) * 128]).then_inc(sems[sm], 16)
            c.n[sm] += 32
            wtg.append((sm, c.n[sm]))

    hst0 = c.n["sq"]
    for f in range(NI):
        bsel = []

        @blk.tensor
        def _(e, f=f):
            e.wait_ge(sems[wtg[f][0]], wtg[f][1])
            if f == 0:
                e.wait_ge(sems["v"], rsdone)
            wvg = G["wbufA" if f % 2 == 0 else "wbufB"].ap()[:, 0:NKD * 128].rearrange(
                "p (k m) -> p k m", k=NKD)
            wvu = G["wbufA" if f % 2 == 0 else "wbufB"].ap()[:, NKD * 128:2 * NKD * 128].rearrange(
                "p (k m) -> p k m", k=NKD)
            for g in range(NT):
                for br, wv_ in ((0, wvg), (1, wvu)):
                    b = (f * 8 + g * 2 + br) % 6
                    bsel.append(b)
                    c.wait_bank(e, b)
                    for k in range(NKD):
                        e.matmul(ps[b][:, 0:TW], wv_[:, k, :],
                                 s2r[:, k, g * TW:(g + 1) * TW],
                                 start=(k == 0), stop=False)
                    mi = e.matmul(ps[b][:, 0:TW], G["neg128f"][:],
                                  rs_bf[0:1, g * TW:(g + 1) * TW],
                                  start=False, stop=True)
                    c.tag(mi, "t")
        tgrp = c.n["t"] - 2 * NT

        for g in range(NT):
            b1 = bsel[g * 2]
            b2 = bsel[g * 2 + 1]
            par = (f * NT + g) % 2

            @blk.scalar
            def _(e, f=f, g=g, b1=b1, b2=b2, par=par, tgrp=tgrp):
                e.wait_ge(sems["t"], tgrp + g * 2 + 2)
                c.wait(e, "ev")
                e.activation(hm1[:, par, :], ps[b1][:, 0:TW], AF.Silu,
                             scale=svec[:, SV["gscale"] + f:SV["gscale"] + f + 1])
                c.tag(e.activation(hm2[:, par, :], ps[b2][:, 0:TW], AF.Identity,
                                   scale=svec[:, SV["uscale"] + f:SV["uscale"] + f + 1]), "es")
            c.free_bank(b1, "es")
            c.free_bank(b2, "es")

            @blk.vector
            def _(e, f=f, g=g, par=par):
                c.wait(e, "es")
                if f >= 2 and g == 0:
                    e.wait_ge(sems["sq"], hst0 + 16 * (f - 1))
                c.tag(e.tensor_tensor(hmrow[:, f % 2, g * TW:(g + 1) * TW],
                                      hm1[:, par, :], hm2[:, par, :], ALU.mult), "ev")

        @blk.sync
        def _(e, f=f):
            c.wait(e, "ev")
            c.tag(e.dma_start(out=sp["hm_sp"].ap()[f * 128:(f + 1) * 128, :],
                              in_=hmrow[:, f % 2, :]), "sq")

    ar.freeb("hbuf", "s2r", "rs_bf", "hmrow", "hm1", "hm2")
    s2t = ar.alloc("s2t", [128, NKD, TW], BF16)
    hmt = ar.alloc("hmt", [128, NI, TW], BF16)
    rs2 = ar.alloc("rs2", [1, TW], F32)
    zq = ar.alloc("zq", [128, 2, TW], F32)
    zf3 = ar.alloc("zf3", [128, NKD, TW], BF16)
    yrow = ar.alloc("yrow", [128, NKD, TW], F32)
    sqh = ar.alloc("sqh", [128, TW], BF16)

    d8r = sp["d8bf_sp"].ap().rearrange("(k p) m -> p k m", p=128)
    for nt in range(NT):
        @blk.sync
        def _(e, nt=nt):
            c.wait(e, "sq")
            if nt > 0:
                c.wait(e, "v")
                c.wait(e, "t", c.n["t"])
            c.tag(e.dma_start(out=hmt[:], in_=sp["hm_sp"].ap().rearrange(
                "(k p) t -> p k t", p=128)[:, :, nt * TW:(nt + 1) * TW]), "sq")
            c.tag(e.dma_start(out=s2t[:], in_=sp["s2_sp"].ap().rearrange(
                "(k p) t -> p k t", p=128)[:, :, nt * TW:(nt + 1) * TW]), "sq")
        hld = c.n["sq"]

        t0s = c.n["t"]

        @blk.tensor
        def _(e, hld=hld):
            e.wait_ge(sems["sq"], hld)
            b = 6
            c.wait_bank(e, b)
            for k in range(NI):
                mi = e.matmul(ps[6][0:1, 0:TW], G["ones_bf"][:], hmt[:, k, :],
                              start=(k == 0), stop=(k == NI - 1))
            c.tag(mi, "t")

        @blk.vector
        def _(e, t0s=t0s):
            e.wait_ge(sems["t"], t0s + 1)
            c.tag(e.tensor_copy(rs2[0:1, :], ps[6][0:1, 0:TW]), "v")
        c.free_bank(6, "v")

        wtd = []
        t0d = c.n["t"]
        rdn = c.n["v"]

        @blk.sync
        def _(e, t0d=t0d):
            for m in range(NKD):
                if m >= 2:
                    e.wait_ge(sems["t"], t0d + (m - 1))
                elif t0d > 0:
                    e.wait_ge(sems["t"], t0d)
                sm = ("ya", "yb")[m % 2]
                e.dma_start(out=G["wbufA" if m % 2 == 0 else "wbufB"].ap()[:, 0:NI * 128].rearrange(
                    "p (k m) -> p k m", k=NI),
                    in_=d8r[:, :, m * 128:(m + 1) * 128]).then_inc(sems[sm], 16)
                c.n[sm] += 16
                wtd.append((sm, c.n[sm]))

        for m in range(NKD):
            @blk.tensor
            def _(e, m=m, rdn=rdn):
                e.wait_ge(sems[wtd[m][0]], wtd[m][1])
                if m == 0:
                    e.wait_ge(sems["v"], rdn)
                wvd = G["wbufA" if m % 2 == 0 else "wbufB"].ap()[:, 0:NI * 128].rearrange(
                    "p (k m) -> p k m", k=NI)
                b = m % 6
                c.wait_bank(e, b)
                for k in range(NI):
                    e.matmul(ps[b][:, 0:TW], wvd[:, k, :], hmt[:, k, :],
                             start=(k == 0), stop=False)
                mi = e.matmul(ps[b][:, 0:TW], G["neg128f"][:], rs2[0:1, 0:TW],
                              start=False, stop=True)
                c.tag(mi, "t")
            tv = c.n["t"]

            @blk.scalar
            def _(e, m=m, tv=tv):
                e.wait_ge(sems["t"], tv)
                c.wait(e, "v")
                c.tag(e.activation(zq[:, m % 2, :], ps[m % 6][:, 0:TW], AF.Identity,
                                   scale=svec[:, SV["dscale"] + m:SV["dscale"] + m + 1]), "es")
            c.free_bank(m % 6, "es")

            @blk.vector
            def _(e, m=m, nt=nt):
                c.wait(e, "es")
                c.tag(e.tensor_tensor(zf3[:, m, :], zq[:, m % 2, :],
                                      s2t[:, m, :], ALU.add), "v")

        def out_fin(e, k, A, Bn):
            e.tensor_tensor(yrow[:, k, :], zf3[:, k, :], A, ALU.mult)
            return e.tensor_tensor(yrow[:, k, :], yrow[:, k, :], Bn, ALU.subtract)

        emit_stats_ln(c, G, lambda k: zf3[:, k, :], TW, out_fin, kind="ln", sq_buf=sqh)

        @blk.sync
        def _(e, nt=nt):
            c.wait(e, "v")
            c.tag(e.dma_start(out=G["y"].rearrange(
                "(k p) t -> p k t", p=128)[:, :, nt * TW:(nt + 1) * TW], in_=yrow[:]), "sq")

    ar.freeb("s2t", "hmt", "rs2", "zq", "zf3", "yrow", "sqh")



def kernel(**inputs):
    sh, per = stage_host(inputs)
    nc = build_program(dbg=DBG)
    in_maps = []
    for cc in range(8):
        m = dict(sh)
        m.update(per[cc])
        in_maps.append(m)
    res = run_bass_kernel_spmd(nc, in_maps, core_ids=list(range(8)))
    outs = []
    for cc in range(8):
        outs.append(res.results[cc]["y"].T)
    full = np.empty((2, 8192, D), np.float32)
    for cc in range(8):
        g, p = cc // 4, cc % 4
        full[g, p * T:(p + 1) * T, :] = outs[cc]
    return full


# version 06
# speedup vs baseline: 1.0252x; 1.0245x over previous
import numpy as np
import ml_dtypes
from contextlib import ExitStack

import concourse.bass as bass
import concourse.mybir as mybir
from concourse.bass_utils import run_bass_kernel_spmd

F32 = mybir.dt.float32
BF16 = mybir.dt.bfloat16
I32 = mybir.dt.int32
AF = mybir.ActivationFunctionType
ALU = mybir.AluOpType
AX = mybir.AxisListType
BF = ml_dtypes.bfloat16

DBG = ()
F_EXPERTS = list(range(9))
D = 2048
TW = 512
T = 4 * TW
NKD = 16
NT = 4
F_UP = 6144
HS = 1408
I8 = 8192
NE = 8

SV = {}
_c = 0
for _n, _k in [("norm1_w", 16), ("left_b", 16), ("right_b", 16), ("conv_b", 16),
               ("cw0", 16), ("cw1", 16), ("cw2", 16), ("cw3", 16),
               ("ga_b", 16), ("gx_b", 16), ("a_param", 16), ("out_b", 16),
               ("norm2_w", 16), ("up1_b", 48), ("up2_b", 48), ("down_b", 16),
               ("ln_g", 16), ("ln_b", 16), ("bq", 16), ("bk", 16), ("bv", 16),
               ("bqa", 16), ("bka", 16),
               ("gscale", 64), ("uscale", 64), ("dscale", 16)]:
    SV[_n] = _c
    _c += _k
NSV = _c

CONSTS = [0.0, 1.0, 1e-8, 1e-5, 1.0 / 2048.0, -0.5, 0.5, -1.0, 2.0,
          float(1.0 / np.sqrt(2048.0))]



def _pack_vec(v):
    v = np.asarray(v, np.float32)
    return np.ascontiguousarray(v.reshape(v.shape[0] // 128, 128).T)


def _t(a):
    return np.ascontiguousarray(np.asarray(a).T)


def stage_host(inputs):
    sh = {}
    x = np.asarray(inputs["L0_attn_x"], np.float32)
    sv = np.zeros((128, NSV), np.float32)

    def put(name, v):
        p = _pack_vec(v)
        sv[:, SV[name]:SV[name] + p.shape[1]] = p

    put("norm1_w", inputs["L0_attn_norm1_w"]); put("left_b", inputs["L0_attn_left_b"])
    put("right_b", inputs["L0_attn_right_b"]); put("conv_b", inputs["L0_attn_conv_b"])
    for k in range(4):
        put(f"cw{k}", inputs["L0_attn_conv_w"][k])
    put("ga_b", inputs["L0_attn_ga_b"]); put("gx_b", inputs["L0_attn_gx_b"])
    put("a_param", inputs["L0_attn_a_param"]); put("out_b", inputs["L0_attn_out_b"])
    put("norm2_w", inputs["L0_attn_norm2_w"])
    put("up1_b", inputs["L0_attn_up1_b"]); put("up2_b", inputs["L0_attn_up2_b"])
    put("down_b", inputs["L0_attn_down_b"])
    put("ln_g", inputs["L1_attn_ln_g"]); put("ln_b", inputs["L1_attn_ln_b"])
    put("bq", inputs["L1_attn_bq"]); put("bk", inputs["L1_attn_bk"]); put("bv", inputs["L1_attn_bv"])
    put("bqa", inputs["L1_attn_bqa"]); put("bka", inputs["L1_attn_bka"])
    put("gscale", inputs["L1_ffn_gate_scale"]); put("uscale", inputs["L1_ffn_up_scale"])
    put("dscale", inputs["L1_ffn_down_scale"])
    sh["svec"] = sv

    sh["w_left"] = np.asarray(inputs["L0_attn_left_W"], np.float32)
    sh["w_right"] = np.asarray(inputs["L0_attn_right_W"], np.float32)
    sh["w_ga"] = np.asarray(inputs["L0_attn_ga_W"], np.float32)
    sh["w_gx"] = np.asarray(inputs["L0_attn_gx_W"], np.float32)
    sh["w_out"] = np.asarray(inputs["L0_attn_out_W"], np.float32)
    sh["w_up1"] = np.asarray(inputs["L0_attn_up1_W"], np.float32)
    sh["w_up2"] = np.asarray(inputs["L0_attn_up2_W"], np.float32)
    sh["w_down"] = np.asarray(inputs["L0_attn_down_W"], np.float32)
    sh["w_gate0"] = _t(inputs["L0_ffn_gate_w"])
    sh["w_sw1"] = _t(inputs["L0_ffn_sw1"]); sh["w_sw3"] = _t(inputs["L0_ffn_sw3"])
    sh["w_sw2"] = _t(inputs["L0_ffn_sw2"])
    sh["w_e1"] = np.ascontiguousarray(np.asarray(inputs["L0_ffn_w1"], np.float32).transpose(0, 2, 1))
    sh["w_e3"] = np.ascontiguousarray(np.asarray(inputs["L0_ffn_w3"], np.float32).transpose(0, 2, 1))
    sh["w_e2"] = np.ascontiguousarray(np.asarray(inputs["L0_ffn_w2"], np.float32).transpose(0, 2, 1))
    sh["ebias_rep"] = np.ascontiguousarray(
        np.broadcast_to(np.asarray(inputs["L0_ffn_expert_bias"], np.float32)[None, :], (128, 8)))
    sh["w_q"] = _t(inputs["L1_attn_Wq"]); sh["w_k"] = _t(inputs["L1_attn_Wk"])
    sh["w_v"] = _t(inputs["L1_attn_Wv"]); sh["w_qa"] = _t(inputs["L1_attn_Wqa"])
    sh["w_ka"] = _t(inputs["L1_attn_Wka"])
    sh["w_g8"] = _t(inputs["L1_ffn_gate_w"]); sh["w_u8"] = _t(inputs["L1_ffn_up_w"])
    sh["w_d8"] = _t(inputs["L1_ffn_down_w"])
    sh["eye"] = np.eye(128, dtype=BF)
    cst = np.zeros((128, 16), np.float32)
    for i, v in enumerate(CONSTS):
        cst[:, i] = v
    sh["consts"] = cst

    per = []
    for cc in range(8):
        g, p = cc // 4, cc % 4
        t0 = p * T
        m = dict()
        m["xT"] = np.ascontiguousarray(x[g, t0:t0 + T, :].T)
        m["xTh"] = (np.zeros((D, 3), np.float32) if p == 0
                    else np.ascontiguousarray(x[g, t0 - 3:t0, :].T))
        m["cmask"] = np.full((128, 4), 0.0 if p == 0 else 1.0, np.float32)
        rm = np.zeros((128, 8), np.float32)
        for r in range(8):
            if r // 4 == g and (r % 4) < p:
                rm[:, r] = 1.0
        m["rmask"] = rm
        m["rmask1m"] = np.ascontiguousarray(1.0 - rm)
        per.append(m)
    return sh, per



SIM_ARENA = False


class Arena:
    def __init__(self, nc, base, top):
        self.nc = nc
        if SIM_ARENA:
            top = base + (1 << 22)
        self.free = [(base, top)]
        self.live = {}

    def alloc(self, name, shape, dtype):
        nb = int(np.prod(shape[1:])) * mybir.dt.size(dtype)
        nb = (nb + 31) & ~31
        for i, (lo, hi) in enumerate(self.free):
            if hi - lo >= nb:
                if lo + nb == hi:
                    del self.free[i]
                else:
                    self.free[i] = (lo + nb, hi)
                h = self.nc.alloc_sbuf_tensor_at(name, shape, dtype, offset=lo)
                self.live[name] = (lo, nb, h)
                return h
        raise RuntimeError(f"arena OOM {name}: need {nb}B, free={self.free}")

    def freeb(self, *names):
        for name in names:
            lo, nb, _ = self.live.pop(name)
            if SIM_ARENA:
                continue
            self.free.append((lo, lo + nb))
        self.free.sort()
        out = []
        for lo, hi in self.free:
            if out and out[-1][1] == lo:
                out[-1] = (out[-1][0], hi)
            else:
                out.append((lo, hi))
        self.free = out


class C:
    def __init__(self, nc, blk, sems):
        self.nc = nc
        self.blk = blk
        self.sems = sems
        self.n = {k: 0 for k in sems}
        self.bank_last = {}

    def tag(self, instr, sem, k=None):
        k = k if k is not None else (16 if sem in ("gq", "sq", "wa", "wb", "xq", "da", "db", "ya", "yb") else 1)
        instr.then_inc(self.sems[sem], k)
        self.n[sem] += k
        return self.n[sem]

    def wait(self, e, sem, val=None):
        v = self.n[sem] if val is None else val
        if v > 0:
            e.wait_ge(self.sems[sem], v)

    def use_bank(self, t, b, free_sem, free_val):
        prev = self.bank_last.get(b)
        if prev is not None:
            t.wait_ge(self.sems[prev[0]], prev[1])
        self.bank_last[b] = (free_sem, free_val)

    def wait_bank(self, t, b):
        prev = self.bank_last.get(b)
        if prev is not None:
            t.wait_ge(self.sems[prev[0]], prev[1])

    def free_bank(self, b, sem):
        self.bank_last[b] = (sem, self.n[sem])

    def barrier(self):
        vals = dict(self.n)
        for secname in ("tensor", "vector", "scalar", "gpsimd", "sync"):
            dec = getattr(self.blk, secname)

            @dec
            def _(e, vals=vals):
                for s, v in vals.items():
                    if v > 0:
                        e.wait_ge(self.sems[s], v)


def mmloop(c, G, wdram, NK, NM, rhs, groups, evac_eng, evac, MW=4, extra=None,
           wq="gq", ready=()):
    blk, sems, ps = c.blk, c.sems, G["ps"]
    NG = len(groups)
    esem = "es" if evac_eng == "s" else "ev"
    t0, e0, q0 = c.n["t"], c.n[esem], c.n[wq]
    if MW * NK * 128 > 8192:
        MW = max(1, 8192 // (NK * 128))
    NS = NM // MW
    assert NS * MW == NM, (NM, MW)
    wrt = wdram.rearrange("(k p) m -> p k m", p=128)
    wv = [G["wbufA"].ap()[:, 0:NK * MW * 128].rearrange("p (k m) -> p k m", k=NK),
          G["wbufB"].ap()[:, 0:NK * MW * 128].rearrange("p (k m) -> p k m", k=NK)]
    qeng = "gpsimd" if wq == "gq" else "sync"

    wsem = ["wa", "wb"]
    wtarget = []

    @getattr(blk, qeng)
    def _(e):
        for ms in range(NS):
            if ms >= 2:
                e.wait_ge(sems["t"], t0 + (ms - 1) * MW * NG)
            elif t0 > 0:
                e.wait_ge(sems["t"], t0)
            sm = wsem[ms % 2]
            e.dma_start(out=wv[ms % 2],
                        in_=wrt[:, :, ms * MW * 128:(ms + 1) * MW * 128]
                        ).then_inc(sems[sm], 16)
            c.n[sm] += 16
            wtarget.append((sm, c.n[sm]))

    @blk.tensor
    def _(e):
        for sm, vl in ready:
            if vl > 0:
                e.wait_ge(sems[sm], vl)
        idx = 0
        for ms in range(NS):
            e.wait_ge(sems[wtarget[ms][0]], wtarget[ms][1])
            for mc in range(MW):
                m = ms * MW + mc
                for g, ncols in enumerate(groups):
                    b = (idx % 6)
                    c.use_bank(e, b, esem, e0 + idx + 1)
                    for k in range(NK):
                        mi = e.matmul(ps[b][:, 0:ncols],
                                      wv[ms % 2][:, k, mc * 128:(mc + 1) * 128],
                                      rhs(k, g),
                                      start=(k == 0),
                                      stop=(k == NK - 1 and extra is None))
                    if extra is not None:
                        mi = extra(e, ps[b], m, g)
                    mi.then_inc(sems["t"], 1)
                    idx += 1
    c.n["t"] = t0 + NM * NG

    sec = blk.scalar if evac_eng == "s" else blk.vector

    @sec
    def _(e):
        idx = 0
        for m in range(NM):
            for g, ncols in enumerate(groups):
                b = (idx % 6)
                e.wait_ge(sems["t"], t0 + idx + 1)
                evac(e, m, g, ps[b]).then_inc(sems[esem], 1)
                idx += 1
    c.n[esem] = e0 + NM * NG
    return t0, e0, esem


def emit_stats_ln(c, G, zsrc_fn, ncols, out_fn, *, kind, sq_buf, zbf_buf=None):
    blk, sems, ps = c.blk, c.sems, G["ps"]
    srow, ones_bf, onesr_f = G["srow"], G["ones_bf"], G["onesr_f"]
    v0, t0, s0 = c.n["v"], c.n["t"], c.n["s"]

    @blk.vector
    def _(e):
        for k in range(NKD):
            e.wait_ge(sems["t"], t0 + k)
            if zbf_buf is not None:
                e.tensor_copy(zbf_buf[:, 0:ncols], zsrc_fn(k))
                src = zbf_buf[:, 0:ncols]
            else:
                src = zsrc_fn(k)
            c.tag(e.tensor_tensor(sq_buf[:, 0:ncols], src, src, ALU.mult), "v")

    @blk.tensor
    def _(e):
        c.use_bank(e, 6, "s", s0 + 1)
        if kind == "ln":
            c.use_bank(e, 4, "v", c.n["v"] + NKD + 2)
        for k in range(NKD):
            e.wait_ge(sems["v"], v0 + k + 1)
            mi = e.matmul(ps[6][0:1, 0:ncols], ones_bf[:], sq_buf[:, 0:ncols],
                          start=(k == 0), stop=(k == NKD - 1))
            if kind == "ln":
                src = zbf_buf[:, 0:ncols] if zbf_buf is not None else zsrc_fn(k)
                mi = e.matmul(ps[4][0:1, 0:ncols], ones_bf[:], src,
                              start=(k == 0), stop=(k == NKD - 1))
            c.tag(mi, "t")

    if kind == "rms":
        @blk.scalar
        def _(e):
            c.wait(e, "t")
            c.tag(e.activation(srow[0:1, 0:ncols], ps[6][0:1, 0:ncols], AF.Ln,
                               bias=1e-8, scale=1.0 / 2048.0), "s")

        @blk.vector
        def _(e):
            c.wait(e, "s")
            c.tag(e.tensor_copy(srow[0:1, 1536:1536 + ncols], srow[0:1, 0:ncols]), "v")

        @blk.scalar
        def _(e):
            c.wait(e, "v")
            c.tag(e.activation(G["srowb"][0:1, 0:ncols], srow[0:1, 1536:1536 + ncols],
                               AF.Exp, scale=-0.5), "s")
        @blk.tensor
        def _(e):
            c.wait(e, "s")
            c.use_bank(e, 5, "v", c.n["v"] + NKD)
            c.tag(e.matmul(ps[5][:, 0:ncols], G["onesr_bf"][:],
                           G["srowb"][0:1, 0:ncols], start=True, stop=True), "t")

        @blk.vector
        def _(e):
            c.wait(e, "t")
            for k in range(NKD):
                c.tag(out_fn(e, k, ps[5][:, 0:ncols], None), "v")
    else:
        @blk.vector
        def _(e):
            c.wait(e, "t")
            e.tensor_scalar(srow[0:1, 1024:1024 + ncols], ps[4][0:1, 0:ncols],
                            1.0 / 2048.0, None, ALU.mult)
            e.tensor_scalar(srow[0:1, 0:ncols], ps[6][0:1, 0:ncols],
                            1.0 / 2048.0, None, ALU.mult)
            e.tensor_tensor(srow[0:1, 512:512 + ncols], srow[0:1, 1024:1024 + ncols],
                            srow[0:1, 1024:1024 + ncols], ALU.mult)
            c.tag(e.tensor_tensor(srow[0:1, 0:ncols], srow[0:1, 0:ncols],
                                  srow[0:1, 512:512 + ncols], ALU.subtract), "v")

        @blk.scalar
        def _(e):
            c.wait(e, "v")
            c.tag(e.activation(srow[0:1, 512:512 + ncols], srow[0:1, 0:ncols],
                               AF.Ln, bias=1e-5), "s")

        @blk.vector
        def _(e):
            c.wait(e, "s")
            c.tag(e.tensor_copy(srow[0:1, 1536:1536 + ncols], srow[0:1, 512:512 + ncols]), "v")

        @blk.scalar
        def _(e):
            c.wait(e, "v")
            c.tag(e.activation(G["srowb"][0:1, 0:ncols], srow[0:1, 1536:1536 + ncols],
                               AF.Exp, scale=-0.5), "s")

        @blk.vector
        def _(e):
            c.wait(e, "s")
            c.tag(e.tensor_tensor(G["srowb"][0:1, 512:512 + ncols],
                                  srow[0:1, 1024:1024 + ncols],
                                  G["srowb"][0:1, 0:ncols], ALU.mult), "v")

        @blk.tensor
        def _(e):
            c.wait(e, "v")
            c.use_bank(e, 5, "v", c.n["v"] + NKD)
            e.matmul(ps[5][:, 0:ncols], G["onesr_bf"][:], G["srowb"][0:1, 0:ncols],
                     start=True, stop=True)
            c.use_bank(e, 3, "v", c.n["v"] + NKD)
            c.tag(e.matmul(ps[3][:, 0:ncols], G["onesr_bf"][:],
                           G["srowb"][0:1, 512:512 + ncols], start=True, stop=True), "t")

        @blk.vector
        def _(e):
            c.wait(e, "t")
            for k in range(NKD):
                c.tag(out_fn(e, k, ps[5][:, 0:ncols], ps[3][:, 0:ncols]), "v")


def emit_allgather(c, G, src_ap, width, dst_sb):
    blk, sems = c.blk, c.sems
    if width == 32:
        cc_in, cc_out = G["cc_in"], G["cc_out"]
    else:
        cc_in, cc_out = G["cc_in16"], G["cc_out16"]

    @blk.gpsimd
    def _(e):
        c.wait(e, "v")
        c.wait(e, "es")
        c.tag(e.dma_start(out=cc_in.ap()[:, 0:width], in_=src_ap), "gq")
        c.wait(e, "gq")
        c.tag(e.collective_compute(
            "AllGather", ALU.bypass,
            replica_groups=[[0, 1, 2, 3, 4, 5, 6, 7]],
            ins=[cc_in.ap()[:, 0:width].opt()],
            outs=[cc_out.ap()[:, 0:width].opt()],
        ), "cc")
        c.wait(e, "cc")
        c.tag(e.dma_start(
            out=dst_sb[:, :, 0:width],
            in_=cc_out.ap()[:, 0:width].rearrange("(r p) w -> p r w", p=128)), "gq")



def build_program(dbg=()):
    nc = bass.Bass("TRN2", target_bir_lowering=False)

    xT = nc.dram_tensor("xT", [D, T], F32, kind="ExternalInput").ap()
    xTh = nc.dram_tensor("xTh", [D, 3], F32, kind="ExternalInput").ap()
    svec_d = nc.dram_tensor("svec", [128, NSV], F32, kind="ExternalInput").ap()
    cmask_d = nc.dram_tensor("cmask", [128, 4], F32, kind="ExternalInput").ap()
    rmask_d = nc.dram_tensor("rmask", [128, 8], F32, kind="ExternalInput").ap()
    rmask1m_d = nc.dram_tensor("rmask1m", [128, 8], F32, kind="ExternalInput").ap()
    ebias_d = nc.dram_tensor("ebias_rep", [128, 8], F32, kind="ExternalInput").ap()
    eye_d = nc.dram_tensor("eye", [128, 128], BF16, kind="ExternalInput").ap()
    consts_d = nc.dram_tensor("consts", [128, 16], F32, kind="ExternalInput").ap()
    wd = {}
    for nm, shp, dt in [
        ("w_left", [D, D], F32), ("w_right", [D, D], F32), ("w_ga", [D, D], F32),
        ("w_gx", [D, D], F32), ("w_out", [D, D], F32), ("w_up1", [D, F_UP], F32),
        ("w_up2", [D, F_UP], F32), ("w_down", [F_UP, D], F32),
        ("w_gate0", [D, 8], F32), ("w_sw1", [D, HS], F32), ("w_sw3", [D, HS], F32),
        ("w_sw2", [HS, D], F32),
        ("w_e1", [NE, D, HS], F32), ("w_e3", [NE, D, HS], F32), ("w_e2", [NE, HS, D], F32),
        ("w_q", [D, D], F32), ("w_k", [D, D], F32), ("w_v", [D, D], F32),
        ("w_qa", [D, D], F32), ("w_ka", [D, D], F32),
        ("w_g8", [D, I8], I32), ("w_u8", [D, I8], I32), ("w_d8", [I8, D], I32),
    ]:
        wd[nm] = nc.dram_tensor(nm, shp, dt, kind="ExternalInput").ap()
    y_out = nc.dram_tensor("y", [D, T], F32, kind="ExternalOutput").ap()

    sp = {}
    for nm, shp, dt in [
        ("left_sp", [D, T], BF16), ("u_sp", [D, T], BF16), ("h_sp", [D, T], BF16),
        ("P_sp", [D, T], BF16), ("x1_sp", [D, T], BF16), ("s0_sp", [D, T], BF16),
        ("zr_sp", [D, T], F32), ("s1_sp", [D, T], BF16), ("k_sp", [D, T], BF16),
        ("v_sp", [D, T], BF16), ("s2_sp", [D, T], BF16),
        ("hm_sp", [I8, T], BF16), ("d8bf_sp", [I8, D], BF16),
        ("a_sp", [D, T], F32), ("b_sp", [D, T], F32),
    ]:
        sp[nm] = nc.dram_tensor(nm, shp, dt, kind="Internal")
    cc_in = nc.dram_tensor("cc_in", [128, 32], F32, kind="Internal")
    cc_out = nc.dram_tensor("cc_out", [1024, 32], F32, kind="Internal")
    cc_in16 = nc.dram_tensor("cc_in16", [128, 16], F32, kind="Internal")
    cc_out16 = nc.dram_tensor("cc_out16", [1024, 16], F32, kind="Internal")
    dbg_d = {nm: nc.dram_tensor("dbg_" + nm, [D, T], F32, kind="ExternalOutput").ap()
             for nm in dbg if nm != "gates"}
    if "gates" in dbg:
        dbg_d["gates"] = nc.dram_tensor("dbg_gates", [128, 4 * T], F32, kind="ExternalOutput").ap()
        dbg_d["ga"] = nc.dram_tensor("dbg_ga", [D, T], F32, kind="ExternalOutput").ap()
        dbg_d["gb"] = nc.dram_tensor("dbg_gb", [D, T], F32, kind="ExternalOutput").ap()

    with ExitStack() as ctx:
        svec = nc.alloc_sbuf_tensor("svt", [128, NSV], F32)
        dvec = nc.alloc_sbuf_tensor("dvt", [128, 48], F32)
        cmask = nc.alloc_sbuf_tensor("cmask_t", [128, 4], F32)
        rmask = nc.alloc_sbuf_tensor("rmask_t", [128, 8], F32)
        rmask1m = nc.alloc_sbuf_tensor("rmask1m_t", [128, 8], F32)
        ebias = nc.alloc_sbuf_tensor("ebias_t", [128, 8], F32)
        eye = nc.alloc_sbuf_tensor("eye_t", [128, 128], BF16)
        ones_bf = nc.alloc_sbuf_tensor("ones_bf", [128, 1], BF16)
        onesr_f = nc.alloc_sbuf_tensor("onesr_f", [1, 128], F32)
        onesr_bf = nc.alloc_sbuf_tensor("onesr_bf", [1, 128], BF16)
        neg128r = nc.alloc_sbuf_tensor("neg128r", [1, 128], BF16)
        neg128f = nc.alloc_sbuf_tensor("neg128f", [1, 128], F32)
        zrow = nc.alloc_sbuf_tensor("zrow", [128, T], BF16)
        wbufA = nc.alloc_sbuf_tensor("wbufA", [128, 8192], BF16)
        wbufB = nc.alloc_sbuf_tensor("wbufB", [128, 8192], BF16)
        srow = nc.alloc_sbuf_tensor("srow", [1, 2048], F32)
        srowb = nc.alloc_sbuf_tensor("srowb", [1, 2048], BF16)
        cbank = nc.alloc_sbuf_tensor("cbank", [128, 16], F32)
        tiny = nc.alloc_sbuf_tensor("tinyt", [128, 96], F32)
        ps = [nc.alloc_psum_tensor(f"pb{i}", [128, 512], F32) for i in range(7)]
        psb = nc.alloc_psum_tensor("pbb", [128, 1024], BF16)
        sems = {s: ctx.enter_context(nc.semaphore("sem_" + s))
                for s in ("gq", "sq", "t", "v", "s", "ev", "es", "cc", "wa", "wb", "xq", "p", "da", "db", "ya", "yb")}
        blk = ctx.enter_context(nc.Block())
        c = C(nc, blk, sems)

        for i, v in enumerate(CONSTS):
            nc.const_aps.register(F32, v, cbank[:, i:i + 1])
        probe = nc.alloc_sbuf_tensor("arena_probe", [128, 8], F32)
        base = nc.lookup_mloc(probe).addr + 64
        ar = Arena(nc, base, nc.sbuf_top - 64)

        G = dict(xT=xT, xTh=xTh, wd=wd, sp=sp, svec=svec, dvec=dvec, cmask=cmask,
                 rmask=rmask, rmask1m=rmask1m, ebias=ebias, eye=eye, ps=ps, psb=psb,
                 ones_bf=ones_bf, onesr_f=onesr_f, onesr_bf=onesr_bf, neg128r=neg128r,
                 zrow=zrow, srow=srow, srowb=srowb, tiny=tiny, wbufA=wbufA, wbufB=wbufB, neg128f=neg128f,
                 cc_in=cc_in, cc_out=cc_out, cc_in16=cc_in16, cc_out16=cc_out16,
                 dbg=dbg, dbg_d=dbg_d, y=y_out)

        G["cbank"] = cbank
        G["consts_d"] = consts_d
        emit_setup(c, G, svec_d, cmask_d, rmask_d, rmask1m_d, ebias_d, eye_d)
        emit_l0_attn(c, ar, G)
        emit_l0_ffn(c, ar, G)
        emit_l1_attn(c, ar, G)
        emit_l1_ffn(c, ar, G)

        if dbg:
            @blk.gpsimd
            def _(e):
                for s in ("sq", "v", "es", "ev", "s"):
                    c.wait(e, s)
                for nm in dbg:
                    src = {"left": "left_sp", "u": "u_sp", "h": "h_sp", "P": "P_sp",
                           "x1": "x1_sp", "s0": "s0_sp"}.get(nm)
                    if src:
                        c.tag(e.dma_start(out=dbg_d[nm].rearrange("d t -> d t"),
                                          in_=sp[src].ap()), "gq")

        @blk.gpsimd
        def _(e):
            for sname in ("sq", "v", "es", "ev", "s", "gq", "wa", "wb", "xq",
                          "da", "db", "ya", "yb", "t", "p", "cc"):
                c.wait(e, sname)
    return nc


def emit_setup(c, G, svec_d, cmask_d, rmask_d, rmask1m_d, ebias_d, eye_d):
    blk, sems = c.blk, c.sems
    svec, dvec, tiny = G["svec"], G["dvec"], G["tiny"]

    @blk.sync
    def _(e):
        c.tag(e.dma_start(out=G["svec"][:], in_=svec_d), "sq")
        c.tag(e.dma_start(out=G["cmask"][:], in_=cmask_d), "sq")
        c.tag(e.dma_start(out=G["rmask"][:], in_=rmask_d), "sq")
        c.tag(e.dma_start(out=G["rmask1m"][:], in_=rmask1m_d), "sq")
        c.tag(e.dma_start(out=G["ebias"][:], in_=ebias_d), "sq")
        c.tag(e.dma_start(out=G["eye"][:], in_=eye_d), "sq")
        c.tag(e.dma_start(out=G["cbank"][:], in_=G["consts_d"]), "sq")

    @blk.vector
    def _(e):
        c.wait(e, "sq")
        e.memset(G["ones_bf"][:], 1.0)
        e.memset(G["onesr_f"][:], 1.0)
        e.memset(G["onesr_bf"][:], 1.0)
        e.memset(G["neg128r"][:], -128.0)
        e.memset(G["neg128f"][:], -128.0)
        e.memset(G["zrow"][:], 0.0)
        a = svec[:, SV["a_param"]:SV["a_param"] + 16]
        c.tag(e.tensor_scalar(tiny[:, 0:16], a, -1.0, None, ALU.mult), "v")

    @blk.scalar
    def _(e):
        c.wait(e, "v")
        c.tag(e.activation(tiny[:, 16:32], tiny[:, 0:16], AF.Exp), "s")

    @blk.vector
    def _(e):
        c.wait(e, "s")
        c.tag(e.tensor_copy(tiny[:, 32:48], tiny[:, 16:32]), "v")

    @blk.scalar
    def _(e):
        c.wait(e, "v")
        c.tag(e.activation(tiny[:, 48:64], tiny[:, 32:48], AF.Ln, bias=1.0), "s")

    @blk.vector
    def _(e):
        c.wait(e, "s")
        a = svec[:, SV["a_param"]:SV["a_param"] + 16]
        c.tag(e.tensor_tensor(tiny[:, 64:80], a, tiny[:, 48:64], ALU.add), "v")

    @blk.scalar
    def _(e):
        c.wait(e, "v")
        c.tag(e.mul(dvec[:, 0:16], tiny[:, 64:80], -8.0), "s")

    @blk.vector
    def _(e):
        c.wait(e, "s")
        sc = float(1.0 / np.sqrt(2048.0))
        e.tensor_scalar(dvec[:, 16:32], svec[:, SV["bqa"]:SV["bqa"] + 16], sc, None, ALU.mult)
        c.tag(e.tensor_scalar(dvec[:, 32:48], svec[:, SV["bka"]:SV["bka"] + 16], sc, None, ALU.mult), "v")



def emit_l0_attn(c, ar, G):
    nc, blk, sems = c.nc, c.blk, c.sems
    sp, wd, svec, dvec, ps = G["sp"], G["wd"], G["svec"], G["dvec"], G["ps"]

    def svc(name, i):
        return svec[:, SV[name] + i:SV[name] + i + 1]

    xn = ar.alloc("xn", [128, NKD, 3 + T], BF16)
    x_t = ar.alloc("x_t", [128, NKD, TW], BF16)
    sqb = ar.alloc("sqb", [128, TW], BF16)

    for it in range(5):
        ncols = 3 if it == 0 else TW
        off = 0 if it == 0 else 3 + (it - 1) * TW
        xsrc = (G["xTh"].rearrange("(k p) t -> p k t", p=128) if it == 0 else
                G["xT"].rearrange("(k p) t -> p k t", p=128)[:, :, (it - 1) * TW:it * TW])

        @blk.gpsimd
        def _(e, xsrc=xsrc, ncols=ncols):
            c.wait(e, "v")
            c.tag(e.dma_start(out=x_t[:, :, 0:ncols], in_=xsrc), "gq")

        v0, t0, s0 = c.n["v"], c.n["t"], c.n["s"]

        @blk.vector
        def _(e, ncols=ncols, v0=v0, t0=t0):
            c.wait(e, "gq")
            for k in range(NKD):
                e.wait_ge(sems["t"], t0 + k)
                c.tag(e.tensor_tensor(sqb[:, 0:ncols], x_t[:, k, 0:ncols],
                                      x_t[:, k, 0:ncols], ALU.mult), "v")

        @blk.tensor
        def _(e, ncols=ncols, v0=v0, s0=s0):
            c.use_bank(e, 6, "s", s0 + 2)
            for k in range(NKD):
                e.wait_ge(sems["v"], v0 + k + 1)
                c.tag(e.matmul(ps[6][0:1, 0:ncols], G["ones_bf"][:], sqb[:, 0:ncols],
                               start=(k == 0), stop=(k == NKD - 1)), "t")

        @blk.scalar
        def _(e, ncols=ncols):
            c.wait(e, "t")
            c.tag(e.activation(G["srow"][0:1, 0:ncols], ps[6][0:1, 0:ncols], AF.Ln,
                               bias=1e-8, scale=1.0 / 2048.0), "s")

        @blk.vector
        def _(e, ncols=ncols):
            c.wait(e, "s")
            c.tag(e.tensor_copy(G["srow"][0:1, 1536:1536 + ncols], G["srow"][0:1, 0:ncols]), "v")

        @blk.scalar
        def _(e, ncols=ncols):
            c.wait(e, "v")
            c.tag(e.activation(G["srowb"][0:1, 0:ncols], G["srow"][0:1, 1536:1536 + ncols],
                               AF.Exp, scale=-0.5), "s")

        @blk.tensor
        def _(e, ncols=ncols):
            c.wait(e, "s")
            c.use_bank(e, 5, "v", c.n["v"] + NKD)
            c.tag(e.matmul(ps[5][:, 0:ncols], G["onesr_bf"][:],
                           G["srowb"][0:1, 0:ncols], start=True, stop=True), "t")

        @blk.vector
        def _(e, ncols=ncols, off=off):
            c.wait(e, "t")
            for k in range(NKD):
                c.tag(e.scalar_tensor_tensor(xn[:, k, off:off + ncols], x_t[:, k, 0:ncols],
                                             svc("norm1_w", k), ps[5][:, 0:ncols],
                                             ALU.mult, ALU.mult), "v")

    if "xn" in G["dbg"]:
        @blk.gpsimd
        def _(e):
            c.wait(e, "v")
            c.tag(e.dma_start(out=G["dbg_d"]["xn"].rearrange("(k p) t -> p k t", p=128),
                              in_=xn[:, :, 3:3 + T]), "gq")

    lrow = ar.alloc("lrow", [128, 2, T], BF16)
    st_base = c.n["sq"]

    def evac_left(e, m, g, pst):
        if m >= 2:
            e.wait_ge(sems["sq"], st_base + 16 * m)
        return e.activation(lrow[:, m % 2, g * TW:(g + 1) * TW], pst[:, 0:TW],
                            AF.Gelu, bias=svc("left_b", m))

    t0, e0, esem = mmloop(c, G, wd["w_left"], NKD, NKD,
                          lambda k, g: xn[:, k, 3 + g * TW:3 + (g + 1) * TW],
                          [TW] * 4, "s", evac_left, ready=[("v", c.n["v"])])

    @blk.sync
    def _(e):
        for m in range(NKD):
            e.wait_ge(sems["es"], e0 + (m + 1) * 4)
            c.tag(e.dma_start(out=sp["left_sp"].ap()[m * 128:(m + 1) * 128, :],
                              in_=lrow[:, m % 2, :]), "sq")

    ci_row = ar.alloc("ci_row", [128, 3 + T], BF16)
    urow = ar.alloc("urow", [128, 2, T], BF16)
    ust_base = c.n["sq"]
    vref = {}

    def rhs_right(k, g):
        if g == 0:
            return xn[:, k, 0:3]
        return xn[:, k, 3 + (g - 1) * TW:3 + g * TW]

    def evac_right(e, m, g, pst):
        if g == 0:
            if m >= 2:
                e.wait_ge(sems["sq"], ust_base + 16 * m)
            return e.tensor_scalar(ci_row[:, 0:3], pst[:, 0:3], svc("right_b", m),
                                   G["cmask"][:, 0:1], ALU.add, ALU.mult)
        li = e.tensor_scalar(ci_row[:, 3 + (g - 1) * TW:3 + g * TW], pst[:, 0:TW],
                             svc("right_b", m), None, ALU.add)
        if g == 4:
            e.tensor_scalar(urow[:, m % 2, :], ci_row[:, 0:T], svc("cw0", m),
                            svc("conv_b", m), ALU.mult, ALU.add)
            for kk in range(1, 4):
                li = e.scalar_tensor_tensor(urow[:, m % 2, :], ci_row[:, kk:kk + T],
                                            svc(f"cw{kk}", m), urow[:, m % 2, :],
                                            ALU.mult, ALU.add)
        return li

    t0r, e0r, _ = mmloop(c, G, wd["w_right"], NKD, NKD, rhs_right,
                         [3, TW, TW, TW, TW], "v", evac_right, ready=[("v", c.n["v"])])

    @blk.sync
    def _(e):
        for m in range(NKD):
            e.wait_ge(sems["ev"], e0r + (m + 1) * 5)
            c.tag(e.dma_start(out=sp["u_sp"].ap()[m * 128:(m + 1) * 128, :],
                              in_=urow[:, m % 2, :]), "sq")

    ar.freeb("xn", "x_t", "sqb", "lrow", "ci_row", "urow")

    emit_scan(c, ar, G)

    emit_gated_x1(c, ar, G)

    emit_l0_mlp(c, ar, G)


def emit_scan(c, ar, G):
    nc, blk, sems = c.nc, c.blk, c.sems
    sp, wd, svec, dvec, ps = G["sp"], G["wd"], G["svec"], G["dvec"], G["ps"]

    u = ar.alloc("u", [128, NKD, T], BF16)
    r_row = ar.alloc("r_row", [128, 2, T], F32)
    i_row = ar.alloc("i_row", [128, 2, T], F32)
    a_row = ar.alloc("a_row", [128, T], F32)
    w_row = ar.alloc("w_row", [128, T], F32)
    b_row = ar.alloc("b_row", [128, T], F32)
    h_row = ar.alloc("h_row", [128, T], F32)
    hP_bf = ar.alloc("hP_bf", [128, 2, T], BF16)
    agg = ar.alloc("agg", [128, 96], F32)
    ag_sb = ar.alloc("ag_sb", [128, 8, 32], F32)

    @blk.sync
    def _(e):
        c.wait(e, "sq")
        c.tag(e.dma_start(out=u[:], in_=sp["u_sp"].ap().rearrange("(k p) t -> p k t", p=128)), "sq")
    uld = c.n["sq"]

    t0, q0 = c.n["t"], c.n["gq"]
    e0 = c.n["es"]
    wrt_ga = wd["w_ga"].rearrange("(k p) m -> p k m", p=128)
    wrt_gx = wd["w_gx"].rearrange("(k p) m -> p k m", p=128)
    MW = 2
    NS = NKD // MW
    wv = [G["wbufA"].ap()[:, 0:32 * MW * 128].rearrange("p (k m) -> p k m", k=32),
          G["wbufB"].ap()[:, 0:32 * MW * 128].rearrange("p (k m) -> p k m", k=32)]

    wsem = ["wa", "wb"]
    wtarget = []

    @blk.gpsimd
    def _(e):
        for ms in range(NS):
            if ms >= 2:
                e.wait_ge(sems["t"], t0 + (ms - 1) * MW * 8)
            elif t0 > 0:
                e.wait_ge(sems["t"], t0)
            sm = wsem[ms % 2]
            e.dma_start(out=wv[ms % 2][:, 0:16, :],
                        in_=wrt_ga[:, :, ms * MW * 128:(ms + 1) * MW * 128]).then_inc(sems[sm], 16)
            e.dma_start(out=wv[ms % 2][:, 16:32, :],
                        in_=wrt_gx[:, :, ms * MW * 128:(ms + 1) * MW * 128]).then_inc(sems[sm], 16)
            c.n[sm] += 32
            wtarget.append((sm, c.n[sm]))

    @blk.tensor
    def _(e):
        e.wait_ge(sems["sq"], uld)
        idx = 0
        for ms in range(NS):
            e.wait_ge(sems[wtarget[ms][0]], wtarget[ms][1])
            for mc in range(MW):
                for br in range(2):
                    for g in range(4):
                        b = idx % 6
                        c.use_bank(e, b, "es", e0 + idx + 1)
                        for k in range(NKD):
                            mi = e.matmul(ps[b][:, 0:TW],
                                          wv[ms % 2][:, br * 16 + k, mc * 128:(mc + 1) * 128],
                                          u[:, k, g * TW:(g + 1) * TW],
                                          start=(k == 0), stop=(k == NKD - 1))
                        mi.then_inc(sems["t"], 1)
                        idx += 1
    c.n["t"] = t0 + NKD * 8

    vc0 = c.n["v"]
    st0 = c.n["sq"]
    for m in range(NKD):
        @blk.scalar
        def _(e, m=m):
            if m >= 2:
                e.wait_ge(sems["v"], vc0 + (m - 1) * 6)
            for br in range(2):
                dst = (r_row if br == 0 else i_row)
                bias = (svec[:, SV["ga_b"] + m:SV["ga_b"] + m + 1] if br == 0
                        else svec[:, SV["gx_b"] + m:SV["gx_b"] + m + 1])
                for g in range(4):
                    idx = m * 8 + br * 4 + g
                    b = idx % 6
                    e.wait_ge(sems["t"], t0 + idx + 1)
                    e.activation(dst[:, m % 2, g * TW:(g + 1) * TW], ps[b][:, 0:TW],
                                 AF.Sigmoid, bias=bias).then_inc(sems["es"], 1)
            c.n["es"] += 8

        @blk.vector
        def _(e, m=m):
            e.wait_ge(sems["es"], e0 + m * 8 + 8)
            if m >= 1 and "gates" in G["dbg"]:
                e.wait_ge(sems["sq"], st0 + 64 * m)
            c.tag(e.tensor_scalar(a_row[:], r_row[:, m % 2, :], dvec[:, m:m + 1], 0.0,
                                  ALU.mult, ALU.add, accum_out=agg[:, m:m + 1]), "v")

        @blk.scalar
        def _(e, m=m):
            c.wait(e, "v")
            c.tag(e.activation(w_row[:], a_row[:], AF.Exp, scale=2.0), "s")
            c.tag(e.activation(a_row[:], a_row[:], AF.Exp), "s")

        @blk.vector
        def _(e, m=m):
            c.wait(e, "s")
            c.tag(e.tensor_scalar(w_row[:], w_row[:], -1.0, 1.0, ALU.mult, ALU.add), "v")

        @blk.scalar
        def _(e, m=m):
            c.wait(e, "v")
            e.activation(w_row[:], w_row[:], AF.Ln)
            c.tag(e.activation(w_row[:], w_row[:], AF.Exp, scale=0.5), "s")

        @blk.vector
        def _(e, m=m):
            c.wait(e, "s")
            e.tensor_tensor(b_row[:], i_row[:, m % 2, :], u[:, m, :], ALU.mult)
            c.tag(e.tensor_tensor(b_row[:], b_row[:], w_row[:], ALU.mult), "v")
            if m >= 1:
                nst = 64 if "gates" in G["dbg"] else 32
                e.wait_ge(sems["sq"], st0 + nst * m)
            c.tag(e.tensor_tensor_scan(hP_bf[:, 0, :], a_row[:], G["zrow"][:],
                                       1.0, ALU.mult, ALU.add), "v")
            c.tag(e.tensor_tensor_scan(h_row[:], a_row[:], b_row[:],
                                       0.0, ALU.mult, ALU.add), "v")
            e.tensor_copy(hP_bf[:, 1, :], h_row[:])
            c.tag(e.tensor_copy(agg[:, 16 + m:17 + m], h_row[:, T - 1:T]), "v")

        @blk.sync
        def _(e, m=m):
            c.wait(e, "v")
            c.tag(e.dma_start(out=sp["P_sp"].ap()[m * 128:(m + 1) * 128, :],
                              in_=hP_bf[:, 0, :]), "sq")
            c.tag(e.dma_start(out=sp["h_sp"].ap()[m * 128:(m + 1) * 128, :],
                              in_=hP_bf[:, 1, :]), "sq")
            if "gates" in G["dbg"]:
                c.tag(e.dma_start(out=sp["a_sp"].ap()[m * 128:(m + 1) * 128, :],
                                  in_=a_row[:]), "sq")
                c.tag(e.dma_start(out=sp["b_sp"].ap()[m * 128:(m + 1) * 128, :],
                                  in_=b_row[:]), "sq")

        if m == 0 and "gates" in G["dbg"]:
            @blk.gpsimd
            def _(e):
                c.wait(e, "v"); c.wait(e, "s")
                dd = G["dbg_d"]["gates"]
                e.dma_start(out=dd[:, 0:T], in_=r_row[:, 0, :]).then_inc(sems["gq"], 16)
                e.dma_start(out=dd[:, T:2 * T], in_=i_row[:, 0, :]).then_inc(sems["gq"], 16)
                e.dma_start(out=dd[:, 2 * T:3 * T], in_=a_row[:]).then_inc(sems["gq"], 16)
                e.dma_start(out=dd[:, 3 * T:4 * T], in_=b_row[:]).then_inc(sems["gq"], 16)
                c.n["gq"] += 64

    @blk.scalar
    def _(e):
        c.wait(e, "v")
        c.tag(e.activation(agg[:, 32:48], agg[:, 0:16], AF.Exp), "s")

    @blk.vector
    def _(e):
        c.wait(e, "s")
        c.tag(e.tensor_copy(agg[:, 0:16], agg[:, 32:48]), "v")

    emit_allgather(c, G, agg[:, 0:32], 32, ag_sb)

    @blk.vector
    def _(e):
        c.wait(e, "gq")
        c.tag(e.memset(agg[:, 48:64], 0.0), "v")
    for r in range(8):
        @blk.vector
        def _(e, r=r):
            if r > 0:
                c.wait(e, "p")
            c.tag(e.tensor_scalar(agg[:, 0:16], ag_sb[:, r, 0:16], G["rmask"][:, r:r + 1],
                                  G["rmask1m"][:, r:r + 1], ALU.mult, ALU.add), "v")

        @blk.gpsimd
        def _(e, r=r):
            c.wait(e, "v")
            e.tensor_scalar(agg[:, 32:48], ag_sb[:, r, 16:32], G["rmask"][:, r:r + 1],
                            None, ALU.mult)
            c.tag(e.tensor_tensor(agg[:, 64:80], agg[:, 48:64], agg[:, 0:16], ALU.mult), "p")

        @blk.vector
        def _(e, r=r):
            c.wait(e, "p")
            c.tag(e.tensor_tensor(agg[:, 48:64], agg[:, 64:80], agg[:, 32:48], ALU.add), "v")

    ar.freeb("u", "r_row", "i_row", "a_row", "w_row", "b_row", "h_row", "hP_bf", "ag_sb")
    G["agg"] = agg


def emit_gated_x1(c, ar, G):
    nc, blk, sems = c.nc, c.blk, c.sems
    sp, wd, svec, ps = G["sp"], G["wd"], G["svec"], G["ps"]
    agg = G["agg"]

    gated = ar.alloc("gated", [128, NKD, T], BF16)
    left_t = ar.alloc("left_t", [128, NKD, TW], BF16)
    h_t = ar.alloc("h_t", [128, NKD, TW], BF16)
    P_t = ar.alloc("P_t", [128, NKD, TW], BF16)
    t2 = ar.alloc("t2", [128, TW], F32)

    ld0 = c.n["sq"]
    v0 = c.n["v"]

    @blk.sync
    def _(e):
        for nt in range(NT):
            if nt == 0:
                c.wait(e, "v")
                c.wait(e, "p")
            else:
                e.wait_ge(sems["v"], v0 + nt * 16)
            src = slice(nt * TW, (nt + 1) * TW)
            e.dma_start(out=left_t[:], in_=sp["left_sp"].ap().rearrange(
                "(k p) t -> p k t", p=128)[:, :, src]).then_inc(sems["sq"], 16)
            e.dma_start(out=h_t[:], in_=sp["h_sp"].ap().rearrange(
                "(k p) t -> p k t", p=128)[:, :, src]).then_inc(sems["sq"], 16)
            e.dma_start(out=P_t[:], in_=sp["P_sp"].ap().rearrange(
                "(k p) t -> p k t", p=128)[:, :, src]).then_inc(sems["sq"], 16)
    c.n["sq"] = ld0 + 48 * NT

    @blk.vector
    def _(e):
        for nt in range(NT):
            e.wait_ge(sems["sq"], ld0 + 48 * (nt + 1))
            for k in range(NKD):
                e.scalar_tensor_tensor(t2[:], P_t[:, k, :], agg[:, 48 + k:49 + k],
                                       h_t[:, k, :], ALU.mult, ALU.add)
                c.tag(e.tensor_tensor(gated[:, k, nt * TW:(nt + 1) * TW],
                                      left_t[:, k, :], t2[:], ALU.mult), "v")
    ar.freeb("left_t", "h_t", "P_t", "t2")

    if "gated" in G["dbg"]:
        @blk.gpsimd
        def _(e):
            c.wait(e, "v")
            c.tag(e.dma_start(out=G["dbg_d"]["gated"].rearrange("(k p) t -> p k t", p=128),
                              in_=gated[:]), "gq")

    xres = ar.alloc("xres", [128, NKD, T], BF16)
    x1row = ar.alloc("x1row", [128, 2, T], BF16)

    @blk.gpsimd
    def _(e):
        c.wait(e, "v")
        c.tag(e.dma_start(out=xres[:], in_=G["xT"].rearrange("(k p) t -> p k t", p=128)), "xq")
    xld = c.n["xq"]
    st0 = c.n["sq"]

    def evac_x1(e, m, g, pst):
        if m >= 2 and g == 0:
            e.wait_ge(sems["sq"], st0 + 16 * m)
        if m == 0 and g == 0:
            e.wait_ge(sems["xq"], xld)
        return e.scalar_tensor_tensor(x1row[:, m % 2, g * TW:(g + 1) * TW], pst[:, 0:TW],
                                      svec[:, SV["out_b"] + m:SV["out_b"] + m + 1],
                                      xres[:, m, g * TW:(g + 1) * TW], ALU.add, ALU.add)

    t0, e0, _ = mmloop(c, G, wd["w_out"], NKD, NKD,
                       lambda k, g: gated[:, k, g * TW:(g + 1) * TW],
                       [TW] * 4, "v", evac_x1, ready=[("v", c.n["v"])])

    @blk.sync
    def _(e):
        for m in range(NKD):
            e.wait_ge(sems["ev"], e0 + (m + 1) * 4)
            c.tag(e.dma_start(out=sp["x1_sp"].ap()[m * 128:(m + 1) * 128, :],
                              in_=x1row[:, m % 2, :]), "sq")

    ar.freeb("gated", "x1row", "xres")


def emit_l0_mlp(c, ar, G):
    nc, blk, sems = c.nc, c.blk, c.sems
    sp, wd, svec, ps = G["sp"], G["wd"], G["svec"], G["ps"]
    NF = 48

    x1_t = ar.alloc("x1_t", [128, NKD, TW], BF16)
    x1n_t = ar.alloc("x1n_t", [128, NKD, TW], BF16)
    mh = ar.alloc("mh", [128, NF, TW], BF16)
    m1b = ar.alloc("m1b", [128, TW], BF16)
    m2b = ar.alloc("m2b", [128, TW], BF16)
    zw = ar.alloc("zw", [128, NKD, TW], BF16)
    sqb2 = ar.alloc("sqb2", [128, TW], BF16)
    s0row = ar.alloc("s0row", [128, NKD, TW], BF16)
    xres_t = ar.alloc("xres_t", [128, NKD, TW], BF16)

    wrt1 = wd["w_up1"].rearrange("(k p) m -> p k m", p=128)
    wrt2 = wd["w_up2"].rearrange("(k p) m -> p k m", p=128)
    wrtd = wd["w_down"].rearrange("(k p) m -> p k m", p=128)
    MWU = 2
    wvU = [G["wbufA"].ap()[:, 0:32 * MWU * 128].rearrange("p (k m) -> p k m", k=32),
           G["wbufB"].ap()[:, 0:32 * MWU * 128].rearrange("p (k m) -> p k m", k=32)]
    wvD = [G["wbufA"].ap()[:, 0:48 * 128].rearrange("p (k m) -> p k m", k=48),
           G["wbufB"].ap()[:, 0:48 * 128].rearrange("p (k m) -> p k m", k=48)]


    for st in range(NT):
        ld0 = c.n["sq"]

        @blk.sync
        def _(e, st=st):
            c.wait(e, "v")
            c.wait(e, "ev")
            c.tag(e.dma_start(out=x1_t[:], in_=sp["x1_sp"].ap().rearrange(
                "(k p) t -> p k t", p=128)[:, :, st * TW:(st + 1) * TW]), "sq")

        @blk.gpsimd
        def _(e, st=st):
            c.wait(e, "ev")
            c.tag(e.dma_start(out=xres_t[:], in_=G["xT"].rearrange(
                "(k p) t -> p k t", p=128)[:, :, st * TW:(st + 1) * TW]), "xq")
        xres_ld = c.n["xq"]

        @blk.vector
        def _(e):
            c.wait(e, "sq")
            c.tag(e.engine_nop(), "v")

        def out_rms(e, k, A, Bn):
            return e.scalar_tensor_tensor(x1n_t[:, k, :], x1_t[:, k, :],
                                          svec[:, SV["norm2_w"] + k:SV["norm2_w"] + k + 1],
                                          A, ALU.mult, ALU.mult)

        emit_stats_ln(c, G, lambda k: x1_t[:, k, :], TW, out_rms, kind="rms", sq_buf=sqb2)

        t0, q0, e0s, e0v = c.n["t"], c.n["gq"], c.n["es"], c.n["ev"]
        MW = MWU
        NSf = NF // MW

        wsem = ["wa", "wb"]
        wtargetU = []

        @blk.gpsimd
        def _(e):
            for ms in range(NSf):
                if ms >= 2:
                    e.wait_ge(sems["t"], t0 + (ms - 1) * MW * 2)
                elif t0 > 0:
                    e.wait_ge(sems["t"], t0)
                sm = wsem[ms % 2]
                e.dma_start(out=wvU[ms % 2][:, 0:16, :],
                            in_=wrt1[:, :, ms * MW * 128:(ms + 1) * MW * 128]).then_inc(sems[sm], 16)
                e.dma_start(out=wvU[ms % 2][:, 16:32, :],
                            in_=wrt2[:, :, ms * MW * 128:(ms + 1) * MW * 128]).then_inc(sems[sm], 16)
                c.n[sm] += 32
                wtargetU.append((sm, c.n[sm]))

        @blk.tensor
        def _(e):
            e.wait_ge(sems["v"], c.n["v"])
            idx = 0
            for ms in range(NSf):
                e.wait_ge(sems[wtargetU[ms][0]], wtargetU[ms][1])
                for mc in range(MW):
                    f = ms * MW + mc
                    for br in range(2):
                        b = idx % 6
                        c.use_bank(e, b, "es", e0s + idx + 1)
                        for k in range(NKD):
                            mi = e.matmul(ps[b][:, 0:TW],
                                          wvU[ms % 2][:, br * 16 + k, mc * 128:(mc + 1) * 128],
                                          x1n_t[:, k, :], start=(k == 0), stop=(k == NKD - 1))
                        mi.then_inc(sems["t"], 1)
                        idx += 1
        c.n["t"] = t0 + NF * 2

        @blk.scalar
        def _(e):
            idx = 0
            for f in range(NF):
                for br in range(2):
                    b = idx % 6
                    e.wait_ge(sems["t"], t0 + idx + 1)
                    if f >= 1 and br == 0:
                        e.wait_ge(sems["ev"], e0v + f)
                    if br == 0:
                        e.activation(m1b[:], ps[b][:, 0:TW], AF.Gelu,
                                     bias=svec[:, SV["up1_b"] + f:SV["up1_b"] + f + 1]
                                     ).then_inc(sems["es"], 1)
                    else:
                        e.activation(m2b[:], ps[b][:, 0:TW], AF.Identity,
                                     bias=svec[:, SV["up2_b"] + f:SV["up2_b"] + f + 1]
                                     ).then_inc(sems["es"], 1)
                    idx += 1
        c.n["es"] = e0s + NF * 2

        @blk.vector
        def _(e):
            for f in range(NF):
                e.wait_ge(sems["es"], e0s + f * 2 + 2)
                e.tensor_tensor(mh[:, f, :], m1b[:], m2b[:], ALU.mult).then_inc(sems["ev"], 1)
        c.n["ev"] = e0v + NF

        t0d, q0d, e0d = c.n["t"], c.n["gq"], c.n["ev"]
        mhdone = c.n["ev"]

        wtargetD = []

        @blk.gpsimd
        def _(e):
            for ms in range(NKD):
                if ms >= 2:
                    e.wait_ge(sems["t"], t0d + (ms - 1))
                elif t0d > 0:
                    e.wait_ge(sems["t"], t0d)
                sm = wsem[ms % 2]
                e.dma_start(out=wvD[ms % 2],
                            in_=wrtd[:, :, ms * 128:(ms + 1) * 128]).then_inc(sems[sm], 16)
                c.n[sm] += 16
                wtargetD.append((sm, c.n[sm]))

        @blk.tensor
        def _(e):
            e.wait_ge(sems["ev"], mhdone)
            idx = 0
            for m in range(NKD):
                e.wait_ge(sems[wtargetD[m][0]], wtargetD[m][1])
                b = idx % 6
                c.use_bank(e, b, "ev", e0d + idx + 1)
                for k in range(NF):
                    mi = e.matmul(ps[b][:, 0:TW],
                                  wvD[m % 2][:, k, :],
                                  mh[:, k, :], start=(k == 0), stop=(k == NF - 1))
                mi.then_inc(sems["t"], 1)
                idx += 1
        c.n["t"] = t0d + NKD

        @blk.vector
        def _(e, st=st, xres_ld=xres_ld):
            e.wait_ge(sems["xq"], xres_ld)
            for m in range(NKD):
                b = m % 6
                e.wait_ge(sems["t"], t0d + m + 1)
                e.scalar_tensor_tensor(zw[:, m, :], ps[b][:, 0:TW],
                                       svec[:, SV["down_b"] + m:SV["down_b"] + m + 1],
                                       x1_t[:, m, :], ALU.add, ALU.add)
                c.tag(e.tensor_tensor(zw[:, m, :], zw[:, m, :],
                                      xres_t[:, m, :], ALU.add),
                      "ev")
        c.n["ev"] = e0d + NKD

        def out_ln(e, k, A, Bn):
            e.tensor_tensor(s0row[:, k, :], zw[:, k, :], A, ALU.mult)
            return e.tensor_tensor(s0row[:, k, :], s0row[:, k, :], Bn, ALU.subtract)

        emit_stats_ln(c, G, lambda k: zw[:, k, :], TW, out_ln, kind="ln", sq_buf=sqb2)

        @blk.sync
        def _(e, st=st):
            c.wait(e, "v")
            c.tag(e.dma_start(out=sp["s0_sp"].ap().rearrange(
                "(k p) t -> p k t", p=128)[:, :, st * TW:(st + 1) * TW],
                in_=s0row[:]), "sq")

    ar.freeb("x1_t", "x1n_t", "mh", "m1b", "m2b", "zw", "sqb2", "s0row", "xres_t", "agg")





def emit_l0_ffn(c, ar, G):
    nc, blk, sems = c.nc, c.blk, c.sems
    sp, wd, svec, ps, psb = G["sp"], G["wd"], G["svec"], G["ps"], G["psb"]
    NH = 11
    NTC = T // 128

    s0r = ar.alloc("s0r", [128, NKD, T], BF16)
    he = ar.alloc("he", [128, NH, T], BF16)
    pbc = ar.alloc("pbc", [128, T], F32)
    prow = ar.alloc("prow", [1, T], BF16)
    gwt = ar.alloc("gwt", [128, NKD * 8], BF16)
    sco = ar.alloc("sco", [128, NTC, 8], F32)
    sbx = ar.alloc("sbx", [128, NTC, 8], F32)
    mk1 = ar.alloc("mk1", [128, NTC, 8], F32)
    sb2 = ar.alloc("sb2", [128, NTC, 8], F32)
    mk2 = ar.alloc("mk2", [128, NTC, 8], F32)
    ptm = ar.alloc("ptm", [128, NTC, 8], BF16)
    mxc = ar.alloc("mxc", [128, 2 * NTC], F32)
    m1b = ar.alloc("fm1", [128, 2, TW], F32)
    m1c = ar.alloc("fm1c", [128, 2, TW], F32)
    m2b = ar.alloc("fm2", [128, 2, TW], F32)
    zb = ar.alloc("fzb", [128, 4, TW], F32)

    gv = gwt.ap().rearrange("p (k e) -> p k e", k=NKD)

    @blk.sync
    def _(e):
        c.wait(e, "sq")
        c.tag(e.dma_start(out=s0r[:], in_=sp["s0_sp"].ap().rearrange("(k p) t -> p k t", p=128)), "sq")
    s0ld = c.n["sq"]

    @blk.gpsimd
    def _(e):
        c.tag(e.dma_start(out=gv, in_=wd["w_gate0"].rearrange("(k p) e -> p k e", p=128)), "xq")
    gwld = c.n["xq"]

    @blk.tensor
    def _(e):
        e.wait_ge(sems["sq"], s0ld)
        e.wait_ge(sems["xq"], gwld)
        for tc in range(NTC):
            b = tc % 6
            c.wait_bank(e, b)
            for k in range(NKD):
                mi = e.matmul(ps[b][:, 0:8], s0r[:, k, tc * 128:(tc + 1) * 128],
                              gv[:, k, :], start=(k == 0), stop=(k == NKD - 1))
            c.tag(mi, "t")
    tsc = c.n["t"] - NTC

    @blk.scalar
    def _(e):
        for tc in range(NTC):
            b = tc % 6
            e.wait_ge(sems["t"], tsc + tc + 1)
            c.tag(e.activation(sco[:, tc, :], ps[b][0:128, 0:8], AF.Sigmoid), "es")
    for tc in range(NTC):
        c.free_bank(tc % 6, "es")

    @blk.vector
    def _(e):
        c.wait(e, "es")

        def flush():
            e.tensor_copy(m1b[:, 0, :], G["zrow"][:, 0:TW])

        for tc in range(NTC):
            e.tensor_tensor(sbx[:, tc, :], sco[:, tc, :], G["ebias"][:], ALU.add)
        flush()
        for tc in range(NTC):
            e.tensor_reduce(mxc[:, tc:tc + 1], sbx[:, tc, :], AX.X, ALU.max)
        flush()
        for tc in range(NTC):
            e.tensor_scalar(mk1[:, tc, :], sbx[:, tc, :], mxc[:, tc:tc + 1], None, ALU.is_ge)
        flush()
        for tc in range(NTC):
            e.scalar_tensor_tensor(sb2[:, tc, :], mk1[:, tc, :], -1e9,
                                   sbx[:, tc, :], ALU.mult, ALU.add)
        flush()
        for tc in range(NTC):
            e.tensor_reduce(mxc[:, NTC + tc:NTC + tc + 1], sb2[:, tc, :], AX.X, ALU.max)
        flush()
        for tc in range(NTC):
            e.tensor_scalar(mk2[:, tc, :], sb2[:, tc, :], mxc[:, NTC + tc:NTC + tc + 1],
                            None, ALU.is_ge)
        flush()
        for tc in range(NTC):
            e.tensor_tensor(mk1[:, tc, :], mk1[:, tc, :], mk2[:, tc, :], ALU.add)
        flush()
        for tc in range(NTC):
            e.tensor_tensor(ptm[:, tc, :], mk1[:, tc, :], sco[:, tc, :], ALU.mult)
        flush()
        c.tag(e.engine_nop(), "p")

    zr_base = [c.n["gq"]]
    dpar = ["da", "db"]
    for exi, ex in enumerate(F_EXPERTS):
        shared = (ex == 8)
        if shared:
            w1d = wd["w_sw1"]; w3d = wd["w_sw3"]; w2d = wd["w_sw2"]
        else:
            w1d = wd["w_e1"][ex]; w3d = wd["w_e3"][ex]; w2d = wd["w_e2"][ex]
        w1r = w1d.rearrange("(k p) m -> p k m", p=128)
        w3r = w3d.rearrange("(k p) m -> p k m", p=128)
        w2r = w2d.rearrange("(k p) m -> p k m", p=128)

        if not shared:
            tp0 = c.n["t"]
            vp0 = c.n["v"]

            NTC = T // 128
            NHH = (NTC + 7) // 8

            @blk.tensor
            def _(e, ex=ex, vp0=vp0, NTC=NTC, NHH=NHH):
                c.wait(e, "p")
                c.wait_bank(e, 7)
                for hh in range(NHH):
                    if hh >= 1:
                        e.wait_ge(sems["v"], vp0 + hh)
                    for tc in range(min(8, NTC - hh * 8)):
                        tci = hh * 8 + tc
                        mi = e.transpose(psb[0:1, tc * 128:(tc + 1) * 128],
                                         ptm[:, tci, ex:ex + 1], G["eye"][:])
                    c.tag(mi, "t")

            @blk.vector
            def _(e, tp0=tp0, NTC=NTC, NHH=NHH):
                for hh in range(NHH):
                    w = 128 * min(8, NTC - hh * 8)
                    e.wait_ge(sems["t"], tp0 + hh + 1)
                    c.tag(e.tensor_copy(prow[0:1, hh * 1024:hh * 1024 + w],
                                        psb[0:1, 0:w]), "v")
            c.free_bank(7, "v")

            tb0 = c.n["t"]

            @blk.tensor
            def _(e):
                c.wait(e, "v")
                for g in range(NT):
                    b = g % 6
                    c.wait_bank(e, b)
                    c.tag(e.matmul(ps[b][:, 0:TW], G["onesr_bf"][:],
                                   prow[0:1, g * TW:(g + 1) * TW], start=True, stop=True), "t")

            @blk.vector
            def _(e, tb0=tb0):
                for g in range(NT):
                    b = g % 6
                    e.wait_ge(sems["t"], tb0 + g + 1)
                    c.tag(e.tensor_copy(pbc[:, g * TW:(g + 1) * TW], ps[b][:, 0:TW]), "v")
            for g in range(NT):
                c.free_bank(g % 6, "v")

        wtg = []
        t0e = c.n["t"]

        @blk.gpsimd
        def _(e, t0e=t0e, w1r=w1r, w3r=w3r):
            for f in range(NH):
                if f >= 2:
                    e.wait_ge(sems["t"], t0e + (f - 1) * 2 * NT)
                elif t0e > 0:
                    e.wait_ge(sems["t"], t0e)
                sm = ("wa", "wb")[f % 2]
                e.dma_start(out=G["wbufA" if f % 2 == 0 else "wbufB"].ap()[:, 0:NKD * 128].rearrange(
                    "p (k m) -> p k m", k=NKD),
                    in_=w1r[:, :, f * 128:(f + 1) * 128]).then_inc(sems[sm], 16)
                e.dma_start(out=G["wbufA" if f % 2 == 0 else "wbufB"].ap()[:, NKD * 128:2 * NKD * 128].rearrange(
                    "p (k m) -> p k m", k=NKD),
                    in_=w3r[:, :, f * 128:(f + 1) * 128]).then_inc(sems[sm], 16)
                c.n[sm] += 32
                wtg.append((sm, c.n[sm]))

        for f in range(NH):
            bsel = []

            @blk.tensor
            def _(e, f=f):
                e.wait_ge(sems[wtg[f][0]], wtg[f][1])
                if f == 0:
                    e.wait_ge(sems["sq"], s0ld)
                wv1 = G["wbufA" if f % 2 == 0 else "wbufB"].ap()[:, 0:NKD * 128].rearrange(
                    "p (k m) -> p k m", k=NKD)
                wv3 = G["wbufA" if f % 2 == 0 else "wbufB"].ap()[:, NKD * 128:2 * NKD * 128].rearrange(
                    "p (k m) -> p k m", k=NKD)
                for g in range(NT):
                    for br, wv_ in ((0, wv1), (1, wv3)):
                        b = (f * 8 + g * 2 + br) % 6
                        bsel.append(b)
                        c.wait_bank(e, b)
                        for k in range(NKD):
                            mi = e.matmul(ps[b][:, 0:TW], wv_[:, k, :],
                                          s0r[:, k, g * TW:(g + 1) * TW],
                                          start=(k == 0), stop=(k == NKD - 1))
                        c.tag(mi, "t")
            tgrp = c.n["t"] - 2 * NT

            for g in range(NT):
                b1 = bsel[g * 2]
                b3 = bsel[g * 2 + 1]
                par = (f * NT + g) % 2

                @blk.vector
                def _(e, f=f, g=g, b1=b1, b3=b3, par=par, tgrp=tgrp, shared=shared):
                    e.wait_ge(sems["t"], tgrp + g * 2 + 2)
                    c.wait(e, "es")
                    if not shared:
                        e.tensor_tensor(m1b[:, par, :], ps[b1][:, 0:TW],
                                        pbc[:, g * TW:(g + 1) * TW], ALU.mult)
                        c.tag(e.tensor_tensor(m2b[:, par, :], ps[b3][:, 0:TW],
                                              pbc[:, g * TW:(g + 1) * TW], ALU.mult), "ev")
                    else:
                        e.tensor_copy(m1b[:, par, :], ps[b1][:, 0:TW])
                        c.tag(e.tensor_copy(m2b[:, par, :], ps[b3][:, 0:TW]), "ev")
                c.free_bank(b1, "ev")
                c.free_bank(b3, "ev")

                @blk.scalar
                def _(e, par=par):
                    c.wait(e, "ev")
                    c.tag(e.activation(m1c[:, par, :], m1b[:, par, :], AF.Silu), "es")

                @blk.vector
                def _(e, f=f, g=g, par=par):
                    c.wait(e, "es")
                    c.tag(e.tensor_tensor(he[:, f, g * TW:(g + 1) * TW],
                                          m1c[:, par, :], m2b[:, par, :], ALU.mult), "ev")

        t0d = c.n["t"]
        hedone = c.n["ev"]
        wtd = []
        ev_of_group = {}
        pend = []

        EXI = [exi]

        def emit_stores(e, ex):
            while pend:
                m_, g_, par_ = pend.pop(0)
                e.wait_ge(sems["ev"], ev_of_group[(m_, g_)])
                if EXI[0] > 0 and m_ == 0 and g_ == 0:
                    c.wait(e, "da")
                    c.wait(e, "db")
                op = ALU.bypass if EXI[0] == 0 else ALU.add
                dst = sp["zr_sp"].ap()[m_ * 128:(m_ + 1) * 128, g_ * TW:(g_ + 1) * TW]
                c.tag(e.dma_start(out=dst, in_=zb[:, par_, :], accum_op=op), dpar[par_ % 2])

        for m in range(NKD):
            @blk.gpsimd
            def _(e, m=m, t0d=t0d, ex=ex, w2r=w2r):
                emit_stores(e, ex)
                if m >= 2:
                    e.wait_ge(sems["t"], t0d + (m - 1) * NT)
                elif t0d > 0:
                    e.wait_ge(sems["t"], t0d)
                sm = ("wa", "wb")[m % 2]
                e.dma_start(out=G["wbufA" if m % 2 == 0 else "wbufB"].ap()[:, 0:NH * 128].rearrange(
                    "p (k m) -> p k m", k=NH),
                    in_=w2r[:, :, m * 128:(m + 1) * 128]).then_inc(sems[sm], 16)
                c.n[sm] += 16
                wtd.append((sm, c.n[sm]))

            bsel = []

            @blk.tensor
            def _(e, m=m, hedone=hedone):
                e.wait_ge(sems[wtd[m][0]], wtd[m][1])
                if m == 0:
                    e.wait_ge(sems["ev"], hedone)
                wv2 = G["wbufA" if m % 2 == 0 else "wbufB"].ap()[:, 0:NH * 128].rearrange(
                    "p (k m) -> p k m", k=NH)
                for g in range(NT):
                    b = (m * NT + g) % 6
                    bsel.append(b)
                    c.wait_bank(e, b)
                    for k in range(NH):
                        mi = e.matmul(ps[b][:, 0:TW], wv2[:, k, :],
                                      he[:, k, g * TW:(g + 1) * TW],
                                      start=(k == 0), stop=(k == NH - 1))
                    c.tag(mi, "t")
            tgrp = c.n["t"] - NT

            for g in range(NT):
                b = bsel[g]
                idx = m * NT + g
                par = idx % 4

                @blk.vector
                def _(e, b=b, par=par, tgrp=tgrp, g=g):
                    e.wait_ge(sems["t"], tgrp + g + 1)
                    c.wait(e, dpar[par % 2])
                    c.tag(e.tensor_copy(zb[:, par, :], ps[b][:, 0:TW]), "ev")
                c.free_bank(b, "ev")
                ev_of_group[(m, g)] = c.n["ev"]
                pend.append((m, g, par))

        @blk.gpsimd
        def _(e, ex=ex):
            emit_stores(e, ex)

    ar.freeb("he", "pbc", "prow", "gwt", "sco", "sbx", "mk1", "sb2", "mk2",
             "ptm", "mxc", "fm1", "fm1c", "fm2", "fzb")
    zrt = ar.alloc("zrt", [128, NKD, TW], BF16)
    zf = ar.alloc("zf", [128, NKD, TW], BF16)
    s1row = ar.alloc("s1row", [128, NKD, TW], BF16)
    sqf = ar.alloc("sqf", [128, TW], BF16)
    for nt in range(NT):
        @blk.gpsimd
        def _(e, nt=nt):
            c.wait(e, "da")
            c.wait(e, "db")
            if nt > 0:
                c.wait(e, "v")
            c.tag(e.dma_start(out=zrt[:], in_=sp["zr_sp"].ap().rearrange(
                "(k p) t -> p k t", p=128)[:, :, nt * TW:(nt + 1) * TW]), "xq")
        zld = c.n["xq"]

        @blk.vector
        def _(e, nt=nt, zld=zld):
            e.wait_ge(sems["xq"], zld)
            for k in range(NKD):
                c.tag(e.tensor_tensor(zf[:, k, :], zrt[:, k, :],
                                      s0r[:, k, nt * TW:(nt + 1) * TW], ALU.add), "v")

        def out_ln1(e, k, A, Bn, nt=nt):
            e.tensor_tensor(s1row[:, k, :], zf[:, k, :], A, ALU.mult)
            return e.tensor_tensor(s1row[:, k, :], s1row[:, k, :], Bn, ALU.subtract)

        emit_stats_ln(c, G, lambda k: zf[:, k, :], TW, out_ln1, kind="ln", sq_buf=sqf)

        @blk.sync
        def _(e, nt=nt):
            c.wait(e, "v")
            c.tag(e.dma_start(out=sp["s1_sp"].ap().rearrange(
                "(k p) t -> p k t", p=128)[:, :, nt * TW:(nt + 1) * TW], in_=s1row[:]), "sq")

    ar.freeb("s0r", "zrt", "zf", "s1row", "sqf")



def emit_l1_attn(c, ar, G):
    nc, blk, sems = c.nc, c.blk, c.sems
    sp, wd, svec, dvec, ps = G["sp"], G["wd"], G["svec"], G["dvec"], G["ps"]

    def svc(name, i):
        return svec[:, SV[name] + i:SV[name] + i + 1]

    hln = ar.alloc("hln", [128, NKD, T], BF16)
    s1t = ar.alloc("s1t", [128, NKD, TW], BF16)
    sqg = ar.alloc("sqg", [128, TW], BF16)
    for nt in range(NT):
        @blk.sync
        def _(e, nt=nt):
            c.wait(e, "sq")
            if nt > 0:
                c.wait(e, "v")
            c.tag(e.dma_start(out=s1t[:], in_=sp["s1_sp"].ap().rearrange(
                "(k p) t -> p k t", p=128)[:, :, nt * TW:(nt + 1) * TW]), "sq")
        sld = c.n["sq"]

        @blk.vector
        def _(e, sld=sld):
            e.wait_ge(sems["sq"], sld)
            c.tag(e.engine_nop(), "v")

        def out_g1(e, k, A, Bn, nt=nt):
            e.tensor_tensor(hln[:, k, nt * TW:(nt + 1) * TW], s1t[:, k, :], A, ALU.mult)
            e.tensor_tensor(hln[:, k, nt * TW:(nt + 1) * TW],
                            hln[:, k, nt * TW:(nt + 1) * TW], Bn, ALU.subtract)
            return e.tensor_scalar(hln[:, k, nt * TW:(nt + 1) * TW],
                                   hln[:, k, nt * TW:(nt + 1) * TW],
                                   svc("ln_g", k), svc("ln_b", k), ALU.mult, ALU.add)

        emit_stats_ln(c, G, lambda k: s1t[:, k, :], TW, out_g1, kind="ln", sq_buf=sqg)

    ar.freeb("s1t", "sqg")
    qr = ar.alloc("qr", [128, NKD, T], BF16)
    krow = ar.alloc("krow", [128, 2, T], BF16)

    def evac_q(e, m, g, pst):
        return e.activation(qr[:, m, g * TW:(g + 1) * TW], pst[:, 0:TW], AF.Identity,
                            bias=svc("bq", m))
    mmloop(c, G, wd["w_q"], NKD, NKD, lambda k, g: hln[:, k, g * TW:(g + 1) * TW],
           [TW] * 4, "s", evac_q, ready=[("v", c.n["v"])])

    for wnm, bnm, spnm in (("w_k", "bk", "k_sp"), ("w_v", "bv", "v_sp")):
        st_b = c.n["sq"]

        def evac_kv(e, m, g, pst, bnm=bnm, st_b=st_b):
            if m >= 2 and g == 0:
                e.wait_ge(sems["sq"], st_b + 16 * m)
            return e.activation(krow[:, m % 2, g * TW:(g + 1) * TW], pst[:, 0:TW],
                                AF.Identity, bias=svc(bnm, m))

        t0k, e0k, _ = mmloop(c, G, wd[wnm], NKD, NKD,
                             lambda k, g: hln[:, k, g * TW:(g + 1) * TW],
                             [TW] * 4, "s", evac_kv, ready=[("v", c.n["v"])])

        @blk.sync
        def _(e, e0k=e0k, spnm=spnm):
            for m in range(NKD):
                e.wait_ge(sems["es"], e0k + (m + 1) * 4)
                c.tag(e.dma_start(out=sp[spnm].ap()[m * 128:(m + 1) * 128, :],
                                  in_=krow[:, m % 2, :]), "sq")

    ar.freeb("hln")
    qw = ar.alloc("qw", [128, NKD, T], BF16)
    gsum = ar.alloc("gsum", [128, 96], F32)
    ag2 = ar.alloc("ag2", [128, 8, 16], F32)
    wrow = ar.alloc("wrow", [128, 2, T], BF16)
    w2row = ar.alloc("w2row", [128, 2, T], BF16)

    sc = float(1.0 / np.sqrt(2048.0))

    def evac_qa(e, m, g, pst):
        return e.activation(qw[:, m, g * TW:(g + 1) * TW], pst[:, 0:TW], AF.Exp,
                            scale=sc, bias=dvec[:, 16 + m:17 + m])

    mmloop(c, G, wd["w_qa"], NKD, NKD, lambda k, g: qr[:, k, g * TW:(g + 1) * TW],
           [TW] * 4, "s", evac_qa, ready=[("es", c.n["es"])])

    @blk.vector
    def _(e):
        c.wait(e, "es")
        for m in range(NKD):
            c.tag(e.tensor_reduce(gsum[:, m:m + 1], qw[:, m, :], AX.X, ALU.add), "v")

    emit_carry_exchange(c, G, gsum, ag2)

    v0, s0g, p0 = c.n["v"], c.n["s"], c.n["p"]

    for m in range(NKD):
        @blk.vector
        def _(e, m=m):
            if m == 0:
                c.wait(e, "p")
            if m >= 2:
                e.wait_ge(sems["s"], s0g + (m - 1))
            c.tag(e.tensor_tensor_scan(wrow[:, m % 2, :], qw[:, m, :], G["zrow"][:],
                                       gsum[:, 16 + m:17 + m], ALU.add, ALU.add), "v")

        @blk.scalar
        def _(e, m=m):
            e.wait_ge(sems["v"], v0 + m + 1)
            if m >= 2:
                e.wait_ge(sems["p"], p0 + (m - 1))
            e.activation(w2row[:, m % 2, :], wrow[:, m % 2, :], AF.Ln)
            c.tag(e.activation(w2row[:, m % 2, :], w2row[:, m % 2, :], AF.Exp,
                               scale=-1.0), "s")

        @blk.gpsimd
        def _(e, m=m):
            e.wait_ge(sems["s"], s0g + m + 1)
            e.tensor_tensor(qw[:, m, :], qw[:, m, :], w2row[:, m % 2, :], ALU.mult)
            c.tag(e.tensor_tensor(qw[:, m, :], qw[:, m, :], qr[:, m, :], ALU.mult), "p")

    for m in range(NKD):
        @blk.vector
        def _(e, m=m):
            e.wait_ge(sems["p"], p0 + m + 1)
            c.tag(e.tensor_reduce(gsum[:, m:m + 1], qw[:, m, :], AX.X, ALU.add), "v")

    emit_carry_exchange(c, G, gsum, ag2)
    ar.freeb("qr")
    mixed = ar.alloc("mixed", [128, NKD, T], BF16)

    kvb = c.n["p"]

    @blk.sync
    def _(e):
        c.wait(e, "p")
        for m in range(NKD):
            if m >= 2:
                e.wait_ge(sems["p"], kvb + (m - 1))
            c.tag(e.dma_start(out=krow[:, m % 2, :],
                              in_=sp["k_sp"].ap()[m * 128:(m + 1) * 128, :]), "sq")
    kld0 = c.n["sq"] - 16 * NKD

    vm0, pm0 = c.n["v"], c.n["p"]
    for m in range(NKD):
        @blk.vector
        def _(e, m=m, kld0=kld0):
            if m == 0:
                c.wait(e, "p")
            if m >= 2:
                e.wait_ge(sems["p"], pm0 + (m - 1))
            c.tag(e.tensor_tensor_scan(wrow[:, m % 2, :], qw[:, m, :], G["zrow"][:],
                                       gsum[:, 16 + m:17 + m], ALU.add, ALU.add), "v")

        @blk.gpsimd
        def _(e, m=m, kld0=kld0):
            e.wait_ge(sems["v"], vm0 + m + 1)
            e.wait_ge(sems["sq"], kld0 + 16 * (m + 1))
            c.tag(e.tensor_tensor(mixed[:, m, :], wrow[:, m % 2, :],
                                  krow[:, m % 2, :], ALU.mult), "p")

    ar.freeb("qw")
    kw = ar.alloc("kw", [128, NKD, T], BF16)

    def evac_ka(e, m, g, pst):
        return e.activation(kw[:, m, g * TW:(g + 1) * TW], pst[:, 0:TW], AF.Exp,
                            scale=sc, bias=dvec[:, 32 + m:33 + m])

    mmloop(c, G, wd["w_ka"], NKD, NKD, lambda k, g: mixed[:, k, g * TW:(g + 1) * TW],
           [TW] * 4, "s", evac_ka, ready=[("v", c.n["v"])])

    @blk.vector
    def _(e):
        c.wait(e, "es")
        for m in range(NKD):
            c.tag(e.tensor_reduce(gsum[:, m:m + 1], kw[:, m, :], AX.X, ALU.add), "v")

    emit_carry_exchange(c, G, gsum, ag2)

    v1, s1g, p1 = c.n["v"], c.n["s"], c.n["p"]
    for m in range(NKD):
        @blk.vector
        def _(e, m=m):
            if m == 0:
                c.wait(e, "p")
            if m >= 2:
                e.wait_ge(sems["s"], s1g + (m - 1))
            c.tag(e.tensor_tensor_scan(wrow[:, m % 2, :], kw[:, m, :], G["zrow"][:],
                                       gsum[:, 16 + m:17 + m], ALU.add, ALU.add), "v")

        @blk.scalar
        def _(e, m=m):
            e.wait_ge(sems["v"], v1 + m + 1)
            if m >= 2:
                e.wait_ge(sems["p"], p1 + (m - 1))
            e.activation(w2row[:, m % 2, :], wrow[:, m % 2, :], AF.Ln)
            c.tag(e.activation(w2row[:, m % 2, :], w2row[:, m % 2, :], AF.Exp,
                               scale=-1.0), "s")

        @blk.gpsimd
        def _(e, m=m):
            e.wait_ge(sems["s"], s1g + m + 1)
            e.tensor_tensor(kw[:, m, :], kw[:, m, :], w2row[:, m % 2, :], ALU.mult)
            c.tag(e.tensor_tensor(kw[:, m, :], kw[:, m, :], mixed[:, m, :], ALU.mult), "p")

    for m in range(NKD):
        @blk.vector
        def _(e, m=m):
            e.wait_ge(sems["p"], p1 + m + 1)
            c.tag(e.tensor_reduce(gsum[:, m:m + 1], kw[:, m, :], AX.X, ALU.add), "v")

    emit_carry_exchange(c, G, gsum, ag2)

    vld_b = c.n["sq"]
    kvb2 = c.n["p"]

    @blk.sync
    def _(e, kvb2=kvb2):
        c.wait(e, "p")
        for m in range(NKD):
            if m >= 2:
                e.wait_ge(sems["p"], kvb2 + (m - 1))
            c.tag(e.dma_start(out=krow[:, m % 2, :],
                              in_=sp["v_sp"].ap()[m * 128:(m + 1) * 128, :]), "sq")

    va0, pa0 = c.n["v"], c.n["p"]
    for m in range(NKD):
        @blk.vector
        def _(e, m=m):
            if m == 0:
                c.wait(e, "p")
            if m >= 2:
                e.wait_ge(sems["p"], pa0 + (m - 1))
            c.tag(e.tensor_tensor_scan(wrow[:, m % 2, :], kw[:, m, :], G["zrow"][:],
                                       gsum[:, 16 + m:17 + m], ALU.add, ALU.add), "v")

        @blk.gpsimd
        def _(e, m=m, vld_b=vld_b):
            e.wait_ge(sems["v"], va0 + m + 1)
            e.wait_ge(sems["sq"], vld_b + 16 * (m + 1))
            c.tag(e.tensor_tensor(mixed[:, m, :], wrow[:, m % 2, :],
                                  krow[:, m % 2, :], ALU.mult), "p")

    ar.freeb("kw", "wrow", "w2row", "krow", "gsum", "ag2")
    s1t = ar.alloc("s1tb", [128, NKD, TW], BF16)
    sqg = ar.alloc("sqgb", [128, TW], BF16)
    s2row = ar.alloc("s2row", [128, NKD, TW], BF16)
    zf2 = ar.alloc("zf2", [128, NKD, TW], BF16)
    for nt in range(NT):
        @blk.sync
        def _(e, nt=nt):
            c.wait(e, "v")
            c.wait(e, "p")
            c.tag(e.dma_start(out=s1t[:], in_=sp["s1_sp"].ap().rearrange(
                "(k p) t -> p k t", p=128)[:, :, nt * TW:(nt + 1) * TW]), "sq")
        sld = c.n["sq"]

        @blk.vector
        def _(e, nt=nt, sld=sld):
            c.wait(e, "p")
            e.wait_ge(sems["sq"], sld)
            for k in range(NKD):
                c.tag(e.tensor_tensor(zf2[:, k, :], s1t[:, k, :],
                                      mixed[:, k, nt * TW:(nt + 1) * TW], ALU.add), "v")

        def out_g7(e, k, A, Bn):
            e.tensor_tensor(s2row[:, k, :], zf2[:, k, :], A, ALU.mult)
            return e.tensor_tensor(s2row[:, k, :], s2row[:, k, :], Bn, ALU.subtract)

        emit_stats_ln(c, G, lambda k: zf2[:, k, :], TW, out_g7, kind="ln", sq_buf=sqg)

        @blk.sync
        def _(e, nt=nt):
            c.wait(e, "v")
            c.tag(e.dma_start(out=sp["s2_sp"].ap().rearrange(
                "(k p) t -> p k t", p=128)[:, :, nt * TW:(nt + 1) * TW], in_=s2row[:]), "sq")

    ar.freeb("s1tb", "sqgb", "mixed", "s2row", "zf2")


def emit_carry_exchange(c, G, gsum, ag2):
    blk, sems = c.blk, c.sems
    emit_allgather(c, G, gsum[:, 0:16], 16, ag2)

    @blk.vector
    def _(e):
        c.wait(e, "gq")
        c.tag(e.memset(gsum[:, 16:32], 0.0), "v")
    for r in range(8):
        if r % 2 == 0:
            @blk.vector
            def _(e, r=r):
                c.wait(e, "p")
                c.tag(e.tensor_scalar(gsum[:, 32:48], ag2[:, r, :], G["rmask"][:, r:r + 1],
                                      None, ALU.mult), "v")

            @blk.gpsimd
            def _(e, r=r):
                c.wait(e, "v")
                c.tag(e.tensor_tensor(gsum[:, 16:32], gsum[:, 16:32], gsum[:, 32:48],
                                      ALU.add), "p")
        else:
            @blk.gpsimd
            def _(e, r=r):
                c.wait(e, "v")
                c.wait(e, "p")
                c.tag(e.tensor_scalar(gsum[:, 48:64], ag2[:, r, :], G["rmask"][:, r:r + 1],
                                      None, ALU.mult), "p")

            @blk.vector
            def _(e, r=r):
                c.wait(e, "p")
                c.tag(e.tensor_tensor(gsum[:, 16:32], gsum[:, 16:32], gsum[:, 48:64],
                                      ALU.add), "v")
    @blk.vector
    def _(e):
        c.wait(e, "p")
        c.tag(e.engine_nop(), "p")



def emit_l1_ffn(c, ar, G):
    nc, blk, sems = c.nc, c.blk, c.sems
    sp, wd, svec, ps = G["sp"], G["wd"], G["svec"], G["ps"]
    NI = 64

    s2r = ar.alloc("s2r", [128, NKD, T], BF16)
    hbuf = ar.alloc("hbuf", [128, 2, 2, D], BF16)
    rs_bf = ar.alloc("rs_bf", [1, T], F32)
    hmrow = ar.alloc("hmrow", [128, 2, T], BF16)
    hm1 = ar.alloc("hm1", [128, 2, TW], F32)
    hm2 = ar.alloc("hm2", [128, 2, TW], F32)

    pc0 = c.n["xq"]
    st0 = c.n["sq"]

    @blk.gpsimd
    def _(e):
        c.wait(e, "v")
        c.wait(e, "sq")
        for i in range(32):
            if i >= 2:
                e.wait_ge(sems["sq"], st0 + 16 * (i - 1))
            c.tag(e.dma_start(out=hbuf[:, i % 2, :, :], in_=wd["w_d8"].rearrange(
                "(i p) d -> p i d", p=128)[:, i * 2:(i + 1) * 2, :]), "xq")

    @blk.sync
    def _(e):
        for i in range(32):
            e.wait_ge(sems["xq"], pc0 + 16 * (i + 1))
            c.tag(e.dma_start(out=sp["d8bf_sp"].ap().rearrange(
                "(i p) d -> p i d", p=128)[:, i * 2:(i + 1) * 2, :],
                in_=hbuf[:, i % 2, :, :]), "sq")

    @blk.sync
    def _(e):
        c.wait(e, "sq")
        c.tag(e.dma_start(out=s2r[:], in_=sp["s2_sp"].ap().rearrange(
            "(k p) t -> p k t", p=128)), "sq")
    s2ld = c.n["sq"]

    t0r = c.n["t"]

    @blk.tensor
    def _(e):
        e.wait_ge(sems["sq"], s2ld)
        for g in range(NT):
            b = g % 6
            c.wait_bank(e, b)
            for k in range(NKD):
                mi = e.matmul(ps[b][0:1, 0:TW], G["ones_bf"][:],
                              s2r[:, k, g * TW:(g + 1) * TW],
                              start=(k == 0), stop=(k == NKD - 1))
            c.tag(mi, "t")

    @blk.vector
    def _(e, t0r=t0r):
        for g in range(NT):
            b = g % 6
            e.wait_ge(sems["t"], t0r + g + 1)
            c.tag(e.tensor_copy(rs_bf[0:1, g * TW:(g + 1) * TW], ps[b][0:1, 0:TW]), "v")
    for g in range(NT):
        c.free_bank(g % 6, "v")

    g8r = wd["w_g8"].rearrange("(k p) m -> p k m", p=128)
    u8r = wd["w_u8"].rearrange("(k p) m -> p k m", p=128)
    wtg = []
    t0h = c.n["t"]
    rsdone = c.n["v"]

    @blk.gpsimd
    def _(e, t0h=t0h):
        for f in range(NI):
            if f >= 2:
                e.wait_ge(sems["t"], t0h + (f - 1) * 2 * NT)
            elif t0h > 0:
                e.wait_ge(sems["t"], t0h)
            sm = ("wa", "wb")[f % 2]
            e.dma_start(out=G["wbufA" if f % 2 == 0 else "wbufB"].ap()[:, 0:NKD * 128].rearrange(
                "p (k m) -> p k m", k=NKD),
                in_=g8r[:, :, f * 128:(f + 1) * 128]).then_inc(sems[sm], 16)
            e.dma_start(out=G["wbufA" if f % 2 == 0 else "wbufB"].ap()[:, NKD * 128:2 * NKD * 128].rearrange(
                "p (k m) -> p k m", k=NKD),
                in_=u8r[:, :, f * 128:(f + 1) * 128]).then_inc(sems[sm], 16)
            c.n[sm] += 32
            wtg.append((sm, c.n[sm]))

    hst0 = c.n["sq"]
    for f in range(NI):
        bsel = []

        @blk.tensor
        def _(e, f=f):
            e.wait_ge(sems[wtg[f][0]], wtg[f][1])
            if f == 0:
                e.wait_ge(sems["v"], rsdone)
            wvg = G["wbufA" if f % 2 == 0 else "wbufB"].ap()[:, 0:NKD * 128].rearrange(
                "p (k m) -> p k m", k=NKD)
            wvu = G["wbufA" if f % 2 == 0 else "wbufB"].ap()[:, NKD * 128:2 * NKD * 128].rearrange(
                "p (k m) -> p k m", k=NKD)
            for g in range(NT):
                for br, wv_ in ((0, wvg), (1, wvu)):
                    b = (f * 8 + g * 2 + br) % 6
                    bsel.append(b)
                    c.wait_bank(e, b)
                    for k in range(NKD):
                        e.matmul(ps[b][:, 0:TW], wv_[:, k, :],
                                 s2r[:, k, g * TW:(g + 1) * TW],
                                 start=(k == 0), stop=False)
                    mi = e.matmul(ps[b][:, 0:TW], G["neg128f"][:],
                                  rs_bf[0:1, g * TW:(g + 1) * TW],
                                  start=False, stop=True)
                    c.tag(mi, "t")
        tgrp = c.n["t"] - 2 * NT

        for g in range(NT):
            b1 = bsel[g * 2]
            b2 = bsel[g * 2 + 1]
            par = (f * NT + g) % 2

            @blk.scalar
            def _(e, f=f, g=g, b1=b1, b2=b2, par=par, tgrp=tgrp):
                e.wait_ge(sems["t"], tgrp + g * 2 + 2)
                c.wait(e, "ev")
                e.activation(hm1[:, par, :], ps[b1][:, 0:TW], AF.Silu,
                             scale=svec[:, SV["gscale"] + f:SV["gscale"] + f + 1])
                c.tag(e.activation(hm2[:, par, :], ps[b2][:, 0:TW], AF.Identity,
                                   scale=svec[:, SV["uscale"] + f:SV["uscale"] + f + 1]), "es")
            c.free_bank(b1, "es")
            c.free_bank(b2, "es")

            @blk.vector
            def _(e, f=f, g=g, par=par):
                c.wait(e, "es")
                if f >= 2 and g == 0:
                    e.wait_ge(sems["sq"], hst0 + 16 * (f - 1))
                c.tag(e.tensor_tensor(hmrow[:, f % 2, g * TW:(g + 1) * TW],
                                      hm1[:, par, :], hm2[:, par, :], ALU.mult), "ev")

        @blk.sync
        def _(e, f=f):
            c.wait(e, "ev")
            c.tag(e.dma_start(out=sp["hm_sp"].ap()[f * 128:(f + 1) * 128, :],
                              in_=hmrow[:, f % 2, :]), "sq")

    ar.freeb("hbuf", "s2r", "rs_bf", "hmrow", "hm1", "hm2")
    s2t = ar.alloc("s2t", [128, NKD, TW], BF16)
    hmt = ar.alloc("hmt", [128, NI, TW], BF16)
    rs2 = ar.alloc("rs2", [1, TW], F32)
    zq = ar.alloc("zq", [128, 2, TW], F32)
    zf3 = ar.alloc("zf3", [128, NKD, TW], BF16)
    yrow = ar.alloc("yrow", [128, NKD, TW], F32)
    sqh = ar.alloc("sqh", [128, TW], BF16)

    d8r = sp["d8bf_sp"].ap().rearrange("(k p) m -> p k m", p=128)
    for nt in range(NT):
        @blk.sync
        def _(e, nt=nt):
            c.wait(e, "sq")
            if nt > 0:
                c.wait(e, "v")
                c.wait(e, "t", c.n["t"])
            c.tag(e.dma_start(out=hmt[:], in_=sp["hm_sp"].ap().rearrange(
                "(k p) t -> p k t", p=128)[:, :, nt * TW:(nt + 1) * TW]), "sq")
            c.tag(e.dma_start(out=s2t[:], in_=sp["s2_sp"].ap().rearrange(
                "(k p) t -> p k t", p=128)[:, :, nt * TW:(nt + 1) * TW]), "sq")
        hld = c.n["sq"]

        t0s = c.n["t"]

        @blk.tensor
        def _(e, hld=hld):
            e.wait_ge(sems["sq"], hld)
            b = 6
            c.wait_bank(e, b)
            for k in range(NI):
                mi = e.matmul(ps[6][0:1, 0:TW], G["ones_bf"][:], hmt[:, k, :],
                              start=(k == 0), stop=(k == NI - 1))
            c.tag(mi, "t")

        @blk.vector
        def _(e, t0s=t0s):
            e.wait_ge(sems["t"], t0s + 1)
            c.tag(e.tensor_copy(rs2[0:1, :], ps[6][0:1, 0:TW]), "v")
        c.free_bank(6, "v")

        wtd = []
        t0d = c.n["t"]
        rdn = c.n["v"]

        @blk.sync
        def _(e, t0d=t0d):
            for m in range(NKD):
                if m >= 2:
                    e.wait_ge(sems["t"], t0d + (m - 1))
                elif t0d > 0:
                    e.wait_ge(sems["t"], t0d)
                sm = ("ya", "yb")[m % 2]
                e.dma_start(out=G["wbufA" if m % 2 == 0 else "wbufB"].ap()[:, 0:NI * 128].rearrange(
                    "p (k m) -> p k m", k=NI),
                    in_=d8r[:, :, m * 128:(m + 1) * 128]).then_inc(sems[sm], 16)
                c.n[sm] += 16
                wtd.append((sm, c.n[sm]))

        for m in range(NKD):
            @blk.tensor
            def _(e, m=m, rdn=rdn):
                e.wait_ge(sems[wtd[m][0]], wtd[m][1])
                if m == 0:
                    e.wait_ge(sems["v"], rdn)
                wvd = G["wbufA" if m % 2 == 0 else "wbufB"].ap()[:, 0:NI * 128].rearrange(
                    "p (k m) -> p k m", k=NI)
                b = m % 6
                c.wait_bank(e, b)
                for k in range(NI):
                    e.matmul(ps[b][:, 0:TW], wvd[:, k, :], hmt[:, k, :],
                             start=(k == 0), stop=False)
                mi = e.matmul(ps[b][:, 0:TW], G["neg128f"][:], rs2[0:1, 0:TW],
                              start=False, stop=True)
                c.tag(mi, "t")
            tv = c.n["t"]

            @blk.scalar
            def _(e, m=m, tv=tv):
                e.wait_ge(sems["t"], tv)
                c.wait(e, "v")
                c.tag(e.activation(zq[:, m % 2, :], ps[m % 6][:, 0:TW], AF.Identity,
                                   scale=svec[:, SV["dscale"] + m:SV["dscale"] + m + 1]), "es")
            c.free_bank(m % 6, "es")

            @blk.vector
            def _(e, m=m, nt=nt):
                c.wait(e, "es")
                c.tag(e.tensor_tensor(zf3[:, m, :], zq[:, m % 2, :],
                                      s2t[:, m, :], ALU.add), "v")

        def out_fin(e, k, A, Bn):
            e.tensor_tensor(yrow[:, k, :], zf3[:, k, :], A, ALU.mult)
            return e.tensor_tensor(yrow[:, k, :], yrow[:, k, :], Bn, ALU.subtract)

        emit_stats_ln(c, G, lambda k: zf3[:, k, :], TW, out_fin, kind="ln", sq_buf=sqh)

        @blk.sync
        def _(e, nt=nt):
            c.wait(e, "v")
            c.tag(e.dma_start(out=G["y"].rearrange(
                "(k p) t -> p k t", p=128)[:, :, nt * TW:(nt + 1) * TW], in_=yrow[:]), "sq")

    ar.freeb("s2t", "hmt", "rs2", "zq", "zf3", "yrow", "sqh")



def kernel(**inputs):
    sh, per = stage_host(inputs)
    nc = build_program(dbg=DBG)
    in_maps = []
    for cc in range(8):
        m = dict(sh)
        m.update(per[cc])
        in_maps.append(m)
    res = run_bass_kernel_spmd(nc, in_maps, core_ids=list(range(8)))
    outs = []
    for cc in range(8):
        outs.append(res.results[cc]["y"].T)
    full = np.empty((2, 8192, D), np.float32)
    for cc in range(8):
        g, p = cc // 4, cc % 4
        full[g, p * T:(p + 1) * T, :] = outs[cc]
    return full
